# version 11
# baseline (speedup 1.0000x reference)
import numpy as np
import ml_dtypes
from contextlib import ExitStack
import concourse.bass as bass
import concourse.mybir as mybir
from concourse.bass_utils import run_bass_kernel_spmd

F32 = mybir.dt.float32
BF16 = mybir.dt.bfloat16
AF = mybir.ActivationFunctionType
ALU = mybir.AluOpType

NCORE = 8
B, T, D = 2, 8192, 1024
S = 2048
NT = S // 128
H, M = 16, 64
NU = 4
NCH = T // 128
BF = ml_dtypes.bfloat16


class Ctr:
    def __init__(self):
        self.v = {}

    def inc(self, name, amt=1):
        self.v[name] = self.v.get(name, 0) + amt
        return self.v[name]

    def get(self, name):
        return self.v.get(name, 0)


def build_a2():
    nc = bass.Bass("TRN2", target_bir_lowering=False)
    x = nc.dram_tensor("x", [S, D], F32, kind="ExternalInput").ap()
    qkv_w = nc.dram_tensor("qkv_w", [D, 3 * D], F32, kind="ExternalInput").ap()
    qkvb_p = nc.dram_tensor("qkvb_p", [128, 16], F32, kind="ExternalInput").ap()
    qkv_b = nc.dram_tensor("qkv_b", [1, 3 * D], F32, kind="ExternalInput").ap()
    proj = nc.dram_tensor("proj", [H, M, M], F32, kind="ExternalInput").ap()
    o_xn = nc.dram_tensor("o_xn", [S, D], F32, kind="ExternalOutput").ap()
    o_qp = nc.dram_tensor("o_qp", [H, M, S], BF16, kind="ExternalOutput").ap()
    o_kp = nc.dram_tensor("o_kp", [H, M, S], BF16, kind="ExternalOutput").ap()
    o_kptm = nc.dram_tensor("o_kptm", [H, 128, 16, M], BF16, kind="ExternalOutput").ap()
    o_vtm = nc.dram_tensor("o_vtm", [S, D], BF16, kind="ExternalOutput").ap()

    xr = x.rearrange("(n p) d -> n p d", p=128)
    xnr = o_xn.rearrange("(n p) d -> n p d", p=128)
    vr = o_vtm.rearrange("(n p) d -> n p d", p=128)

    es = ExitStack()
    sb = lambda name, shape, dt: es.enter_context(nc.sbuf_tensor(name, shape, dt))
    with es:
        consts = sb("consts", [128, 2], F32)
        EPS, NEGH = consts[:, 0:1], consts[:, 1:2]
        xt = [sb(f"xt{i}", [128, D], F32) for i in range(2)]
        xnt = [sb(f"xnt{i}", [128, D], F32) for i in range(2)]
        xnb = [sb(f"xnb{i}", [128, D], BF16) for i in range(2)]
        stat = sb("stat", [128, 8], F32)
        xnT = sb("xnT", [128, 8, S], BF16)
        qkvw_sb = sb("qkvw_sb", [128, 8, 3 * D], BF16)
        qb_row = sb("qb_row", [1, 3 * D], BF16)
        qkvb_sb = sb("qkvb_sb", [128, 16], F32)
        proj_sb = sb("proj_sb", [128, H, M], BF16)
        ident = sb("ident", [128, 128], BF16)
        idq = sb("idq", [128, 128], F32)
        onesrow = sb("onesrow", [1, 128], BF16)
        qk_buf = [sb(f"qk_buf{i}", [128, 512], BF16) for i in range(2)]
        qkbig = [sb(f"qkbig{i}", [128, S], BF16) for i in range(2)]
        kptm_big = [sb(f"kptm_big{i}", [128, 2048], BF16) for i in range(2)]
        v_big = [sb(f"v_big{i}", [128, D], BF16) for i in range(2)]
        ps_t = [es.enter_context(nc.psum_tensor(f"ps_t{i}", [128, 128], BF16)) for i in range(2)]
        ps_qk = [es.enter_context(nc.psum_tensor(f"ps_qk{i}", [128, 512], F32)) for i in range(2)]
        ps_pf = [es.enter_context(nc.psum_tensor(f"ps_pf{i}", [128, 512], F32)) for i in range(2)]
        ps_ptm = [es.enter_context(nc.psum_tensor(f"ps_ptm{i}", [128, 512], F32)) for i in range(2)]
        LD0 = es.enter_context(nc.semaphore("LD0"))
        LD1 = es.enter_context(nc.semaphore("LD1"))
        LDp = [LD0, LD1]
        GD = es.enter_context(nc.semaphore("GD"))
        TS_ = es.enter_context(nc.semaphore("TS"))
        VS = es.enter_context(nc.semaphore("VS"))
        SC = es.enter_context(nc.semaphore("SC"))
        ST = es.enter_context(nc.semaphore("ST"))
        IO = es.enter_context(nc.semaphore("IO"))
        TQ = es.enter_context(nc.semaphore("TQ"))
        block = es.enter_context(nc.Block())
        c = Ctr()

        @block.sync
        def _(sync):
            for i in range(2):
                sync.dma_start(out=xt[i][:], in_=xr[i]).then_inc(LDp[i], 16)

        qkvw_r = qkv_w.rearrange("(k p) o -> p k o", p=128)
        proj_r = proj.rearrange("h d m -> d h m")

        @block.gpsimd
        def _(gpsimd):
            gpsimd.dma_start(out=qkvw_sb[:, 0:4, :], in_=qkvw_r[:, 0:4, :]).then_inc(GD, 16)
            gpsimd.dma_start(out=qkvw_sb[:, 4:8, :], in_=qkvw_r[:, 4:8, :]).then_inc(GD, 16)
            gpsimd.dma_start(out=qb_row[:], in_=qkv_b).then_inc(GD, 16)
            gpsimd.dma_start(out=qkvb_sb[:], in_=qkvb_p).then_inc(GD, 16)
            gpsimd.dma_start(out=proj_sb[0:64, :, :], in_=proj_r).then_inc(GD, 16)
            gpsimd.dma_start(out=proj_sb[64:128, :, :], in_=proj_r).then_inc(GD, 16)
            c.inc("GD", 16 * 6)
            gpsimd.iota(out=idq[:], pattern=[[1, 128]], base=0, channel_multiplier=-1,
                        allow_small_or_imprecise_dtypes=True).then_inc(IO, 1)
        GD_ALL = c.get("GD")

        @block.vector
        def _(vector):
            vector.memset(consts[:, 0:1], 1e-5)
            vector.memset(consts[:, 1:2], -0.5)
            vector.memset(onesrow[:], 1.0)
            vector.wait_ge(GD, GD_ALL)
            vector.wait_ge(IO, 1)
            vector.tensor_scalar(out=ident[:], in0=idq[:], scalar1=0.0, scalar2=None,
                                 op0=ALU.is_equal).then_inc(VS, 1)
            c.inc("VS")
        SETUP_READY = c.get("VS")

        for tt in range(NT):
            pb = tt % 2

            @block.scalar
            def _(scalar, tt=tt, pb=pb):
                scalar.wait_ge(LDp[pb], 16 * (tt // 2 + 1))
                scalar.wait_ge(VS, SETUP_READY)
                if c.get("ST"):
                    scalar.wait_ge(ST, c.get("ST"))
                scalar.activation(out=xnt[pb][:], in_=xt[pb][:], func=AF.Copy, accum_out=stat[:, 0:1])
                scalar.drain()
                scalar.activation(out=xnb[pb].bitcast(F32)[:, 0:512], in_=xt[pb][:, 0:512], func=AF.Square,
                                  accum_out=stat[:, 1:2])
                scalar.drain()
                scalar.activation(out=xnb[pb].bitcast(F32)[:, 0:512], in_=xt[pb][:, 512:1024], func=AF.Square,
                                  accum_out=stat[:, 7:8]).then_inc(SC, 1)
                c.inc("SC")
            SC_STAT = c.get("SC")

            @block.vector
            def _(vector, SC_STAT=SC_STAT):
                vector.wait_ge(SC, SC_STAT)
                vector.tensor_scalar(out=stat[:, 2:3], in0=stat[:, 0:1], scalar1=1.0 / D, scalar2=None, op0=ALU.mult)
                vector.tensor_tensor(out=stat[:, 3:4], in0=stat[:, 1:2], in1=stat[:, 7:8], op=ALU.add)
                vector.drain()
                vector.tensor_tensor(out=stat[:, 4:5], in0=stat[:, 2:3], in1=stat[:, 2:3], op=ALU.mult)
                vector.tensor_scalar(out=stat[:, 3:4], in0=stat[:, 3:4], scalar1=1.0 / D, scalar2=None, op0=ALU.mult)
                vector.drain()
                vector.tensor_tensor(out=stat[:, 5:6], in0=stat[:, 3:4], in1=stat[:, 4:5], op=ALU.subtract).then_inc(VS, 1)
                c.inc("VS")
            VS_VAR = c.get("VS")

            @block.scalar
            def _(scalar, VS_VAR=VS_VAR):
                scalar.wait_ge(VS, VS_VAR)
                scalar.activation(out=stat[:, 6:7], in_=stat[:, 5:6], func=AF.Ln, bias=EPS)
                scalar.drain()
                scalar.activation(out=stat[:, 6:7], in_=stat[:, 6:7], func=AF.Exp, scale=NEGH).then_inc(SC, 1)
                c.inc("SC")
            SC_RSTD = c.get("SC")

            @block.vector
            def _(vector, tt=tt, pb=pb, SC_RSTD=SC_RSTD):
                vector.wait_ge(SC, SC_RSTD)
                prev = c.get(f"tp_mm_{tt - 2}")
                if prev:
                    vector.wait_ge(TS_, prev)
                vector.tensor_scalar(out=xnt[pb][:], in0=xt[pb][:], scalar1=stat[:, 2:3], scalar2=stat[:, 6:7],
                                     op0=ALU.subtract, op1=ALU.mult)
                vector.drain()
                vector.tensor_copy(out=xnb[pb][:], in_=xnt[pb][:]).then_inc(VS, 1)
                c.inc("VS")
            VS_XN = c.get("VS")

            @block.gpsimd
            def _(gpsimd, tt=tt, pb=pb, VS_XN=VS_XN):
                gpsimd.wait_ge(VS, VS_XN)
                gpsimd.dma_start(out=xnr[tt], in_=xnt[pb][:]).then_inc(ST, 16)
                c.inc("ST", 16)
                c.v[f"st_xn_{tt}"] = c.get("ST")

            @block.sync
            def _(sync, tt=tt, pb=pb):
                if tt + 2 < NT:
                    sync.wait_ge(ST, c.get("ST"))
                    sync.dma_start(out=xt[pb][:], in_=xr[tt + 2]).then_inc(LDp[pb], 16)

            for kc in range(8):
                @block.tensor
                def _(tensor, tt=tt, pb=pb, VS_XN=VS_XN, kc=kc):
                    if kc == 0:
                        tensor.wait_ge(VS, VS_XN)
                    prev = c.get(f"tp_ev_{tt}_{kc - 2}") if kc >= 2 else c.get(f"tp_ev_{tt - 1}_{kc + 6}")
                    if prev:
                        tensor.wait_ge(VS, prev)
                    tensor.transpose(out=ps_t[kc % 2][:], in_=xnb[pb][:, kc * 128:(kc + 1) * 128],
                                     identity=ident[:]).then_inc(TS_, 1)
                    c.inc("TS")
                    c.v[f"tp_mm_{tt}_{kc}"] = c.get("TS")
                    c.v[f"tp_mm_{tt}"] = c.get("TS")

                @block.vector
                def _(vector, tt=tt, kc=kc):
                    vector.wait_ge(TS_, c.get(f"tp_mm_{tt}_{kc}"))
                    vector.tensor_copy(out=xnT[:, kc, tt * 128:(tt + 1) * 128], in_=ps_t[kc % 2][:]).then_inc(VS, 1)
                    c.inc("VS")
                    c.v[f"tp_ev_{tt}_{kc}"] = c.get("VS")
        XNT_READY = c.get("VS")

        for oc in range(16):
            is_k = oc >= 8
            for nt4 in range(4):
                it = oc * 4 + nt4
                bk = it % 2

                @block.tensor
                def _(tensor, oc=oc, nt4=nt4, it=it, bk=bk):
                    if it == 0:
                        tensor.wait_ge(VS, XNT_READY)
                    prev = c.get(f"qk_ev_{it - 2}")
                    if prev:
                        tensor.wait_ge(VS, prev)
                    for kc in range(8):
                        mm = tensor.matmul(out=ps_qk[bk][:], lhsT=qkvw_sb[:, kc, oc * 128:(oc + 1) * 128],
                                           rhs=xnT[:, kc, nt4 * 512:(nt4 + 1) * 512],
                                           start=(kc == 0), stop=(kc == 7))
                    mm.then_inc(TS_, 1)
                    c.inc("TS")
                    c.v[f"qk_mm_{it}"] = c.get("TS")

                @block.vector
                def _(vector, oc=oc, it=it, bk=bk):
                    vector.wait_ge(TS_, c.get(f"qk_mm_{it}"))
                    prev = c.get(f"pj_mm_{it - 2}")
                    if prev:
                        vector.wait_ge(TS_, prev)
                    vector.tensor_scalar(out=qk_buf[bk][:], in0=ps_qk[bk][:], scalar1=qkvb_sb[:, oc:oc + 1],
                                         scalar2=None, op0=ALU.add).then_inc(VS, 1)
                    c.inc("VS")
                    c.v[f"qk_ev_{it}"] = c.get("VS")

                @block.tensor
                def _(tensor, oc=oc, nt4=nt4, it=it, bk=bk, is_k=is_k):
                    tensor.wait_ge(VS, c.get(f"qk_ev_{it}"))
                    prev = c.get(f"pj_ev_{it - 2}")
                    if prev:
                        tensor.wait_ge(SC, prev)
                    for h2x in range(2):
                        h = 2 * (oc % 8) + h2x
                        pbase = h2x * 64
                        mm = tensor.matmul(out=ps_pf[bk][pbase:pbase + 64, :],
                                           lhsT=proj_sb[pbase:pbase + 64, h, :],
                                           rhs=qk_buf[bk][pbase:pbase + 64, :], start=True, stop=True,
                                           tile_position=(pbase, pbase))
                        if is_k:
                            for ts4 in range(4):
                                mm = tensor.matmul(out=ps_ptm[bk][:, h2x * 256 + ts4 * 64:h2x * 256 + (ts4 + 1) * 64],
                                                   lhsT=qk_buf[bk][pbase:pbase + 64, ts4 * 128:(ts4 + 1) * 128],
                                                   rhs=proj_sb[pbase:pbase + 64, h, :], start=True, stop=True,
                                                   tile_position=(pbase, 0))
                    mm.then_inc(TS_, 1)
                    c.inc("TS")
                    c.v[f"pj_mm_{it}"] = c.get("TS")

                @block.scalar
                def _(scalar, oc=oc, nt4=nt4, it=it, bk=bk, is_k=is_k):
                    scalar.wait_ge(TS_, c.get(f"pj_mm_{it}"))
                    ob = oc % 2
                    if nt4 == 0:
                        prev = c.get(f"qkst_{ob}")
                        if prev:
                            scalar.wait_ge(ST, prev)
                    ins = scalar.activation(out=qkbig[ob][:, nt4 * 512:(nt4 + 1) * 512],
                                            in_=ps_pf[bk][:], func=AF.Relu)
                    if is_k:
                        for h2x in range(2):
                            co_ = h2x * 1024 + nt4 * 256
                            ins = scalar.activation(out=kptm_big[ob][:, co_:co_ + 256],
                                                    in_=ps_ptm[bk][:, h2x * 256:(h2x + 1) * 256], func=AF.Relu)
                    ins.then_inc(SC, 1)
                    c.inc("SC")
                    c.v[f"pj_ev_{it}"] = c.get("SC")

            @block.gpsimd
            def _(gpsimd, oc=oc, is_k=is_k):
                ob = oc % 2
                hp = oc % 8
                gpsimd.wait_ge(SC, c.get(f"pj_ev_{oc * 4 + 3}"))
                dstq = (o_kp if is_k else o_qp)[2 * hp:2 * hp + 2].rearrange("h m t -> (h m) t")
                gpsimd.dma_start(out=dstq, in_=qkbig[ob][:]).then_inc(ST, 16)
                c.inc("ST", 16)
                if is_k:
                    for h2x in range(2):
                        gpsimd.dma_start(out=o_kptm[2 * hp + h2x].rearrange("p n m -> p (n m)"),
                                         in_=kptm_big[ob][:, h2x * 1024:(h2x + 1) * 1024]).then_inc(ST, 16)
                        c.inc("ST", 16)
                c.v[f"qkst_{ob}"] = c.get("ST")
        QK_LAST = [c.get("qk_ev_62"), c.get("qk_ev_63")]

        for tt in range(NT):
            for nb in range(2):
                itv = tt * 2 + nb
                pv = itv % 2

                @block.tensor
                def _(tensor, tt=tt, nb=nb, itv=itv, pv=pv):
                    if itv < 2:
                        tensor.wait_ge(VS, QK_LAST[pv])
                    prev = c.get(f"v_ev_{itv - 2}")
                    if prev:
                        tensor.wait_ge(SC, prev)
                    for kc in range(8):
                        tensor.matmul(out=ps_qk[pv][:], lhsT=xnT[:, kc, tt * 128:(tt + 1) * 128],
                                      rhs=qkvw_sb[:, kc, 2 * D + nb * 512:2 * D + (nb + 1) * 512],
                                      start=(kc == 0), stop=False)
                    tensor.matmul(out=ps_qk[pv][:], lhsT=onesrow[:],
                                  rhs=qb_row[:, 2 * D + nb * 512:2 * D + (nb + 1) * 512],
                                  start=False, stop=True).then_inc(TS_, 1)
                    c.inc("TS")
                    c.v[f"v_mm_{itv}"] = c.get("TS")

                @block.scalar
                def _(scalar, tt=tt, nb=nb, itv=itv, pv=pv):
                    scalar.wait_ge(TS_, c.get(f"v_mm_{itv}"))
                    tb = tt % 2
                    if nb == 0:
                        prev = c.get(f"vst_{tb}")
                        if prev:
                            scalar.wait_ge(ST, prev)
                    scalar.copy(out=v_big[tb][:, nb * 512:(nb + 1) * 512], in_=ps_qk[pv][:]).then_inc(SC, 1)
                    c.inc("SC")
                    c.v[f"v_ev_{itv}"] = c.get("SC")

            @block.gpsimd
            def _(gpsimd, tt=tt):
                gpsimd.wait_ge(SC, c.get(f"v_ev_{tt * 2 + 1}"))
                tb = tt % 2
                gpsimd.dma_start(out=vr[tt], in_=v_big[tb][:]).then_inc(ST, 16)
                c.inc("ST", 16)
                c.v[f"vst_{tb}"] = c.get("ST")

        @block.gpsimd
        def _(gpsimd):
            gpsimd.wait_ge(ST, c.get("ST"))

    return nc


def build_a3():
    nc = bass.Bass("TRN2", target_bir_lowering=False)
    i_qp = nc.dram_tensor("i_qp", [NU, M, T], BF16, kind="ExternalInput").ap()
    i_kp = nc.dram_tensor("i_kp", [NU, M, T], BF16, kind="ExternalInput").ap()
    i_kptm = nc.dram_tensor("i_kptm", [NU, 128, NCH, M], BF16, kind="ExternalInput").ap()
    i_vtm = nc.dram_tensor("i_vtm", [NU, 128, NCH, M], BF16, kind="ExternalInput").ap()
    o_att = nc.dram_tensor("o_att", [NU, 128, NCH, M], BF16, kind="ExternalOutput").ap()

    es = ExitStack()
    sb = lambda name, shape, dt: es.enter_context(nc.sbuf_tensor(name, shape, dt))
    with es:
        qp2 = [sb(f"qp2_{i}", [128, T], BF16) for i in range(2)]
        kp2 = [sb(f"kp2_{i}", [128, T], BF16) for i in range(2)]
        kptm_sb = [sb(f"kptm_sb{i}", [128, NCH, M], BF16) for i in range(NU)]
        vtm_sb = [sb(f"vtm_sb{i}", [128, NCH, M], BF16) for i in range(NU)]
        att_u = [sb(f"att_u{i}", [128, NCH, M], BF16) for i in range(NU)]
        nrm_u = [sb(f"nrm_u{i}", [128, NCH], F32) for i in range(NU)]
        recn = sb("recn", [128, NCH], F32)
        recb = sb("recb", [128, NCH], BF16)
        maskU = sb("maskU", [128, 128], BF16)
        idq = sb("idq", [128, 128], F32)
        onescol = sb("onescol", [128, 1], BF16)
        Sf2 = [sb(f"Sf2_{i}", [128, 72], F32) for i in range(2)]
        Sb2 = [sb(f"Sb2_{i}", [128, 72], BF16) for i in range(2)]
        a_t = [sb(f"a_t{i}", [128, 128], BF16) for i in range(4)]
        rec = sb("rec", [128, 4], F32)
        ps_s = [es.enter_context(nc.psum_tensor(f"ps_s{i}", [128, 128], F32)) for i in range(2)]
        ps_c = [es.enter_context(nc.psum_tensor(f"ps_c{i}", [128, 128], F32)) for i in range(2)]
        ps_u = [es.enter_context(nc.psum_tensor(f"ps_u{i}", [128, 128], F32)) for i in range(2)]
        LU = [es.enter_context(nc.semaphore(f"LU{i}")) for i in range(NU)]
        TS_ = es.enter_context(nc.semaphore("TS"))
        VS = es.enter_context(nc.semaphore("VS"))
        SC = es.enter_context(nc.semaphore("SC"))
        ST = es.enter_context(nc.semaphore("ST"))
        IO = es.enter_context(nc.semaphore("IO"))
        block = es.enter_context(nc.Block())
        c = Ctr()

        @block.gpsimd
        def _(gpsimd):
            gpsimd.iota(out=idq[:], pattern=[[1, 128]], base=0, channel_multiplier=-1,
                        allow_small_or_imprecise_dtypes=True).then_inc(IO, 1)

        @block.vector
        def _(vector):
            vector.memset(onescol[:], 1.0)
            for pr in range(2):
                vector.memset(Sf2[pr][:], 0.0)
            vector.wait_ge(IO, 1)
            vector.tensor_scalar(out=maskU[:], in0=idq[:], scalar1=0.0, scalar2=None,
                                 op0=ALU.is_ge)
            vector.drain()
            vector.tensor_copy(out=Sb2[0][:], in_=Sf2[0][:])
            vector.drain()
            vector.tensor_copy(out=Sb2[1][:], in_=Sf2[1][:]).then_inc(VS, 1)
            c.inc("VS")
        SETUP = c.get("VS")

        @block.sync
        def _(sync):
            for u in range(NU):
                pr, b0 = u // 2, (u % 2) * 64
                sync.dma_start(out=qp2[pr][b0:b0 + 64, :], in_=i_qp[u]).then_inc(LU[u], 16)
                sync.dma_start(out=kp2[pr][b0:b0 + 64, :], in_=i_kp[u]).then_inc(LU[u], 16)
                sync.dma_start(out=kptm_sb[u][:], in_=i_kptm[u]).then_inc(LU[u], 16)
                sync.dma_start(out=vtm_sb[u][:], in_=i_vtm[u]).then_inc(LU[u], 16)

        for n in range(NCH):
            for u in range(NU):
                g = n * NU + u
                pb = g % 2
                ab = g % 4

                pr, b0 = u // 2, (u % 2) * 64

                @block.tensor
                def _(tensor, n=n, u=u, g=g, pb=pb, pr=pr, b0=b0):
                    if n == 0:
                        if u == 0:
                            tensor.wait_ge(VS, SETUP)
                        tensor.wait_ge(LU[u], 64)
                    prev = c.get(f"at_ev_{g - 2}")
                    if prev:
                        tensor.wait_ge(VS, prev)
                    tensor.matmul(out=ps_s[pb][:], lhsT=kp2[pr][b0:b0 + 64, n * 128:(n + 1) * 128],
                                  rhs=qp2[pr][b0:b0 + 64, n * 128:(n + 1) * 128], start=True, stop=True,
                                  tile_position=(b0, 0)).then_inc(TS_, 1)
                    c.inc("TS")
                    c.v[f"s_mm_{g}"] = c.get("TS")

                @block.vector
                def _(vector, g=g, pb=pb, ab=ab):
                    vector.wait_ge(TS_, c.get(f"s_mm_{g}"))
                    prev = c.get(f"c_mm_{g - 4}")
                    if prev:
                        vector.wait_ge(TS_, prev)
                    vector.tensor_tensor(out=a_t[ab][:], in0=ps_s[pb][:], in1=maskU[:], op=ALU.mult).then_inc(VS, 1)
                    c.inc("VS")
                    c.v[f"at_ev_{g}"] = c.get("VS")

                @block.tensor
                def _(tensor, n=n, u=u, g=g, pb=pb, ab=ab, pr=pr, b0=b0):
                    tensor.wait_ge(VS, c.get(f"at_ev_{g}"))
                    prev = c.get(f"sfu_{g - 2}")
                    if prev:
                        tensor.wait_ge(VS, prev)
                    prev = c.get(f"up_ev_{g - 4}")
                    if prev:
                        tensor.wait_ge(VS, prev)
                    tensor.matmul(out=ps_c[pb][:, 0:64], lhsT=a_t[ab][:], rhs=vtm_sb[u][:, n, :],
                                  start=True, stop=False)
                    tensor.matmul(out=ps_c[pb][:, 64:65], lhsT=a_t[ab][:], rhs=onescol[:],
                                  start=False, stop=False)
                    tensor.matmul(out=ps_c[pb][:, 0:64], lhsT=qp2[pr][b0:b0 + 64, n * 128:(n + 1) * 128],
                                  rhs=Sb2[pr][b0:b0 + 64, 0:64], start=False, stop=False,
                                  tile_position=(b0, 0))
                    tensor.matmul(out=ps_c[pb][:, 64:65], lhsT=qp2[pr][b0:b0 + 64, n * 128:(n + 1) * 128],
                                  rhs=Sb2[pr][b0:b0 + 64, 64:65], start=False, stop=True,
                                  tile_position=(b0, 0))
                    tensor.matmul(out=ps_u[pb][b0:b0 + 64, 0:64], lhsT=kptm_sb[u][:, n, :],
                                  rhs=vtm_sb[u][:, n, :], start=True, stop=False,
                                  tile_position=(0, b0))
                    tensor.matmul(out=ps_u[pb][b0:b0 + 64, 64:65], lhsT=kptm_sb[u][:, n, :],
                                  rhs=onescol[:], start=False, stop=True,
                                  tile_position=(0, b0)).then_inc(TS_, 1)
                    c.inc("TS")
                    c.v[f"c_mm_{g}"] = c.get("TS")

                @block.vector
                def _(vector, n=n, u=u, g=g, pb=pb, pr=pr, b0=b0):
                    vector.wait_ge(TS_, c.get(f"c_mm_{g}"))
                    vector.tensor_copy(out=att_u[u][:, n, :], in_=ps_c[pb][:, 0:64])
                    vector.tensor_copy(out=nrm_u[u][:, n:n + 1], in_=ps_c[pb][:, 64:65])
                    vector.tensor_tensor(out=Sf2[pr][b0:b0 + 64, 0:65], in0=Sf2[pr][b0:b0 + 64, 0:65],
                                         in1=ps_u[pb][b0:b0 + 64, 0:65], op=ALU.add).then_inc(VS, 1)
                    c.inc("VS")
                    c.v[f"sfu_{g}"] = c.get("VS")
                    vector.drain()
                    vector.tensor_copy(out=Sb2[pr][b0:b0 + 64, 0:65],
                                       in_=Sf2[pr][b0:b0 + 64, 0:65]).then_inc(VS, 1)
                    c.inc("VS")
                    c.v[f"up_ev_{g}"] = c.get("VS")

        for u in range(NU):
            @block.vector
            def _(vector, u=u):
                vector.wait_ge(VS, c.get(f"sfu_{(NCH - 1) * NU + u}"))
                vector.tensor_scalar(out=recn[:], in0=nrm_u[u][:], scalar1=1e-6, scalar2=None, op0=ALU.add)
                vector.drain()
                vector.reciprocal(out=recn[:], in_=recn[:])
                vector.drain()
                vector.tensor_copy(out=recb[:], in_=recn[:])
                vector.drain()
                rb = bass.AP(recb, 0, [[NCH, 128], [1, NCH], [0, M]])
                vector.tensor_tensor(out=att_u[u][:], in0=att_u[u][:], in1=rb, op=ALU.mult).then_inc(VS, 1)
                c.inc("VS")
                c.v[f"attn_{u}"] = c.get("VS")

            @block.gpsimd
            def _(gpsimd, u=u):
                gpsimd.wait_ge(VS, c.get(f"attn_{u}"))
                gpsimd.dma_start(out=o_att[u], in_=att_u[u][:]).then_inc(ST, 16)
                c.inc("ST", 16)

        @block.gpsimd
        def _(gpsimd):
            gpsimd.wait_ge(ST, c.get("ST"))

    return nc




def build_a4():
    nc = bass.Bass("TRN2", target_bir_lowering=False)
    xn_i = nc.dram_tensor("xn", [S, D], F32, kind="ExternalInput").ap()
    x_i = nc.dram_tensor("x", [S, D], F32, kind="ExternalInput").ap()
    attT_i = nc.dram_tensor("attT", [128, 8, S], BF16, kind="ExternalInput").ap()
    outw_i = nc.dram_tensor("out_w", [D, D], F32, kind="ExternalInput").ap()
    outb_i = nc.dram_tensor("out_b", [1, D], F32, kind="ExternalInput").ap()
    gateT_i = nc.dram_tensor("gateT", [D, 16], F32, kind="ExternalInput").ap()
    expw_i = nc.dram_tensor("expw", [16, D, D], F32, kind="ExternalInput").ap()
    expb_i = nc.dram_tensor("expb", [1, 16 * D], F32, kind="ExternalInput").ap()
    ln1wb_i = nc.dram_tensor("ln1wb", [128, D], F32, kind="ExternalInput").ap()
    o_x1 = nc.dram_tensor("o_x1", [S, D], F32, kind="ExternalOutput").ap()
    o_xn1 = nc.dram_tensor("o_xn1", [S, D], F32, kind="ExternalOutput").ap()
    o_h1 = nc.dram_tensor("o_h1", [S, D], F32, kind="ExternalOutput").ap()

    xnr = xn_i.rearrange("(n p) d -> n p d", p=128)
    xr = x_i.rearrange("(n p) d -> n p d", p=128)
    x1r = o_x1.rearrange("(n p) d -> n p d", p=128)
    xn1r = o_xn1.rearrange("(n p) d -> n p d", p=128)
    h1r = o_h1.rearrange("(n p) d -> n p d", p=128)

    es = ExitStack()
    es_g = ExitStack()
    es_x = ExitStack()
    es_m = ExitStack()
    es_s = ExitStack()
    sb = lambda st, name, shape, dt: st.enter_context(nc.sbuf_tensor(name, shape, dt))
    with es:
        consts = sb(es, "consts", [128, 4], F32)
        EPS5, NEGH, EPS6, INVD = (consts[:, i:i + 1] for i in range(4))
        onesrow = sb(es, "onesrow", [1, 128], BF16)
        gateT_sb = sb(es, "gateT_sb", [128, 8, 16], F32)
        coef = sb(es, "coef", [128, 16 * 16], F32)
        stat = sb(es, "stat", [128, 16], F32)
        acc = sb(es, "acc", [128, NT, D], F32)
        ln1wb = sb(es, "ln1wb_sb", [128, D], F32)
        outb_sb = sb(es, "outb_sb", [1, D], BF16)
        xnT = sb(es_x, "xnT", [128, 8, S], BF16)
        identf = sb(es_g, "identf", [128, 128], F32)
        idq = sb(es_g, "idq", [128, 128], F32)
        xnt = [sb(es_g, f"xnt{i}", [128, D], F32) for i in range(2)]
        xTf = sb(es_g, "xTf", [128, 8, 128], F32)
        glog = sb(es_g, "glog", [128, 256], F32)
        gt1 = sb(es_g, "gt1", [128, 240], F32)
        gst = sb(es_g, "gst", [128, 160], F32)
        ps_t = [es_g.enter_context(nc.psum_tensor(f"ps_t{i}", [128, 128], F32)) for i in range(2)]
        ps_g = es_g.enter_context(nc.psum_tensor("ps_g", [128, 256], F32))
        LD0 = es.enter_context(nc.semaphore("LD0"))
        LD1 = es.enter_context(nc.semaphore("LD1"))
        LDp = [LD0, LD1]
        GC = es.enter_context(nc.semaphore("GC"))
        GW0 = es.enter_context(nc.semaphore("GW0"))
        GW1 = es.enter_context(nc.semaphore("GW1"))
        GWp = [GW0, GW1]
        GA = es.enter_context(nc.semaphore("GA"))
        TS_ = es.enter_context(nc.semaphore("TS"))
        VS = es.enter_context(nc.semaphore("VS"))
        SC = es.enter_context(nc.semaphore("SC"))
        ST = es.enter_context(nc.semaphore("ST"))
        IO = es.enter_context(nc.semaphore("IO"))
        PO = es.enter_context(nc.semaphore("PO"))
        GX = es.enter_context(nc.semaphore("GX"))
        block = es.enter_context(nc.Block())
        c = Ctr()

        @block.gpsimd
        def _(gpsimd):
            gpsimd.iota(out=idq[:], pattern=[[1, 128]], base=0, channel_multiplier=-1,
                        allow_small_or_imprecise_dtypes=True).then_inc(IO, 1)
            gpsimd.dma_start(out=gateT_sb[:], in_=gateT_i.rearrange("(k p) e -> p k e", p=128)).then_inc(GC, 16)
            gpsimd.dma_start(out=outb_sb[:], in_=outb_i).then_inc(GC, 16)
            gpsimd.dma_start(out=ln1wb[:], in_=ln1wb_i).then_inc(GC, 16)
            c.inc("GC", 16 * 3)
        GD_ALL = c.get("GC")

        @block.vector
        def _(vector):
            vector.memset(consts[:, 0:1], 1e-5)
            vector.memset(consts[:, 1:2], -0.5)
            vector.memset(consts[:, 2:3], 1e-6)
            vector.memset(consts[:, 3:4], 1.0 / D)
            vector.memset(onesrow[:], 1.0)
            vector.wait_ge(IO, 1)
            vector.tensor_scalar(out=identf[:], in0=idq[:], scalar1=0.0, scalar2=None,
                                 op0=ALU.is_equal).then_inc(VS, 1)
            c.inc("VS")
        SETUP = c.get("VS")

        @block.sync
        def _(sync):
            for i in range(2):
                sync.dma_start(out=xnt[i][:], in_=xnr[i]).then_inc(LDp[i], 16)
            c.v["xn_loads_0"] = 1
            c.v["xn_loads_1"] = 1

        GSTW = 160
        for tt in range(NT):
            pb = tt % 2
            for kc in range(8):
                @block.tensor
                def _(tensor, tt=tt, pb=pb, kc=kc):
                    if tt == 0 and kc == 0:
                        tensor.wait_ge(VS, SETUP)
                    if kc == 0:
                        tensor.wait_ge(LDp[pb], 16 * c.get(f"xn_loads_{pb}"))
                    prev = c.get(f"gtp_ev_{tt}_{kc - 2}") if kc >= 2 else c.get(f"gtp_ev_{tt - 1}_{kc + 6}")
                    if prev:
                        tensor.wait_ge(VS, prev)
                    tensor.transpose(out=ps_t[kc % 2][:], in_=xnt[pb][:, kc * 128:(kc + 1) * 128],
                                     identity=identf[:]).then_inc(TS_, 1)
                    c.inc("TS")
                    c.v[f"gtp_mm_{tt}_{kc}"] = c.get("TS")

                @block.vector
                def _(vector, tt=tt, kc=kc):
                    vector.wait_ge(TS_, c.get(f"gtp_mm_{tt}_{kc}"))
                    if kc == 0:
                        prev = c.get(f"glog_mm_{tt - 1}")
                        if prev:
                            vector.wait_ge(TS_, prev)
                        prev = c.get(f"xnTc_{tt - 1}")
                        if prev:
                            vector.wait_ge(GX, prev)
                    vector.tensor_copy(out=xTf[:, kc, :], in_=ps_t[kc % 2][:]).then_inc(VS, 1)
                    c.inc("VS")
                    c.v[f"gtp_ev_{tt}_{kc}"] = c.get("VS")

            @block.gpsimd
            def _(gpsimd, tt=tt):
                gpsimd.wait_ge(VS, c.get(f"gtp_ev_{tt}_7"))
                gpsimd.dma_start(out=xnT[:, :, tt * 128:(tt + 1) * 128], in_=xTf[:]).then_inc(GX, 16)
                c.inc("GX", 16)
                c.v[f"xnTc_{tt}"] = c.get("GX")

            @block.tensor
            def _(tensor, tt=tt):
                tensor.wait_ge(VS, c.get(f"gtp_ev_{tt}_7"))
                if tt == 0:
                    tensor.wait_ge(GC, GD_ALL)
                for kc in range(8):
                    mm = tensor.matmul(out=ps_g[:, tt * 16:(tt + 1) * 16], lhsT=xTf[:, kc, :],
                                       rhs=gateT_sb[:, kc, :], start=(kc == 0), stop=(kc == 7))
                mm.then_inc(TS_, 1)
                c.inc("TS")
                c.v[f"glog_mm_{tt}"] = c.get("TS")

            @block.sync
            def _(sync, tt=tt, pb=pb):
                if tt + 2 < NT:
                    sync.wait_ge(TS_, c.get(f"gtp_mm_{tt}_7"))
                    sync.dma_start(out=xnt[pb][:], in_=xnr[tt + 2]).then_inc(LDp[pb], 16)
                    c.inc(f"xn_loads_{pb}")

        def BC(off):
            return bass.AP(gst, off, [[GSTW, 128], [1, 16], [0, 16]])

        def segtree(vector, src, op):
            vector.tensor_tensor(out=gt1[:, 0:128], in0=src[:, 0:256:2], in1=src[:, 1:256:2], op=op)
            vector.drain()
            vector.tensor_tensor(out=gt1[:, 128:192], in0=gt1[:, 0:128:2], in1=gt1[:, 1:128:2], op=op)
            vector.drain()
            vector.tensor_tensor(out=gt1[:, 192:224], in0=gt1[:, 128:192:2], in1=gt1[:, 129:192:2], op=op)
            vector.drain()
            vector.tensor_tensor(out=gt1[:, 224:240], in0=gt1[:, 192:224:2], in1=gt1[:, 193:224:2], op=op)
            vector.drain()

        @block.vector
        def _(vector):
            vector.wait_ge(TS_, c.get(f"glog_mm_{NT - 1}"))
            vector.tensor_copy(out=glog[:], in_=ps_g[:])
            vector.drain()
            segtree(vector, glog, ALU.max)
            vector.tensor_copy(out=gst[:, 0:16], in_=gt1[:, 224:240])
            vector.drain()
            vector.tensor_tensor(out=glog[:], in0=glog[:], in1=BC(0), op=ALU.subtract).then_inc(VS, 1)
            c.inc("VS")
        GSUB = c.get("VS")

        @block.scalar
        def _(scalar):
            scalar.wait_ge(VS, GSUB)
            scalar.activation(out=glog[:], in_=glog[:], func=AF.Exp).then_inc(SC, 1)
            c.inc("SC")
        GEXP = c.get("SC")

        @block.vector
        def _(vector):
            vector.wait_ge(SC, GEXP)
            segtree(vector, glog, ALU.add)
            vector.reciprocal(out=gst[:, 16:32], in_=gt1[:, 224:240])
            vector.drain()
            vector.tensor_tensor(out=glog[:], in0=glog[:], in1=BC(16), op=ALU.mult)
            vector.drain()
            segtree(vector, glog, ALU.max)
            vector.tensor_copy(out=gst[:, 32:48], in_=gt1[:, 224:240])
            vector.drain()
            vector.tensor_tensor(out=coef[:], in0=glog[:], in1=BC(32), op=ALU.is_ge)
            vector.drain()
            segtree(vector, coef, ALU.add)
            vector.tensor_copy(out=gst[:, 48:64], in_=gt1[:, 224:240])
            vector.drain()
            vector.scalar_tensor_tensor(out=coef[:], in0=coef[:], scalar=-1.0, in1=glog[:],
                                        op0=ALU.mult, op1=ALU.mult)
            vector.drain()
            vector.tensor_tensor(out=coef[:], in0=glog[:], in1=coef[:], op=ALU.add)
            vector.drain()
            segtree(vector, coef, ALU.max)
            vector.tensor_scalar(out=gst[:, 64:80], in0=gst[:, 48:64], scalar1=2.0, scalar2=None,
                                 op0=ALU.is_ge)
            vector.tensor_tensor(out=gst[:, 80:96], in0=gst[:, 32:48], in1=gt1[:, 224:240],
                                 op=ALU.subtract)
            vector.drain()
            vector.tensor_tensor(out=gst[:, 96:112], in0=gst[:, 64:80], in1=gst[:, 80:96], op=ALU.mult)
            vector.drain()
            vector.tensor_tensor(out=gst[:, 96:112], in0=gt1[:, 224:240], in1=gst[:, 96:112],
                                 op=ALU.add)
            vector.drain()
            vector.tensor_tensor(out=coef[:], in0=glog[:], in1=BC(96), op=ALU.is_ge)
            vector.drain()
            vector.tensor_tensor(out=coef[:], in0=glog[:], in1=coef[:], op=ALU.mult).then_inc(VS, 1)
            c.inc("VS")
        COEF_ALL = c.get("VS")

        wbuf = [sb(es_m, f"wbuf{i}", [128, 8, D], BF16) for i in range(2)]
        expb_row = [sb(es_m, f"expb_row{i}", [1, D], BF16) for i in range(2)]
        ge_buf = [sb(es_m, f"ge_buf{i}", [128, 512], F32) for i in range(4)]
        ps_e = [es_m.enter_context(nc.psum_tensor(f"ps_e{i}", [128, 512], F32)) for i in range(4)]

        expw_r = expw_i.rearrange("e (k p) d -> e p k d", p=128)

        def emit_wload_g(gpsimd, e):
            p = e % 2
            gpsimd.dma_start(out=wbuf[p][:], in_=expw_r[e]).then_inc(GWp[p], 16)
            gpsimd.dma_start(out=expb_row[p][:], in_=expb_i[0:1, e * D:(e + 1) * D]).then_inc(GWp[p], 16)
            c.inc(f"gw_{p}", 2)
            c.v[f"gwv_{e}"] = 16 * c.get(f"gw_{p}")

        @block.gpsimd
        def _(gpsimd):
            emit_wload_g(gpsimd, 0)
            emit_wload_g(gpsimd, 1)

        for e in range(16):
            eb = e % 2
            for tt in range(NT):
                it = e * NT + tt

                @block.tensor
                def _(tensor, e=e, eb=eb, tt=tt, it=it):
                    if e == 0 and tt == 0:
                        tensor.wait_ge(VS, COEF_ALL)
                        tensor.wait_ge(GX, c.get("GX"))
                    if tt == 0:
                        tensor.wait_ge(GWp[eb], c.get(f"gwv_{e}"))
                    prev = c.get(f"moe_ev_{it - 2}")
                    if prev:
                        tensor.wait_ge(SC, prev)
                    for nb in range(2):
                        bk = 2 * (it % 2) + nb
                        for kc in range(8):
                            tensor.matmul(out=ps_e[bk][:], lhsT=xnT[:, kc, tt * 128:(tt + 1) * 128],
                                          rhs=wbuf[eb][:, kc, nb * 512:(nb + 1) * 512],
                                          start=(kc == 0), stop=False)
                        tensor.matmul(out=ps_e[bk][:], lhsT=onesrow[:],
                                      rhs=expb_row[eb][:, nb * 512:(nb + 1) * 512],
                                      start=False, stop=True).then_inc(TS_, 1)
                        c.inc("TS")
                    c.v[f"moe_mm_{it}"] = c.get("TS")

                @block.scalar
                def _(scalar, e=e, tt=tt, it=it):
                    scalar.wait_ge(TS_, c.get(f"moe_mm_{it}"))
                    prev = c.get(f"moe_acc_{it - 2}")
                    if prev:
                        scalar.wait_ge(PO, prev)
                    for nb in range(2):
                        bk = 2 * (it % 2) + nb
                        scalar.activation(out=ge_buf[bk][:], in_=ps_e[bk][:], func=AF.Relu,
                                          scale=coef[:, tt * 16 + e:tt * 16 + e + 1]).then_inc(SC, 1)
                        c.inc("SC")
                    c.v[f"moe_ev_{it}"] = c.get("SC")

                @block.gpsimd
                def _(gpsimd, e=e, tt=tt, it=it):
                    gpsimd.wait_ge(SC, c.get(f"moe_ev_{it}"))
                    for nb in range(2):
                        bk = 2 * (it % 2) + nb
                        if e == 0:
                            ins = gpsimd.tensor_copy(out=acc[:, tt, nb * 512:(nb + 1) * 512], in_=ge_buf[bk][:])
                        else:
                            ins = gpsimd.tensor_tensor(out=acc[:, tt, nb * 512:(nb + 1) * 512],
                                                       in0=acc[:, tt, nb * 512:(nb + 1) * 512],
                                                       in1=ge_buf[bk][:], op=ALU.add)
                    ins.then_inc(PO, 1)
                    c.inc("PO")
                    c.v[f"moe_acc_{it}"] = c.get("PO")
                    if e == 15:
                        c.v[f"moe_acc_t{tt}"] = c.get("PO")

            @block.gpsimd
            def _(gpsimd, e=e):
                if e + 2 < 16:
                    gpsimd.wait_ge(TS_, c.get(f"moe_mm_{e * NT + NT - 1}"))
                    emit_wload_g(gpsimd, e + 2)
        MOE_MM_ALL = c.get("TS")
        MOE_VS_ALL = c.get("VS")
        MOE_PO_ALL = c.get("PO")
        es_m.close()
        es_g.close()
        es_x.close()

        ps_s = [es_s.enter_context(nc.psum_tensor(f"ps_s{i}", [128, 512], F32)) for i in range(4)]
        attT_sb = sb(es_s, "attT_sb", [128, 8, S], BF16)
        outw_sb = sb(es_s, "outw_sb", [128, 8, D], BF16)
        xt = [sb(es_s, f"xt{i}", [128, D], F32) for i in range(2)]
        g0t = [sb(es_s, f"g0t{i}", [128, D], F32) for i in range(2)]
        x1t = [sb(es_s, f"x1t{i}", [128, D], F32) for i in range(2)]
        xn1t = [sb(es_s, f"xn1t{i}", [128, D], F32) for i in range(2)]
        h1t = [sb(es_s, f"h1t{i}", [128, D], F32) for i in range(2)]

        outw_r = outw_i.rearrange("(k p) d -> p k d", p=128)

        @block.gpsimd
        def _(gpsimd):
            gpsimd.wait_ge(TS_, MOE_MM_ALL)
            gpsimd.wait_ge(VS, MOE_VS_ALL)
            gpsimd.wait_ge(PO, MOE_PO_ALL)
            gpsimd.dma_start(out=outw_sb[:, 0:4, :], in_=outw_r[:, 0:4, :]).then_inc(GA, 16)
            gpsimd.dma_start(out=outw_sb[:, 4:8, :], in_=outw_r[:, 4:8, :]).then_inc(GA, 16)
            gpsimd.dma_start(out=attT_sb[:, 0:4, :], in_=attT_i[:, 0:4, :]).then_inc(GA, 16)
            c.inc("GA", 16 * 3)

        @block.sync
        def _(sync):
            sync.wait_ge(TS_, MOE_MM_ALL)
            sync.wait_ge(VS, MOE_VS_ALL)
            sync.wait_ge(PO, MOE_PO_ALL)
            sync.dma_start(out=attT_sb[:, 4:8, :], in_=attT_i[:, 4:8, :]).then_inc(GA, 16)
            c.inc("GA", 16 * 1)
        GA_ALL = c.get("GA")

        for tt in range(NT):
            pb = tt % 2
            ob = tt % 2

            @block.sync
            def _(sync, tt=tt, pb=pb):
                sync.wait_ge(LDp[pb], (c.get(f"xn_loads_{pb}") + c.get(f"x_loads_{pb}")) * 16)
                prev = c.get(f"x1_{tt - 2}")
                if prev:
                    sync.wait_ge(VS, prev)
                sync.dma_start(out=xt[pb][:], in_=xr[tt]).then_inc(LDp[pb], 16)
                c.inc(f"x_loads_{pb}")
                c.v[f"xld_{tt}"] = (c.get(f"xn_loads_{pb}") + c.get(f"x_loads_{pb}")) * 16

            @block.tensor
            def _(tensor, tt=tt):
                if tt == 0:
                    tensor.wait_ge(GA, GA_ALL)
                prev = c.get(f"s0_ev_{tt - 2}")
                if prev:
                    tensor.wait_ge(VS, prev)
                for nb in range(2):
                    bk = 2 * (tt % 2) + nb
                    for kc in range(8):
                        tensor.matmul(out=ps_s[bk][:], lhsT=attT_sb[:, kc, tt * 128:(tt + 1) * 128],
                                      rhs=outw_sb[:, kc, nb * 512:(nb + 1) * 512],
                                      start=(kc == 0), stop=False)
                    tensor.matmul(out=ps_s[bk][:], lhsT=onesrow[:],
                                  rhs=outb_sb[:, nb * 512:(nb + 1) * 512],
                                  start=False, stop=True).then_inc(TS_, 1)
                    c.inc("TS")
                c.v[f"s0_mm_{tt}"] = c.get("TS")

            @block.scalar
            def _(scalar, tt=tt, ob=ob):
                scalar.wait_ge(PO, c.get(f"moe_acc_t{tt}"))
                prev = c.get(f"comb_done_{tt - 2}")
                if prev:
                    scalar.wait_ge(VS, prev)
                scalar.activation(out=g0t[ob][:], in_=acc[:, tt, :], func=AF.Silu).then_inc(SC, 1)
                c.inc("SC")
                c.v[f"g0_{tt}"] = c.get("SC")

            @block.vector
            def _(vector, tt=tt, pb=pb, ob=ob):
                vector.wait_ge(TS_, c.get(f"s0_mm_{tt}"))
                vector.wait_ge(SC, c.get(f"g0_{tt}"))
                vector.wait_ge(LDp[pb], c.get(f"xld_{tt}"))
                prev = c.get(f"stv_{tt - 2}")
                if prev:
                    vector.wait_ge(ST, prev)
                for nb in range(2):
                    bk = 2 * (tt % 2) + nb
                    vector.tensor_tensor(out=x1t[ob][:, nb * 512:(nb + 1) * 512], in0=ps_s[bk][:],
                                         in1=g0t[ob][:, nb * 512:(nb + 1) * 512], op=ALU.mult)
                vector.drain()
                vector.tensor_tensor(out=x1t[ob][:], in0=xt[pb][:], in1=x1t[ob][:], op=ALU.add).then_inc(VS, 1)
                c.inc("VS")
                c.v[f"s0_ev_{tt}"] = c.get("VS")
                c.v[f"x1_{tt}"] = c.get("VS")

            @block.scalar
            def _(scalar, tt=tt, ob=ob):
                scalar.wait_ge(VS, c.get(f"x1_{tt}"))
                co = 8 + (tt % 2) * 4
                scalar.activation(out=xn1t[ob][:], in_=x1t[ob][:], func=AF.Copy, accum_out=stat[:, co:co + 1])
                scalar.drain()
                scalar.activation(out=xn1t[ob][:],
                                  in_=x1t[ob][:], func=AF.Square, accum_out=stat[:, co + 1:co + 2]).then_inc(SC, 1)
                c.inc("SC")
                c.v[f"lnstat_{tt}"] = c.get("SC")

            @block.vector
            def _(vector, tt=tt):
                vector.wait_ge(SC, c.get(f"lnstat_{tt}"))
                co = 8 + (tt % 2) * 4
                vector.tensor_scalar(out=stat[:, co:co + 1], in0=stat[:, co:co + 1], scalar1=1.0 / D, scalar2=None, op0=ALU.mult)
                vector.tensor_scalar(out=stat[:, co + 1:co + 2], in0=stat[:, co + 1:co + 2], scalar1=1.0 / D, scalar2=None, op0=ALU.mult)
                vector.drain()
                vector.tensor_tensor(out=stat[:, co + 2:co + 3], in0=stat[:, co:co + 1], in1=stat[:, co:co + 1], op=ALU.mult)
                vector.drain()
                vector.tensor_tensor(out=stat[:, co + 2:co + 3], in0=stat[:, co + 1:co + 2], in1=stat[:, co + 2:co + 3], op=ALU.subtract).then_inc(VS, 1)
                c.inc("VS")
                c.v[f"lnvar_{tt}"] = c.get("VS")

            @block.scalar
            def _(scalar, tt=tt):
                scalar.wait_ge(VS, c.get(f"lnvar_{tt}"))
                co = 8 + (tt % 2) * 4
                scalar.activation(out=stat[:, co + 3:co + 4], in_=stat[:, co + 2:co + 3], func=AF.Ln, bias=EPS5)
                scalar.drain()
                scalar.activation(out=stat[:, co + 3:co + 4], in_=stat[:, co + 3:co + 4], func=AF.Exp, scale=NEGH).then_inc(SC, 1)
                c.inc("SC")
                c.v[f"lnrstd_{tt}"] = c.get("SC")

            @block.vector
            def _(vector, tt=tt, ob=ob):
                vector.wait_ge(SC, c.get(f"lnrstd_{tt}"))
                co = 8 + (tt % 2) * 4
                vector.tensor_scalar(out=xn1t[ob][:], in0=x1t[ob][:], scalar1=stat[:, co:co + 1],
                                     scalar2=stat[:, co + 3:co + 4], op0=ALU.subtract, op1=ALU.mult).then_inc(VS, 1)
                c.inc("VS")
                c.v[f"xn1_{tt}"] = c.get("VS")

            @block.scalar
            def _(scalar, tt=tt, ob=ob):
                scalar.wait_ge(VS, c.get(f"xn1_{tt}"))
                co = 8 + (tt % 2) * 4
                scalar.activation(out=h1t[ob][:], in_=xn1t[ob][:], func=AF.Square, accum_out=stat[:, co + 1:co + 2])
                scalar.drain()
                scalar.activation(out=stat[:, co + 2:co + 3], in_=stat[:, co + 1:co + 2], func=AF.Ln, bias=EPS6, scale=INVD)
                scalar.drain()
                scalar.activation(out=stat[:, co + 2:co + 3], in_=stat[:, co + 2:co + 3], func=AF.Exp, scale=NEGH).then_inc(SC, 1)
                c.inc("SC")
                c.v[f"rms_{tt}"] = c.get("SC")

            @block.vector
            def _(vector, tt=tt, ob=ob):
                vector.wait_ge(SC, c.get(f"rms_{tt}"))
                co = 8 + (tt % 2) * 4
                if tt == 0:
                    vector.wait_ge(GC, GD_ALL)
                vector.tensor_scalar(out=h1t[ob][:], in0=xn1t[ob][:], scalar1=stat[:, co + 2:co + 3],
                                     scalar2=None, op0=ALU.mult)
                vector.drain()
                vector.tensor_tensor(out=h1t[ob][:], in0=h1t[ob][:], in1=ln1wb[:], op=ALU.mult).then_inc(VS, 1)
                c.inc("VS")
                c.v[f"comb_done_{tt}"] = c.get("VS")

            @block.gpsimd
            def _(gpsimd, tt=tt, ob=ob):
                gpsimd.wait_ge(VS, c.get(f"comb_done_{tt}"))
                gpsimd.dma_start(out=x1r[tt], in_=x1t[ob][:]).then_inc(ST, 16)
                gpsimd.dma_start(out=xn1r[tt], in_=xn1t[ob][:]).then_inc(ST, 16)
                gpsimd.dma_start(out=h1r[tt], in_=h1t[ob][:]).then_inc(ST, 16)
                c.inc("ST", 48)
                c.v[f"stv_{tt}"] = c.get("ST")

        @block.gpsimd
        def _(gpsimd):
            gpsimd.wait_ge(ST, c.get("ST"))
        es_s.close()

    return nc



def build_b2():
    nc = bass.Bass("TRN2", target_bir_lowering=False)
    h1T_i = nc.dram_tensor("h1T", [D, S + 1], F32, kind="ExternalInput").ap()
    tmp_i = nc.dram_tensor("tmp", [128, 24], F32, kind="ExternalInput").ap()
    wk_i = nc.dram_tensor("wk", [D, D], F32, kind="ExternalInput").ap()
    wv_i = nc.dram_tensor("wv", [D, D], F32, kind="ExternalInput").ap()
    wr_i = nc.dram_tensor("wr", [D, D], F32, kind="ExternalInput").ap()
    o_kT = nc.dram_tensor("o_kT", [D, S], F32, kind="ExternalOutput").ap()
    o_vT = nc.dram_tensor("o_vT", [D, S], F32, kind="ExternalOutput").ap()
    o_rT = nc.dram_tensor("o_rT", [D, S], F32, kind="ExternalOutput").ap()
    W_IN = [wk_i, wv_i, wr_i]
    W_OUT = [o_kT, o_vT, o_rT]

    es = ExitStack()
    sb = lambda name, shape, dt: es.enter_context(nc.sbuf_tensor(name, shape, dt))
    with es:
        h1c = [sb(f"h1c{i}", [128, S + 1], F32) for i in range(2)]
        tmp_sb = sb("tmp_sb", [128, 24], F32)
        dscr = sb("dscr", [128, S], F32)
        xf = [sb(f"xf{i}", [128, 8, S], BF16) for i in range(3)]
        wslab = [sb(f"wslab{i}", [128, 8, 512], BF16) for i in range(2)]
        ob_ = [sb(f"ob{i}", [128, S], F32) for i in range(2)]
        ps = [es.enter_context(nc.psum_tensor(f"ps{i}", [128, 512], F32)) for i in range(4)]
        LD0 = es.enter_context(nc.semaphore("LD0"))
        LD1 = es.enter_context(nc.semaphore("LD1"))
        LDp = [LD0, LD1]
        GD = es.enter_context(nc.semaphore("GD"))
        TS_ = es.enter_context(nc.semaphore("TS"))
        VS = es.enter_context(nc.semaphore("VS"))
        SC = es.enter_context(nc.semaphore("SC"))
        ST = es.enter_context(nc.semaphore("ST"))
        block = es.enter_context(nc.Block())
        c = Ctr()

        GC = es.enter_context(nc.semaphore("GC"))

        @block.gpsimd
        def _(gpsimd):
            gpsimd.dma_start(out=tmp_sb[:], in_=tmp_i).then_inc(GC, 16)
            c.inc("GC", 16)
        GD0 = c.get("GC")

        @block.sync
        def _(sync):
            for i in range(2):
                sync.dma_start(out=h1c[i][:], in_=h1T_i[i * 128:(i + 1) * 128, :]).then_inc(LDp[i], 16)
                c.inc(f"hl_{i}")

        for cc in range(8):
            pb = cc % 2

            @block.vector
            def _(vector, cc=cc, pb=pb):
                vector.wait_ge(LDp[pb], 16 * c.get(f"hl_{pb}"))
                if cc == 0:
                    vector.wait_ge(GC, GD0)
                vector.tensor_tensor(out=dscr[:], in0=h1c[pb][:, 1:S + 1], in1=h1c[pb][:, 0:S], op=ALU.subtract)
                vector.drain()
                for w in range(3):
                    vector.scalar_tensor_tensor(out=xf[w][:, cc, :], in0=dscr[:],
                                                scalar=tmp_sb[:, 8 * w + cc:8 * w + cc + 1],
                                                in1=h1c[pb][:, 0:S], op0=ALU.mult, op1=ALU.add).then_inc(VS, 1)
                    c.inc("VS")
                c.v[f"mix_{cc}"] = c.get("VS")

            @block.sync
            def _(sync, cc=cc, pb=pb):
                if cc + 2 < 8:
                    sync.wait_ge(LDp[pb], 16 * c.get(f"hl_{pb}"))
                    sync.wait_ge(VS, c.get(f"mix_{cc}"))
                    sync.dma_start(out=h1c[pb][:], in_=h1T_i[(cc + 2) * 128:(cc + 3) * 128, :]).then_inc(LDp[pb], 16)
                    c.inc(f"hl_{pb}")
        MIX_ALL = c.get("VS")

        first = [True]
        for w in range(3):
            w_r = W_IN[w].rearrange("(k p) o -> p k o", p=128)
            for og in range(2):
                sl_i = w * 2 + og
                slb = sl_i % 2

                def emit_slab(q, sl_i=sl_i, slb=slb, w_r=w_r, og=og):
                    prev = c.get(f"wmm_{sl_i - 2}")
                    if prev:
                        q.wait_ge(TS_, prev)
                    q.dma_start(out=wslab[slb][:], in_=w_r[:, :, og * 512:(og + 1) * 512]).then_inc(GD, 16)
                    c.inc("GD", 16)
                    c.v[f"wld_{sl_i}"] = c.get("GD")

                @block.gpsimd
                def _(gpsimd, f=emit_slab):
                    f(gpsimd)

                for oc4 in range(4):
                    oc = og * 4 + oc4
                    ocb = oc4 % 2
                    for nt4 in range(4):
                        it = sl_i * 16 + oc4 * 4 + nt4
                        bk = it % 4

                        @block.tensor
                        def _(tensor, w=w, sl_i=sl_i, slb=slb, oc4=oc4, nt4=nt4, it=it, bk=bk):
                            if oc4 == 0 and nt4 == 0:
                                tensor.wait_ge(GD, c.get(f"wld_{sl_i}"))
                            if first[0]:
                                tensor.wait_ge(VS, MIX_ALL)
                                first[0] = False
                            prev = c.get(f"wev_{it - 4}")
                            if prev:
                                tensor.wait_ge(SC, prev)
                            for kc in range(8):
                                mm = tensor.matmul(out=ps[bk][:],
                                                   lhsT=wslab[slb][:, kc, oc4 * 128:(oc4 + 1) * 128],
                                                   rhs=xf[w][:, kc, nt4 * 512:(nt4 + 1) * 512],
                                                   start=(kc == 0), stop=(kc == 7))
                            mm.then_inc(TS_, 1)
                            c.inc("TS")
                            c.v[f"wmm_{sl_i}"] = c.get("TS")
                            c.v[f"wmm_it_{it}"] = c.get("TS")

                        @block.scalar
                        def _(scalar, w=w, oc=oc, ocb=ocb, nt4=nt4, it=it, bk=bk):
                            scalar.wait_ge(TS_, c.get(f"wmm_it_{it}"))
                            if nt4 == 0:
                                prev = c.get(f"gob_{ocb}")
                                if prev:
                                    scalar.wait_ge(ST, prev)
                            if w == 2:
                                scalar.activation(out=ob_[ocb][:, nt4 * 512:(nt4 + 1) * 512],
                                                  in_=ps[bk][:], func=AF.Sigmoid).then_inc(SC, 1)
                            else:
                                scalar.copy(out=ob_[ocb][:, nt4 * 512:(nt4 + 1) * 512],
                                            in_=ps[bk][:]).then_inc(SC, 1)
                            c.inc("SC")
                            c.v[f"wev_{it}"] = c.get("SC")

                    @block.gpsimd
                    def _(gpsimd, w=w, oc=oc, ocb=ocb, sl_i=sl_i, oc4=oc4):
                        gpsimd.wait_ge(SC, c.get(f"wev_{sl_i * 16 + oc4 * 4 + 3}"))
                        gpsimd.dma_start(out=W_OUT[w][oc * 128:(oc + 1) * 128, :],
                                         in_=ob_[ocb][:]).then_inc(ST, 16)
                        c.inc("ST", 16)
                        c.v[f"gob_{ocb}"] = c.get("ST")

        @block.gpsimd
        def _(gpsimd):
            gpsimd.wait_ge(ST, c.get("ST"))

    return nc


NBLK = 2
NSTR = 4
SW = T // NSTR


def build_b3():
    nc = bass.Bass("TRN2", target_bir_lowering=False)
    kT_i = nc.dram_tensor("kT", [NBLK, 128, T], F32, kind="ExternalInput").ap()
    vT_i = nc.dram_tensor("vT", [NBLK, 128, T], F32, kind="ExternalInput").ap()
    up_i = nc.dram_tensor("up", [128, NBLK], F32, kind="ExternalInput").ap()
    tdp_i = nc.dram_tensor("tdp", [128, NBLK], F32, kind="ExternalInput").ap()
    o_wkv = nc.dram_tensor("o_wkv", [NBLK, 128, T], F32, kind="ExternalOutput").ap()

    es = ExitStack()
    sb = lambda name, shape, dt: es.enter_context(nc.sbuf_tensor(name, shape, dt))
    with es:
        cst3 = sb("cst3", [128, 1], F32)
        kt = [sb(f"kt{i}", [128, SW], F32) for i in range(2)]
        vt = [sb(f"vt{i}", [128, SW], F32) for i in range(2)]
        vec = sb("vec", [128, 8], F32)
        EW = sb("EW", [128, SW + 1], F32)
        ekvb = sb("ekvb", [128, SW + 1], F32)
        ekb = sb("ekb", [128, SW + 1], F32)
        Aout = sb("Aout", [128, SW + 1], F32)
        Bout = sb("Bout", [128, SW + 1], F32)
        ekt = sb("ekt", [128, SW], F32)
        n1 = sb("n1", [128, SW], F32)
        d1 = sb("d1", [128, SW], F32)
        d2 = sb("d2", [128, SW], F32)
        wout = [sb(f"wout{i}", [128, SW], F32) for i in range(2)]
        LD0 = es.enter_context(nc.semaphore("LD0"))
        LD1 = es.enter_context(nc.semaphore("LD1"))
        LDp = [LD0, LD1]
        GD = es.enter_context(nc.semaphore("GD"))
        VS = es.enter_context(nc.semaphore("VS"))
        SC = es.enter_context(nc.semaphore("SC"))
        ST = es.enter_context(nc.semaphore("ST"))
        block = es.enter_context(nc.Block())
        c = Ctr()

        @block.gpsimd
        def _(gpsimd):
            gpsimd.dma_start(out=vec[:, 0:2], in_=up_i).then_inc(GD, 16)
            gpsimd.dma_start(out=vec[:, 2:4], in_=tdp_i).then_inc(GD, 16)
            c.inc("GD", 32)
        GD0 = c.get("GD")

        @block.vector
        def _(vector):
            vector.memset(cst3[:], -1.0).then_inc(VS, 1)
            c.inc("VS")
        NEG1 = c.get("VS")

        @block.sync
        def _(sync):
            for i in range(2):
                sync.dma_start(out=kt[i][:], in_=kT_i[0, :, i * SW:(i + 1) * SW]).then_inc(LDp[i], 16)
                sync.dma_start(out=vt[i][:], in_=vT_i[0, :, i * SW:(i + 1) * SW]).then_inc(LDp[i], 16)
                c.inc(f"kl_{i}", 2)

        for bc in range(NBLK):
            @block.scalar
            def _(scalar, bc=bc):
                scalar.wait_ge(GD, GD0)
                if bc > 0:
                    scalar.wait_ge(VS, c.get(f"wkvd_{bc - 1}_{NSTR - 1}"))
                scalar.activation(out=vec[:, 6:7], in_=vec[:, 2 + bc:3 + bc], func=AF.Exp).then_inc(SC, 1)
                c.inc("SC")
            E1 = c.get("SC")

            @block.vector
            def _(vector, bc=bc, E1=E1):
                vector.wait_ge(SC, E1)
                vector.tensor_scalar(out=vec[:, 6:7], in0=vec[:, 6:7], scalar1=-1.0, scalar2=-20.0,
                                     op0=ALU.mult, op1=ALU.max)
                vector.drain()
                vector.tensor_scalar(out=vec[:, 6:7], in0=vec[:, 6:7], scalar1=20.0, scalar2=None,
                                     op0=ALU.min).then_inc(VS, 1)
                c.inc("VS")
            E2 = c.get("VS")

            @block.scalar
            def _(scalar, E2=E2):
                scalar.wait_ge(VS, E2)
                scalar.activation(out=vec[:, 7:8], in_=vec[:, 6:7], func=AF.Exp).then_inc(SC, 1)
                c.inc("SC")
            E3 = c.get("SC")

            @block.vector
            def _(vector, bc=bc, E3=E3):
                vector.wait_ge(SC, E3)
                vector.memset(EW[:, 0:1], 1.0)
                vector.memset(ekvb[:, 0:1], 0.0)
                vector.memset(ekb[:, 0:1], 0.0)
                vector.memset(vec[:, 4:5], 0.0)
                vector.memset(vec[:, 5:6], 0.0)
                ew_b = bass.AP(vec, 7, [[8, 128], [0, SW]])
                vector.tensor_copy(out=EW[:, 1:SW + 1], in_=ew_b).then_inc(VS, 1)
                c.inc("VS")
                c.v[f"blkinit_{bc}"] = c.get("VS")

            for st in range(NSTR):
                pb = st % 2
                gl = bc * NSTR + st

                @block.vector
                def _(vector, bc=bc, st=st, pb=pb, gl=gl):
                    vector.wait_ge(VS, c.get(f"blkinit_{bc}"))
                    vector.wait_ge(LDp[pb], 16 * c.get(f"kl_{pb}"))
                    prev = c.get(f"rden_{gl - 1}")
                    if prev:
                        vector.wait_ge(SC, prev)
                    vector.drain()
                    vector.tensor_scalar(out=d2[:], in0=kt[pb][:], scalar1=-20.0, scalar2=20.0,
                                         op0=ALU.max, op1=ALU.min)
                    vector.tensor_scalar(out=n1[:], in0=kt[pb][:], scalar1=vec[:, 0 + bc:1 + bc], scalar2=-20.0,
                                         op0=ALU.add, op1=ALU.max)
                    vector.drain()
                    vector.tensor_scalar(out=n1[:], in0=n1[:], scalar1=20.0, scalar2=None,
                                         op0=ALU.min).then_inc(VS, 1)
                    c.inc("VS")
                    c.v[f"clip_{gl}"] = c.get("VS")

                @block.scalar
                def _(scalar, gl=gl, pb=pb):
                    scalar.wait_ge(VS, c.get(f"clip_{gl}"))
                    scalar.activation(out=ekb[:, 1:SW + 1], in_=d2[:], func=AF.Exp)
                    scalar.drain()
                    scalar.activation(out=ekt[:], in_=n1[:], func=AF.Exp).then_inc(SC, 1)
                    c.inc("SC")
                    c.v[f"sc_ekt_{gl}"] = c.get("SC")

                @block.vector
                def _(vector, bc=bc, st=st, pb=pb, gl=gl):
                    vector.wait_ge(SC, c.get(f"sc_ekt_{gl}"))
                    vector.tensor_tensor(out=ekvb[:, 1:SW + 1], in0=ekb[:, 1:SW + 1], in1=vt[pb][:], op=ALU.mult)
                    vector.drain()
                    vector.tensor_tensor_scan(out=Aout[:], data0=EW[:], data1=ekvb[:],
                                              initial=vec[:, 4:5], op0=ALU.mult, op1=ALU.add)
                    vector.tensor_tensor_scan(out=Bout[:], data0=EW[:], data1=ekb[:],
                                              initial=vec[:, 5:6], op0=ALU.mult, op1=ALU.add)
                    vector.drain()
                    vector.tensor_copy(out=vec[:, 4:5], in_=Aout[:, SW:SW + 1])
                    vector.tensor_copy(out=vec[:, 5:6], in_=Bout[:, SW:SW + 1])
                    vector.drain()
                    vector.tensor_tensor(out=n1[:], in0=ekt[:], in1=vt[pb][:], op=ALU.mult)
                    vector.tensor_scalar(out=d1[:], in0=ekt[:], scalar1=1e-8, scalar2=None, op0=ALU.add)
                    vector.drain()
                    vector.tensor_tensor(out=n1[:], in0=Aout[:, 0:SW], in1=n1[:], op=ALU.add)
                    vector.tensor_tensor(out=d1[:], in0=Bout[:, 0:SW], in1=d1[:], op=ALU.add).then_inc(VS, 1)
                    c.inc("VS")
                    c.v[f"numden_{gl}"] = c.get("VS")

                @block.scalar
                def _(scalar, gl=gl):
                    scalar.wait_ge(VS, c.get(f"numden_{gl}"))
                    scalar.wait_ge(VS, NEG1)
                    scalar.drain()
                    scalar.activation(out=d2[:], in_=d1[:], func=AF.Ln)
                    scalar.drain()
                    scalar.activation(out=d2[:], in_=d2[:], func=AF.Exp, scale=cst3[:, 0:1]).then_inc(SC, 1)
                    c.inc("SC")
                    c.v[f"rden_{gl}"] = c.get("SC")

                @block.vector
                def _(vector, bc=bc, st=st, gl=gl):
                    vector.wait_ge(SC, c.get(f"rden_{gl}"))
                    ob2 = gl % 2
                    if c.get("ST"):
                        vector.wait_ge(ST, c.get("ST"))
                    vector.tensor_tensor(out=wout[ob2][:], in0=n1[:], in1=d2[:], op=ALU.mult).then_inc(VS, 1)
                    c.inc("VS")
                    c.v[f"wkvd_{bc}_{st}"] = c.get("VS")

                @block.gpsimd
                def _(gpsimd, bc=bc, st=st, gl=gl):
                    gpsimd.wait_ge(VS, c.get(f"wkvd_{bc}_{st}"))
                    ob2 = gl % 2
                    gpsimd.dma_start(out=o_wkv[bc, :, st * SW:(st + 1) * SW], in_=wout[ob2][:]).then_inc(ST, 16)
                    c.inc("ST", 16)

                @block.sync
                def _(sync, bc=bc, st=st, pb=pb, gl=gl):
                    if gl + 2 < NBLK * NSTR:
                        nbc, nst = divmod(gl + 2, NSTR)
                        sync.wait_ge(LDp[pb], 16 * c.get(f"kl_{pb}"))
                        sync.wait_ge(VS, c.get(f"numden_{gl}"))
                        sync.dma_start(out=kt[pb][:], in_=kT_i[nbc, :, nst * SW:(nst + 1) * SW]).then_inc(LDp[pb], 16)
                        sync.dma_start(out=vt[pb][:], in_=vT_i[nbc, :, nst * SW:(nst + 1) * SW]).then_inc(LDp[pb], 16)
                        c.inc(f"kl_{pb}", 2)

        @block.gpsimd
        def _(gpsimd):
            gpsimd.wait_ge(ST, c.get("ST"))

    return nc




def build_b4():
    nc = bass.Bass("TRN2", target_bir_lowering=False)
    xn1_i = nc.dram_tensor("xn1", [S, D], F32, kind="ExternalInput").ap()
    rT_i = nc.dram_tensor("rT", [D, S], F32, kind="ExternalInput").ap()
    wkvT_i = nc.dram_tensor("wkvT", [D, S], F32, kind="ExternalInput").ap()
    wout_i = nc.dram_tensor("wout", [D, D], F32, kind="ExternalInput").ap()
    ln2wb_i = nc.dram_tensor("ln2wb", [128, D], F32, kind="ExternalInput").ap()
    o_xm = nc.dram_tensor("o_xm", [S, D], F32, kind="ExternalOutput").ap()
    o_h2 = nc.dram_tensor("o_h2", [S, D], F32, kind="ExternalOutput").ap()
    xn1r = xn1_i.rearrange("(n p) d -> n p d", p=128)
    xmr = o_xm.rearrange("(n p) d -> n p d", p=128)
    h2r = o_h2.rearrange("(n p) d -> n p d", p=128)

    es = ExitStack()
    sb = lambda name, shape, dt: es.enter_context(nc.sbuf_tensor(name, shape, dt))
    with es:
        consts = sb("consts", [128, 4], F32)
        EPS6, NEGH, INVD = (consts[:, i:i + 1] for i in range(3))
        rc = [sb(f"rc{i}", [128, S], F32) for i in range(2)]
        wc = [sb(f"wc{i}", [128, S], F32) for i in range(2)]
        rw = sb("rw", [128, 8, S], BF16)
        wout_sb = sb("wout_sb", [128, 8, D], BF16)
        ln2wb = sb("ln2wb_sb", [128, D], F32)
        xnt = [sb(f"xnt{i}", [128, D], F32) for i in range(2)]
        xmt = [sb(f"xmt{i}", [128, D], F32) for i in range(2)]
        h2t = [sb(f"h2t{i}", [128, D], F32) for i in range(2)]
        stat = sb("stat", [128, 4], F32)
        ps = [es.enter_context(nc.psum_tensor(f"ps{i}", [128, 512], F32)) for i in range(4)]
        LD0 = es.enter_context(nc.semaphore("LD0"))
        LD1 = es.enter_context(nc.semaphore("LD1"))
        LDp = [LD0, LD1]
        GD = es.enter_context(nc.semaphore("GD"))
        TS_ = es.enter_context(nc.semaphore("TS"))
        VS = es.enter_context(nc.semaphore("VS"))
        SC = es.enter_context(nc.semaphore("SC"))
        ST = es.enter_context(nc.semaphore("ST"))
        block = es.enter_context(nc.Block())
        c = Ctr()

        @block.gpsimd
        def _(gpsimd):
            gpsimd.dma_start(out=ln2wb[:], in_=ln2wb_i).then_inc(GD, 16)
            for kc in range(8):
                gpsimd.dma_start(out=wout_sb[:, kc, :], in_=wout_i[kc * 128:(kc + 1) * 128, :]).then_inc(GD, 16)
            c.inc("GD", 16 * 9)
        GD0 = c.get("GD")

        @block.vector
        def _(vector):
            vector.memset(consts[:, 0:1], 1e-6)
            vector.memset(consts[:, 1:2], -0.5)
            vector.memset(consts[:, 2:3], 1.0 / D).then_inc(VS, 1)
            c.inc("VS")
        SETUP = c.get("VS")

        @block.sync
        def _(sync):
            for i in range(2):
                sync.dma_start(out=rc[i][:], in_=rT_i[i * 128:(i + 1) * 128, :]).then_inc(LDp[i], 16)
                sync.dma_start(out=wc[i][:], in_=wkvT_i[i * 128:(i + 1) * 128, :]).then_inc(LDp[i], 16)
                c.inc(f"rl_{i}", 2)

        for cc in range(8):
            pb = cc % 2

            @block.vector
            def _(vector, cc=cc, pb=pb):
                vector.wait_ge(LDp[pb], 16 * c.get(f"rl_{pb}"))
                vector.tensor_tensor(out=rw[:, cc, :], in0=rc[pb][:], in1=wc[pb][:], op=ALU.mult).then_inc(VS, 1)
                c.inc("VS")
                c.v[f"rw_{cc}"] = c.get("VS")

            @block.sync
            def _(sync, cc=cc, pb=pb):
                if cc + 2 < 8:
                    sync.wait_ge(LDp[pb], 16 * c.get(f"rl_{pb}"))
                    sync.wait_ge(VS, c.get(f"rw_{cc}"))
                    sync.dma_start(out=rc[pb][:], in_=rT_i[(cc + 2) * 128:(cc + 3) * 128, :]).then_inc(LDp[pb], 16)
                    sync.dma_start(out=wc[pb][:], in_=wkvT_i[(cc + 2) * 128:(cc + 3) * 128, :]).then_inc(LDp[pb], 16)
                    c.inc(f"rl_{pb}", 2)
        RW_ALL = c.get("VS")

        for tt in range(NT):
            pb = tt % 2

            @block.sync
            def _(sync, tt=tt, pb=pb):
                sync.wait_ge(LDp[pb], 16 * (c.get(f"rl_{pb}") + c.get(f"xl_{pb}")))
                prev = c.get(f"xm_{tt - 2}")
                if prev:
                    sync.wait_ge(VS, prev)
                sync.dma_start(out=xnt[pb][:], in_=xn1r[tt]).then_inc(LDp[pb], 16)
                c.inc(f"xl_{pb}")
                c.v[f"xnl_{tt}"] = 16 * (c.get(f"rl_{pb}") + c.get(f"xl_{pb}"))

            @block.tensor
            def _(tensor, tt=tt):
                if tt == 0:
                    tensor.wait_ge(VS, RW_ALL)
                    tensor.wait_ge(GD, GD0)
                prev = c.get(f"xm_{tt - 2}")
                if prev:
                    tensor.wait_ge(VS, prev)
                for nb in range(2):
                    bk = 2 * (tt % 2) + nb
                    for kc in range(8):
                        mm = tensor.matmul(out=ps[bk][:], lhsT=rw[:, kc, tt * 128:(tt + 1) * 128],
                                           rhs=wout_sb[:, kc, nb * 512:(nb + 1) * 512],
                                           start=(kc == 0), stop=(kc == 7))
                    mm.then_inc(TS_, 1)
                    c.inc("TS")
                c.v[f"o1_{tt}"] = c.get("TS")

            @block.vector
            def _(vector, tt=tt, pb=pb):
                vector.wait_ge(TS_, c.get(f"o1_{tt}"))
                vector.wait_ge(LDp[pb], c.get(f"xnl_{tt}"))
                if c.get("ST"):
                    vector.wait_ge(ST, c.get("ST"))
                for nb in range(2):
                    bk = 2 * (tt % 2) + nb
                    ins = vector.tensor_tensor(out=xmt[pb][:, nb * 512:(nb + 1) * 512], in0=ps[bk][:],
                                               in1=xnt[pb][:, nb * 512:(nb + 1) * 512], op=ALU.add)
                ins.then_inc(VS, 1)
                c.inc("VS")
                c.v[f"xm_{tt}"] = c.get("VS")

            @block.scalar
            def _(scalar, tt=tt, pb=pb):
                scalar.wait_ge(VS, c.get(f"xm_{tt}"))
                scalar.drain()
                scalar.activation(out=h2t[pb][:], in_=xmt[pb][:], func=AF.Square, accum_out=stat[:, 0:1])
                scalar.drain()
                scalar.activation(out=stat[:, 1:2], in_=stat[:, 0:1], func=AF.Ln, bias=EPS6, scale=INVD)
                scalar.drain()
                scalar.activation(out=stat[:, 1:2], in_=stat[:, 1:2], func=AF.Exp, scale=NEGH).then_inc(SC, 1)
                c.inc("SC")
                c.v[f"rms_{tt}"] = c.get("SC")

            @block.vector
            def _(vector, tt=tt, pb=pb):
                vector.wait_ge(SC, c.get(f"rms_{tt}"))
                if tt == 0:
                    vector.wait_ge(GD, GD0)
                vector.tensor_scalar(out=h2t[pb][:], in0=xmt[pb][:], scalar1=stat[:, 1:2], scalar2=None, op0=ALU.mult)
                vector.drain()
                vector.tensor_tensor(out=h2t[pb][:], in0=h2t[pb][:], in1=ln2wb[:], op=ALU.mult).then_inc(VS, 1)
                c.inc("VS")
                c.v[f"h2_{tt}"] = c.get("VS")

            @block.gpsimd
            def _(gpsimd, tt=tt, pb=pb):
                gpsimd.wait_ge(VS, c.get(f"h2_{tt}"))
                gpsimd.dma_start(out=xmr[tt], in_=xmt[pb][:]).then_inc(ST, 16)
                gpsimd.dma_start(out=h2r[tt], in_=h2t[pb][:]).then_inc(ST, 16)
                c.inc("ST", 32)

        @block.gpsimd
        def _(gpsimd):
            gpsimd.wait_ge(ST, c.get("ST"))

    return nc


def build_b5():
    nc = bass.Bass("TRN2", target_bir_lowering=False)
    h2T_i = nc.dram_tensor("h2T", [D, S + 1], F32, kind="ExternalInput").ap()
    tm2p_i = nc.dram_tensor("tm2p", [128, 16], F32, kind="ExternalInput").ap()
    wk2_i = nc.dram_tensor("wk2", [D, 4 * D], F32, kind="ExternalInput").ap()
    wr2_i = nc.dram_tensor("wr2", [D, D], F32, kind="ExternalInput").ap()
    wv2_i = nc.dram_tensor("wv2", [4 * D, D], F32, kind="ExternalInput").ap()
    xmT_i = nc.dram_tensor("xmT", [D, S], F32, kind="ExternalInput").ap()
    o_sT = nc.dram_tensor("o_sT", [D, S], F32, kind="ExternalOutput").ap()
    HW = S // 2

    es = ExitStack()
    sb = lambda name, shape, dt: es.enter_context(nc.sbuf_tensor(name, shape, dt))
    with es:
        h2c = [sb(f"h2c{i}", [128, HW + 1], F32) for i in range(2)]
        tm2 = sb("tm2", [128, 16], F32)
        dscr = sb("dscr", [128, HW], F32)
        xk2 = sb("xk2", [128, 8, HW], BF16)
        xr2 = sb("xr2", [128, 8, HW], BF16)
        k2 = sb("k2", [128, 32, HW], BF16)
        r2 = sb("r2", [128, 8, HW], F32)
        kslab = [sb(f"kslab{i}", [128, 8, 512], BF16) for i in range(2)]
        vslab = [sb(f"vslab{i}", [128, 16, 256], BF16) for i in range(2)]
        relu_b = [sb(f"relu_b{i}", [128, 512], BF16) for i in range(2)]
        xmc = [sb(f"xmc{i}", [128, HW], F32) for i in range(2)]
        s_b = [sb(f"s_b{i}", [128, HW], F32) for i in range(2)]
        ps = [es.enter_context(nc.psum_tensor(f"ps{i}", [128, 512], F32)) for i in range(8)]
        LD0 = es.enter_context(nc.semaphore("LD0"))
        LD1 = es.enter_context(nc.semaphore("LD1"))
        LDp = [LD0, LD1]
        GD = es.enter_context(nc.semaphore("GD"))
        GC = es.enter_context(nc.semaphore("GC"))
        GW0 = es.enter_context(nc.semaphore("GW0"))
        GW1 = es.enter_context(nc.semaphore("GW1"))
        GWp = [GW0, GW1]
        TS_ = es.enter_context(nc.semaphore("TS"))
        VS = es.enter_context(nc.semaphore("VS"))
        SC = es.enter_context(nc.semaphore("SC"))
        ST = es.enter_context(nc.semaphore("ST"))
        block = es.enter_context(nc.Block())
        c = Ctr()

        @block.gpsimd
        def _(gpsimd):
            gpsimd.dma_start(out=tm2[:], in_=tm2p_i).then_inc(GC, 16)
            c.inc("GC", 16)
        GD0 = c.get("GC")
        wk2_r = wk2_i.rearrange("(k p) o -> p k o", p=128)
        wr2_r = wr2_i.rearrange("(k p) o -> p k o", p=128)
        wv2_r = wv2_i.rearrange("(k p) o -> p k o", p=128)

        for hf in range(2):
            co = hf * HW

            @block.sync
            def _(sync, hf=hf, co=co):
                for i in range(2):
                    if c.get(f"ldc_{i}"):
                        sync.wait_ge(LDp[i], 16 * c.get(f"ldc_{i}"))
                    if hf == 1:
                        prev = c.get(f"mix_{hf - 1}_{6 + i}")
                        if prev:
                            sync.wait_ge(VS, prev)
                    sync.dma_start(out=h2c[i][:], in_=h2T_i[i * 128:(i + 1) * 128, co:co + HW + 1]).then_inc(LDp[i], 16)
                    c.inc(f"ldc_{i}")
                    c.v[f"h2ld_{hf}_{i}"] = 16 * c.get(f"ldc_{i}")

            for cc in range(8):
                pb = cc % 2

                @block.vector
                def _(vector, hf=hf, cc=cc, pb=pb, co=co):
                    vector.wait_ge(LDp[pb], c.get(f"h2ld_{hf}_{cc}") if c.get(f"h2ld_{hf}_{cc}") else c.get(f"h2ld_{hf}_{pb}"))
                    if hf == 0 and cc == 0:
                        vector.wait_ge(GC, GD0)
                    prevk = c.get(f"r2mm_all_{hf - 1}") if cc < 2 else 0
                    if prevk:
                        vector.wait_ge(TS_, prevk)
                    vector.drain()
                    vector.tensor_tensor(out=dscr[:], in0=h2c[pb][:, 1:HW + 1], in1=h2c[pb][:, 0:HW], op=ALU.subtract)
                    vector.drain()
                    vector.scalar_tensor_tensor(out=xk2[:, cc, :], in0=dscr[:], scalar=tm2[:, cc:cc + 1],
                                                in1=h2c[pb][:, 0:HW], op0=ALU.mult, op1=ALU.add)
                    vector.scalar_tensor_tensor(out=xr2[:, cc, :], in0=dscr[:], scalar=tm2[:, 8 + cc:9 + cc],
                                                in1=h2c[pb][:, 0:HW], op0=ALU.mult, op1=ALU.add).then_inc(VS, 1)
                    c.inc("VS")
                    c.v[f"mix_{hf}_{cc}"] = c.get("VS")

                @block.sync
                def _(sync, hf=hf, cc=cc, pb=pb, co=co):
                    if cc + 2 < 8:
                        sync.wait_ge(LDp[pb], 16 * c.get(f"ldc_{pb}"))
                        sync.wait_ge(VS, c.get(f"mix_{hf}_{cc}"))
                        sync.dma_start(out=h2c[pb][:], in_=h2T_i[(cc + 2) * 128:(cc + 3) * 128, co:co + HW + 1]).then_inc(LDp[pb], 16)
                        c.inc(f"ldc_{pb}")
                        c.v[f"h2ld_{hf}_{cc + 2}"] = 16 * c.get(f"ldc_{pb}")
            MIX = c.get("VS")

            for og in range(8):
                slq = c.inc("slq")
                slb = slq % 2

                def emit_kslab(q, og=og, slb=slb, slq=slq):
                    prev = c.get(f"kslabmm_{slq - 2}")
                    if prev:
                        q.wait_ge(TS_, prev)
                    q.dma_start(out=kslab[slb][:],
                                in_=wk2_r[:, :, og * 512:(og + 1) * 512]).then_inc(GD, 16)
                    c.inc("GD", 16)
                    c.v[f"kw_{slq}"] = c.get("GD")

                @block.gpsimd
                def _(gpsimd, f=emit_kslab):
                    f(gpsimd)

                for oc4 in range(4):
                    oc = og * 4 + oc4
                    for nt in range(2):
                        it = oc * 2 + nt
                        bk = it % 2

                        @block.tensor
                        def _(tensor, hf=hf, og=og, slb=slb, slq=slq, oc4=oc4, oc=oc, nt=nt, it=it, bk=bk):
                            if oc4 == 0 and nt == 0:
                                tensor.wait_ge(GD, c.get(f"kw_{slq}"))
                                if og == 0:
                                    tensor.wait_ge(VS, MIX)
                                    prev = c.get(f"sev_all_{hf - 1}")
                                    if prev:
                                        tensor.wait_ge(VS, prev)
                            prev = c.get(f"k2ev_{hf}_{it - 2}")
                            if prev:
                                tensor.wait_ge(SC, prev)
                            for kc in range(8):
                                mm = tensor.matmul(out=ps[bk][:],
                                                   lhsT=kslab[slb][:, kc, oc4 * 128:(oc4 + 1) * 128],
                                                   rhs=xk2[:, kc, nt * 512:(nt + 1) * 512],
                                                   start=(kc == 0), stop=(kc == 7))
                            mm.then_inc(TS_, 1)
                            c.inc("TS")
                            c.v[f"kslabmm_{slq}"] = c.get("TS")
                            c.v[f"k2mm_it_{hf}_{it}"] = c.get("TS")

                        @block.scalar
                        def _(scalar, hf=hf, oc=oc, nt=nt, it=it, bk=bk):
                            scalar.wait_ge(TS_, c.get(f"k2mm_it_{hf}_{it}"))
                            prev = c.get(f"k2sq_{hf}_{it - 2}")
                            if prev:
                                scalar.wait_ge(VS, prev)
                            scalar.activation(out=relu_b[bk][:], in_=ps[bk][:], func=AF.Relu).then_inc(SC, 1)
                            c.inc("SC")
                            c.v[f"k2ev_{hf}_{it}"] = c.get("SC")

                        @block.vector
                        def _(vector, hf=hf, oc=oc, nt=nt, it=it, bk=bk):
                            vector.wait_ge(SC, c.get(f"k2ev_{hf}_{it}"))
                            vector.tensor_tensor(out=k2[:, oc, nt * 512:(nt + 1) * 512], in0=relu_b[bk][:],
                                                 in1=relu_b[bk][:], op=ALU.mult).then_inc(VS, 1)
                            c.inc("VS")
                            c.v[f"k2sq_{hf}_{it}"] = c.get("VS")
                            c.v[f"k2sq_all_{hf}"] = c.get("VS")

            for og in range(2):
                slq = c.inc("slq")
                slb = slq % 2

                def emit_rslab(q, og=og, slb=slb, slq=slq):
                    prev = c.get(f"kslabmm_{slq - 2}")
                    if prev:
                        q.wait_ge(TS_, prev)
                    q.dma_start(out=kslab[slb][:],
                                in_=wr2_r[:, :, og * 512:(og + 1) * 512]).then_inc(GD, 16)
                    c.inc("GD", 16)
                    c.v[f"kw_{slq}"] = c.get("GD")

                @block.gpsimd
                def _(gpsimd, f=emit_rslab):
                    f(gpsimd)

                for oc4 in range(4):
                    oc = og * 4 + oc4
                    for nt in range(2):
                        it = oc * 2 + nt
                        bk = 2 + it % 2

                        @block.tensor
                        def _(tensor, hf=hf, og=og, slb=slb, slq=slq, oc4=oc4, oc=oc, nt=nt, it=it, bk=bk):
                            if oc4 == 0 and nt == 0:
                                tensor.wait_ge(GD, c.get(f"kw_{slq}"))
                            prev = c.get(f"r2ev_{hf}_{it - 2}")
                            if prev:
                                tensor.wait_ge(SC, prev)
                            for kc in range(8):
                                mm = tensor.matmul(out=ps[bk][:],
                                                   lhsT=kslab[slb][:, kc, oc4 * 128:(oc4 + 1) * 128],
                                                   rhs=xr2[:, kc, nt * 512:(nt + 1) * 512],
                                                   start=(kc == 0), stop=(kc == 7))
                            mm.then_inc(TS_, 1)
                            c.inc("TS")
                            c.v[f"kslabmm_{slq}"] = c.get("TS")
                            c.v[f"r2mm_it_{hf}_{it}"] = c.get("TS")

                        @block.scalar
                        def _(scalar, hf=hf, oc=oc, nt=nt, it=it, bk=bk):
                            scalar.wait_ge(TS_, c.get(f"r2mm_it_{hf}_{it}"))
                            scalar.drain()
                            scalar.activation(out=r2[:, oc, nt * 512:(nt + 1) * 512], in_=ps[bk][:],
                                              func=AF.Sigmoid).then_inc(SC, 1)
                            c.inc("SC")
                            c.v[f"r2ev_{hf}_{it}"] = c.get("SC")
            R2_ALL = c.get("SC")
            c.v[f"r2mm_all_{hf}"] = c.get("TS")

            for og in range(4):
                for kg in range(2):
                    vsl = c.inc("vsl")
                    vslb = vsl % 2

                    def emit_vslab(q, og=og, kg=kg, vsl=vsl, vslb=vslb):
                        prev = c.get(f"vslabmm_{vsl - 2}")
                        if prev:
                            q.wait_ge(TS_, prev)
                        q.dma_start(out=vslab[vslb][:],
                                    in_=wv2_r[:, kg * 16:(kg + 1) * 16, og * 256:(og + 1) * 256]).then_inc(GD, 16)
                        c.inc("GD", 16)
                        c.v[f"vw_{vsl}"] = c.get("GD")

                    @block.gpsimd
                    def _(gpsimd, f=emit_vslab):
                        f(gpsimd)

                    @block.tensor
                    def _(tensor, hf=hf, og=og, kg=kg, vsl=vsl, vslb=vslb):
                        tensor.wait_ge(GD, c.get(f"vw_{vsl}"))
                        if og == 0 and kg == 0:
                            tensor.wait_ge(SC, R2_ALL)
                            tensor.wait_ge(VS, c.get(f"k2sq_all_{hf}"))
                        if kg == 0:
                            prev = c.get(f"sev_og_{hf}_{og - 1}") if og else c.get(f"sev_og_{hf - 1}_3")
                            if prev:
                                tensor.wait_ge(VS, prev)
                        for oc2 in range(2):
                            for nt in range(2):
                                bk = 4 + oc2 * 2 + nt
                                for kc in range(16):
                                    mm = tensor.matmul(out=ps[bk][:],
                                                       lhsT=vslab[vslb][:, kc, oc2 * 128:(oc2 + 1) * 128],
                                                       rhs=k2[:, kg * 16 + kc, nt * 512:(nt + 1) * 512],
                                                       start=(kg == 0 and kc == 0), stop=(kg == 1 and kc == 15))
                        mm.then_inc(TS_, 1)
                        c.inc("TS")
                        c.v[f"vslabmm_{vsl}"] = c.get("TS")
                        c.v[f"o2mm_{hf}_{og}"] = c.get("TS")

                for oc2 in range(2):
                    oc = og * 2 + oc2
                    pb = oc % 2

                    @block.sync
                    def _(sync, hf=hf, oc=oc, pb=pb, co=co):
                        sync.wait_ge(LDp[pb], 16 * c.get(f"ldc_{pb}"))
                        prev = c.get(f"sdone_{hf}_{oc - 2}")
                        if prev:
                            sync.wait_ge(VS, prev)
                        sync.dma_start(out=xmc[pb][:], in_=xmT_i[oc * 128:(oc + 1) * 128, co:co + HW]).then_inc(LDp[pb], 16)
                        c.inc(f"ldc_{pb}")
                        c.v[f"xmld_{hf}_{oc}"] = 16 * c.get(f"ldc_{pb}")

                    for nt in range(2):
                        bk = 4 + oc2 * 2 + nt

                        @block.vector
                        def _(vector, hf=hf, og=og, oc=oc, oc2=oc2, nt=nt, bk=bk, pb=pb):
                            vector.wait_ge(TS_, c.get(f"o2mm_{hf}_{og}"))
                            vector.wait_ge(LDp[pb], c.get(f"xmld_{hf}_{oc}"))
                            if nt == 0:
                                prev = c.get(f"gsb_{oc % 2}")
                                if prev:
                                    vector.wait_ge(ST, prev)
                            vector.tensor_tensor(out=s_b[oc % 2][:, nt * 512:(nt + 1) * 512], in0=ps[bk][:],
                                                 in1=r2[:, oc, nt * 512:(nt + 1) * 512], op=ALU.mult)
                            vector.drain()
                            vector.tensor_tensor(out=s_b[oc % 2][:, nt * 512:(nt + 1) * 512],
                                                 in0=s_b[oc % 2][:, nt * 512:(nt + 1) * 512],
                                                 in1=xmc[pb][:, nt * 512:(nt + 1) * 512], op=ALU.add).then_inc(VS, 1)
                            c.inc("VS")
                            c.v[f"sev_{hf}_{oc}_{nt}"] = c.get("VS")
                            c.v[f"sdone_{hf}_{oc}"] = c.get("VS")
                            c.v[f"sev_og_{hf}_{og}"] = c.get("VS")
                            c.v[f"sev_all_{hf}"] = c.get("VS")

                    @block.gpsimd
                    def _(gpsimd, hf=hf, oc=oc, co=co):
                        gpsimd.wait_ge(VS, c.get(f"sdone_{hf}_{oc}"))
                        gpsimd.dma_start(out=o_sT[oc * 128:(oc + 1) * 128, co:co + HW],
                                         in_=s_b[oc % 2][:]).then_inc(ST, 16)
                        c.inc("ST", 16)
                        c.v[f"gsb_{oc % 2}"] = c.get("ST")

        @block.gpsimd
        def _(gpsimd):
            gpsimd.wait_ge(ST, c.get("ST"))

    return nc


def build_b6():
    nc = bass.Bass("TRN2", target_bir_lowering=False)
    xn1_i = nc.dram_tensor("xn1", [S, D], F32, kind="ExternalInput").ap()
    x1_i = nc.dram_tensor("x1", [S, D], F32, kind="ExternalInput").ap()
    s_i = nc.dram_tensor("s", [S, D], F32, kind="ExternalInput").ap()
    w_i = nc.dram_tensor("w", [D, D], F32, kind="ExternalInput").ap()
    b_i = nc.dram_tensor("b", [1, D], F32, kind="ExternalInput").ap()
    o_out = nc.dram_tensor("o_out", [S, D], F32, kind="ExternalOutput").ap()
    xn1r = xn1_i.rearrange("(n p) d -> n p d", p=128)
    x1r = x1_i.rearrange("(n p) d -> n p d", p=128)
    sr = s_i.rearrange("(n p) d -> n p d", p=128)
    outr = o_out.rearrange("(n p) d -> n p d", p=128)
    wr = w_i.rearrange("(n p) d -> n p d", p=128)
    MAGIC = float(3 * 2 ** 22)

    es = ExitStack()
    sb = lambda name, shape, dt: es.enter_context(nc.sbuf_tensor(name, shape, dt))
    with es:
        mgc = sb("mgc", [128, 2], F32)
        MG, MGN = mgc[:, 0:1], mgc[:, 1:2]
        qin = [sb(f"qin{i}", [128, D], F32) for i in range(2)]
        at = [sb(f"at{i}", [128, D], F32) for i in range(2)]
        bmx = [sb(f"bmx{i}", [128, 1024], F32) for i in range(2)]
        yt = sb("yt", [128, D], F32)
        m2t = sb("m2t", [128, D], F32)
        m4t = sb("m4t", [128, D], F32)
        rcst = sb("rcst", [128, D], F32)
        stt = sb("stt", [128, D], F32)
        zt = sb("zt", [128, D], F32)
        qt = sb("qt", [128, D], F32)
        xdb = [sb(f"xdb{i}", [128, D], BF16) for i in range(2)]
        xdT = sb("xdT", [128, 8, S], BF16)
        wdT = sb("wdT", [128, 8, D], BF16)
        brow = sb("brow", [1, D], BF16)
        onesrow = sb("onesrow", [1, 128], BF16)
        ident = sb("ident", [128, 128], BF16)
        idq = sb("idq", [128, 128], F32)
        gt = sb("gt", [128, D], F32)
        x1t = [sb(f"x1t{i}", [128, D], F32) for i in range(2)]
        st_ = [sb(f"st{i}", [128, D], F32) for i in range(2)]
        outb = [sb(f"outb{i}", [128, D], F32) for i in range(2)]
        ps_t = [es.enter_context(nc.psum_tensor(f"ps_t{i}", [128, 128], BF16)) for i in range(2)]
        ps = [es.enter_context(nc.psum_tensor(f"ps{i}", [128, 512], F32)) for i in range(4)]
        LD0 = es.enter_context(nc.semaphore("LD0"))
        LD1 = es.enter_context(nc.semaphore("LD1"))
        LDp = [LD0, LD1]
        GD = es.enter_context(nc.semaphore("GD"))
        TQ = es.enter_context(nc.semaphore("TQ"))
        IO = es.enter_context(nc.semaphore("IO"))
        PO = es.enter_context(nc.semaphore("PO"))
        TS_ = es.enter_context(nc.semaphore("TS"))
        VS = es.enter_context(nc.semaphore("VS"))
        SC = es.enter_context(nc.semaphore("SC"))
        ST = es.enter_context(nc.semaphore("ST"))
        block = es.enter_context(nc.Block())
        c = Ctr()

        @block.gpsimd
        def _(gpsimd):
            gpsimd.iota(out=idq[:], pattern=[[1, 128]], base=0, channel_multiplier=-1,
                        allow_small_or_imprecise_dtypes=True).then_inc(IO, 1)
            gpsimd.dma_start(out=brow[:], in_=b_i).then_inc(GD, 16)
            c.inc("GD", 16)
        GD0 = c.get("GD")

        @block.vector
        def _(vector):
            vector.memset(mgc[:, 0:1], MAGIC)
            vector.memset(mgc[:, 1:2], -MAGIC)
            vector.memset(onesrow[:], 1.0)
            vector.wait_ge(IO, 1)
            vector.tensor_scalar(out=ident[:], in0=idq[:], scalar1=0.0, scalar2=None,
                                 op0=ALU.is_equal).then_inc(VS, 1)
            c.inc("VS")
        SETUP = c.get("VS")

        for wt_ in range(24):
            pb = wt_ % 2
            srcr = wr[wt_] if wt_ < 8 else xn1r[wt_ - 8]
            dst, dcol = (wdT, wt_) if wt_ < 8 else (xdT, wt_ - 8)

            @block.sync
            def _(sync, wt_=wt_, pb=pb, srcr=srcr):
                sync.wait_ge(LDp[pb], 16 * c.get(f"ql_{pb}"))
                prev = c.get(f"qy_{wt_ - 2}")
                if prev:
                    sync.wait_ge(VS, prev)
                sync.dma_start(out=qin[pb][:], in_=srcr).then_inc(LDp[pb], 16)
                c.inc(f"ql_{pb}")
                c.v[f"qld_{wt_}"] = 16 * c.get(f"ql_{pb}")

            @block.scalar
            def _(scalar, wt_=wt_, pb=pb):
                scalar.wait_ge(LDp[pb], c.get(f"qld_{wt_}"))
                prev = c.get(f"qz_{wt_ - 2}")
                if prev:
                    scalar.wait_ge(VS, prev)
                scalar.activation(out=at[pb][:], in_=qin[pb][:], func=AF.Abs).then_inc(SC, 1)
                c.inc("SC")
                c.v[f"qabs_{wt_}"] = c.get("SC")

            @block.vector
            def _(vector, wt_=wt_, pb=pb):
                vector.wait_ge(SC, c.get(f"qabs_{wt_}"))
                prev = c.get(f"qd_{wt_ - 2}")
                if prev:
                    vector.wait_ge(VS, prev)
                b_ = bmx[pb]
                vector.tensor_tensor(out=b_[:, 0:512], in0=at[pb][:, 0:1024:2], in1=at[pb][:, 1:1024:2], op=ALU.max)
                vector.drain()
                vector.tensor_tensor(out=b_[:, 512:768], in0=b_[:, 0:512:2], in1=b_[:, 1:512:2], op=ALU.max)
                vector.drain()
                vector.tensor_tensor(out=b_[:, 768:896], in0=b_[:, 512:768:2], in1=b_[:, 513:768:2], op=ALU.max)
                vector.drain()
                vector.tensor_tensor(out=b_[:, 896:960], in0=b_[:, 768:896:2], in1=b_[:, 769:896:2], op=ALU.max)
                vector.drain()
                vector.tensor_tensor(out=b_[:, 960:992], in0=b_[:, 896:960:2], in1=b_[:, 897:960:2], op=ALU.max)
                vector.drain()
                vector.tensor_scalar(out=b_[:, 960:992], in0=b_[:, 960:992], scalar1=1.0 / 7.5, scalar2=1e-30,
                                     op0=ALU.mult, op1=ALU.max)
                vector.drain()
                vector.reciprocal(out=b_[:, 992:1024], in_=b_[:, 960:992])
                vector.drain()
                rsv = bass.AP(b_, 992, [[1024, 128], [1, 32], [0, 32]])
                vector.tensor_tensor(out=yt[:], in0=qin[pb][:], in1=rsv, op=ALU.mult)
                vector.tensor_tensor(out=at[pb][:], in0=at[pb][:], in1=rsv, op=ALU.mult).then_inc(VS, 1)
                c.inc("VS")
                c.v[f"qy_{wt_}"] = c.get("VS")

            @block.vector
            def _(vector, wt_=wt_, pb=pb):
                vector.wait_ge(VS, c.get(f"qy_{wt_}"))
                vector.drain()
                vector.tensor_scalar(out=m2t[:], in0=at[pb][:], scalar1=2.0, scalar2=None, op0=ALU.is_ge)
                vector.tensor_scalar(out=m4t[:], in0=at[pb][:], scalar1=4.0, scalar2=None, op0=ALU.is_ge)
                vector.drain()
                vector.tensor_scalar(out=rcst[:], in0=m2t[:], scalar1=-4.0, scalar2=8.0, op0=ALU.mult, op1=ALU.add)
                vector.tensor_scalar(out=stt[:], in0=m2t[:], scalar1=0.125, scalar2=0.125, op0=ALU.mult, op1=ALU.add)
                vector.drain()
                vector.scalar_tensor_tensor(out=rcst[:], in0=m4t[:], scalar=-2.0, in1=rcst[:],
                                            op0=ALU.mult, op1=ALU.add)
                vector.scalar_tensor_tensor(out=stt[:], in0=m4t[:], scalar=0.25, in1=stt[:],
                                            op0=ALU.mult, op1=ALU.add)
                vector.drain()
                vector.tensor_tensor(out=zt[:], in0=yt[:], in1=rcst[:], op=ALU.mult).then_inc(VS, 1)
                c.inc("VS")
                c.v[f"qz_{wt_}"] = c.get("VS")

            @block.scalar
            def _(scalar, wt_=wt_):
                scalar.wait_ge(VS, c.get(f"qz_{wt_}"))
                scalar.activation(out=zt[:], in_=zt[:], func=AF.Copy, bias=MAGIC)
                scalar.drain()
                scalar.activation(out=zt[:], in_=zt[:], func=AF.Copy, bias=-MAGIC).then_inc(SC, 1)
                c.inc("SC")
                c.v[f"qmg_{wt_}"] = c.get("SC")

            @block.vector
            def _(vector, wt_=wt_, pb=pb):
                vector.wait_ge(SC, c.get(f"qmg_{wt_}"))
                prev = c.get(f"qtp_{wt_ - 2}")
                if prev:
                    vector.wait_ge(TS_, prev)
                vector.tensor_tensor(out=qt[:], in0=zt[:], in1=stt[:], op=ALU.mult)
                vector.drain()
                sv = bass.AP(bmx[pb], 960, [[1024, 128], [1, 32], [0, 32]])
                vector.tensor_tensor(out=xdb[pb][:], in0=qt[:], in1=sv, op=ALU.mult).then_inc(VS, 1)
                c.inc("VS")
                c.v[f"qd_{wt_}"] = c.get("VS")

            for kc in range(8):
                @block.tensor
                def _(tensor, wt_=wt_, pb=pb, kc=kc, dst=dst, dcol=dcol):
                    if kc == 0:
                        tensor.wait_ge(VS, c.get(f"qd_{wt_}"))
                        if wt_ == 0:
                            tensor.wait_ge(VS, SETUP)
                    prev = c.get(f"qtpc_{wt_}_{kc - 2}") if kc >= 2 else c.get(f"qtpc_{wt_ - 1}_{kc + 6}")
                    if prev:
                        tensor.wait_ge(VS, prev)
                    tensor.transpose(out=ps_t[kc % 2][:], in_=xdb[pb][:, kc * 128:(kc + 1) * 128],
                                     identity=ident[:]).then_inc(TS_, 1)
                    c.inc("TS")
                    c.v[f"qtpm_{wt_}_{kc}"] = c.get("TS")
                    if kc == 7:
                        c.v[f"qtp_{wt_}"] = c.get("TS")

                @block.vector
                def _(vector, wt_=wt_, kc=kc, dst=dst, dcol=dcol):
                    vector.wait_ge(TS_, c.get(f"qtpm_{wt_}_{kc}"))
                    vector.tensor_copy(out=dst[:, kc, dcol * 128:(dcol + 1) * 128],
                                       in_=ps_t[kc % 2][:]).then_inc(VS, 1)
                    c.inc("VS")
                    c.v[f"qtpc_{wt_}_{kc}"] = c.get("VS")
                    if kc == 7:
                        c.v[f"qtpv_{wt_}"] = c.get("VS")
        QALL = c.get("VS")

        for tt in range(NT):
            pb = tt % 2

            @block.sync
            def _(sync, tt=tt, pb=pb):
                sync.wait_ge(LDp[pb], 16 * (c.get(f"ql_{pb}") + c.get(f"fl_{pb}")))
                prev = c.get(f"fin_{tt - 2}")
                if prev:
                    sync.wait_ge(VS, prev)
                sync.dma_start(out=x1t[pb][:], in_=x1r[tt]).then_inc(LDp[pb], 16)
                sync.dma_start(out=st_[pb][:], in_=sr[tt]).then_inc(LDp[pb], 16)
                c.inc(f"fl_{pb}", 2)
                c.v[f"fld_{tt}"] = 16 * (c.get(f"ql_{pb}") + c.get(f"fl_{pb}"))

            @block.tensor
            def _(tensor, tt=tt):
                if tt == 0:
                    tensor.wait_ge(VS, QALL)
                    tensor.wait_ge(GD, GD0)
                prev = c.get(f"gsil_{tt - 2}")
                if prev:
                    tensor.wait_ge(SC, prev)
                for nb in range(2):
                    bk = 2 * (tt % 2) + nb
                    for kc in range(8):
                        tensor.matmul(out=ps[bk][:], lhsT=xdT[:, kc, tt * 128:(tt + 1) * 128],
                                      rhs=wdT[:, kc, nb * 512:(nb + 1) * 512],
                                      start=(kc == 0), stop=False)
                    tensor.matmul(out=ps[bk][:], lhsT=onesrow[:], rhs=brow[:, nb * 512:(nb + 1) * 512],
                                  start=False, stop=True).then_inc(TS_, 1)
                    c.inc("TS")
                c.v[f"ffn_{tt}"] = c.get("TS")

            @block.scalar
            def _(scalar, tt=tt):
                scalar.wait_ge(TS_, c.get(f"ffn_{tt}"))
                prev = c.get(f"fin_{tt - 1}")
                if prev:
                    scalar.wait_ge(VS, prev)
                for nb in range(2):
                    bk = 2 * (tt % 2) + nb
                    ins = scalar.activation(out=gt[:, nb * 512:(nb + 1) * 512], in_=ps[bk][:], func=AF.Silu)
                ins.then_inc(SC, 1)
                c.inc("SC")
                c.v[f"gsil_{tt}"] = c.get("SC")

            @block.vector
            def _(vector, tt=tt, pb=pb):
                vector.wait_ge(SC, c.get(f"gsil_{tt}"))
                vector.wait_ge(LDp[pb], c.get(f"fld_{tt}"))
                prev = c.get(f"fst_{tt - 2}")
                if prev:
                    vector.wait_ge(ST, prev)
                vector.tensor_tensor(out=outb[pb][:], in0=st_[pb][:], in1=gt[:], op=ALU.mult)
                vector.drain()
                vector.tensor_tensor(out=outb[pb][:], in0=x1t[pb][:], in1=outb[pb][:], op=ALU.add).then_inc(VS, 1)
                c.inc("VS")
                c.v[f"fin_{tt}"] = c.get("VS")

            @block.gpsimd
            def _(gpsimd, tt=tt, pb=pb):
                gpsimd.wait_ge(VS, c.get(f"fin_{tt}"))
                gpsimd.dma_start(out=outr[tt], in_=outb[pb][:]).then_inc(ST, 16)
                c.inc("ST", 16)
                c.v[f"fst_{tt}"] = c.get("ST")

        @block.gpsimd
        def _(gpsimd):
            gpsimd.wait_ge(ST, c.get("ST"))

    return nc


def kernel(**inputs):
    asf = lambda a: np.ascontiguousarray(np.asarray(a, dtype=np.float32))
    bfv = lambda a: (np.asarray(a).view(BF) if np.asarray(a).dtype == np.uint16 else np.asarray(a))
    x = asf(inputs["L0_attn_x"])
    cores = list(range(NCORE))

    def shard(t):
        return [np.ascontiguousarray(t[i // 4, (i % 4) * S:(i % 4 + 1) * S]) for i in range(NCORE)]

    def gather(parts):
        g = np.zeros((B, T, D), dtype=parts[0].dtype)
        for i in range(NCORE):
            g[i // 4, (i % 4) * S:(i % 4 + 1) * S] = parts[i]
        return g

    nc = build_a2()
    qkvb_p = np.ascontiguousarray(asf(inputs["L0_attn_qkv_b"])[:2048].reshape(16, 128).T)
    xs = shard(x)
    in_maps = [{
        "x": xs[i],
        "qkv_w": asf(inputs["L0_attn_qkv_w"]),
        "qkvb_p": qkvb_p,
        "qkv_b": asf(inputs["L0_attn_qkv_b"]).reshape(1, 3 * D),
        "proj": asf(inputs["L0_attn_proj"]),
    } for i in range(NCORE)]
    r2_ = run_bass_kernel_spmd(nc, in_maps, core_ids=cores).results

    xn = gather([np.asarray(r["o_xn"]) for r in r2_])

    qp = np.zeros((B, H, M, T), dtype=BF)
    kp = np.zeros((B, H, M, T), dtype=BF)
    kptmP = np.zeros((B, H, 128, NCH, M), dtype=BF)
    vtmP = np.zeros((B, H, 128, NCH, M), dtype=BF)
    for i in range(NCORE):
        b, q = i // 4, i % 4
        sl = slice(q * S, (q + 1) * S)
        cs = slice(q * 16, (q + 1) * 16)
        qp[b, :, :, sl] = bfv(r2_[i]["o_qp"])
        kp[b, :, :, sl] = bfv(r2_[i]["o_kp"])
        kptmP[b, :, :, cs, :] = bfv(r2_[i]["o_kptm"])
        v = bfv(r2_[i]["o_vtm"]).reshape(16, 128, H, M)
        vtmP[b, :, :, cs, :] = v.transpose(2, 1, 0, 3)
    nc3 = build_a3()
    in_maps = []
    for i in range(NCORE):
        us = [divmod(i * NU + j, H) for j in range(NU)]
        in_maps.append({
            "i_qp": np.ascontiguousarray(np.stack([qp[b, h] for (b, h) in us])),
            "i_kp": np.ascontiguousarray(np.stack([kp[b, h] for (b, h) in us])),
            "i_kptm": np.ascontiguousarray(np.stack([kptmP[b, h] for (b, h) in us])),
            "i_vtm": np.ascontiguousarray(np.stack([vtmP[b, h] for (b, h) in us])),
        })
    r3_ = run_bass_kernel_spmd(nc3, in_maps, core_ids=cores).results
    att = np.zeros((B, T, H, M), dtype=BF)
    for i in range(NCORE):
        a = bfv(r3_[i]["o_att"])
        for j in range(NU):
            b, h = divmod(i * NU + j, H)
            att[b, :, h, :] = a[j].transpose(1, 0, 2).reshape(T, M)
    att = att.reshape(B, T, D)

    nc4 = build_a4()
    ln1wb = np.ascontiguousarray(np.broadcast_to(asf(inputs["L1_attn_ln1_w"]), (128, D)))
    in_maps = []
    for i in range(NCORE):
        b, q = i // 4, i % 4
        sl = slice(q * S, (q + 1) * S)
        in_maps.append({
            "xn": np.ascontiguousarray(xn[b, sl]),
            "x": xs[i],
            "attT": np.ascontiguousarray(att[b, sl].T.reshape(8, 128, S).transpose(1, 0, 2)),
            "out_w": asf(inputs["L0_attn_out_w"]),
            "out_b": asf(inputs["L0_attn_out_b"]).reshape(1, D),
            "gateT": np.ascontiguousarray(asf(inputs["L0_ffn_gate_w"]).T),
            "expw": np.ascontiguousarray(asf(inputs["L0_ffn_expert_w"]).transpose(0, 2, 1)),
            "expb": asf(inputs["L0_ffn_expert_b"]).reshape(1, 16 * D),
            "ln1wb": ln1wb,
        })
    r4_ = run_bass_kernel_spmd(nc4, in_maps, core_ids=cores).results
    x1 = gather([np.asarray(r["o_x1"]) for r in r4_])
    xn1 = gather([np.asarray(r["o_xn1"]) for r in r4_])
    h1 = gather([np.asarray(r["o_h1"]) for r in r4_])

    nc5 = build_b2()
    tmp = np.concatenate([asf(inputs[k]).reshape(8, 128).T for k in
                          ["L1_attn_time_mix_k", "L1_attn_time_mix_v", "L1_attn_time_mix_r"]], axis=1)
    in_maps = []
    for i in range(NCORE):
        b, q = i // 4, i % 4
        lo = q * S
        h1T = np.zeros((D, S + 1), np.float32)
        h1T[:, 1:] = h1[b, lo:lo + S].T
        if q > 0:
            h1T[:, 0] = h1[b, lo - 1]
        in_maps.append({
            "h1T": np.ascontiguousarray(h1T), "tmp": np.ascontiguousarray(tmp),
            "wk": asf(inputs["L1_attn_W_key"]),
            "wv": asf(inputs["L1_attn_W_value"]),
            "wr": asf(inputs["L1_attn_W_recept"]),
        })
    r5_ = run_bass_kernel_spmd(nc5, in_maps, core_ids=cores).results
    kT = np.zeros((B, D, T), np.float32)
    vT = np.zeros((B, D, T), np.float32)
    rT = np.zeros((B, D, T), np.float32)
    for i in range(NCORE):
        b, q = i // 4, i % 4
        sl = slice(q * S, (q + 1) * S)
        kT[b, :, sl] = np.asarray(r5_[i]["o_kT"])
        vT[b, :, sl] = np.asarray(r5_[i]["o_vT"])
        rT[b, :, sl] = np.asarray(r5_[i]["o_rT"])

    nc6 = build_b3()
    u_ = asf(inputs["L1_attn_time_first"])
    td_ = asf(inputs["L1_attn_time_decay"])
    in_maps = []
    for i in range(NCORE):
        blocks = [2 * i, 2 * i + 1]
        kk = np.stack([kT[bc // 8][(bc % 8) * 128:(bc % 8 + 1) * 128] for bc in blocks])
        vv = np.stack([vT[bc // 8][(bc % 8) * 128:(bc % 8 + 1) * 128] for bc in blocks])
        up = np.stack([u_[(bc % 8) * 128:(bc % 8 + 1) * 128] for bc in blocks], axis=1)
        tdp = np.stack([td_[(bc % 8) * 128:(bc % 8 + 1) * 128] for bc in blocks], axis=1)
        in_maps.append({"kT": np.ascontiguousarray(kk), "vT": np.ascontiguousarray(vv),
                        "up": np.ascontiguousarray(up), "tdp": np.ascontiguousarray(tdp)})
    r6_ = run_bass_kernel_spmd(nc6, in_maps, core_ids=cores).results
    wkvT = np.zeros((B, D, T), np.float32)
    for i in range(NCORE):
        w = np.asarray(r6_[i]["o_wkv"])
        for j, bc in enumerate([2 * i, 2 * i + 1]):
            wkvT[bc // 8, (bc % 8) * 128:(bc % 8 + 1) * 128] = w[j]

    nc7 = build_b4()
    ln2wb = np.ascontiguousarray(np.broadcast_to(asf(inputs["L1_attn_ln2_w"]), (128, D)))
    in_maps = []
    for i in range(NCORE):
        b, q = i // 4, i % 4
        sl = slice(q * S, (q + 1) * S)
        in_maps.append({
            "xn1": np.ascontiguousarray(xn1[b, sl]),
            "rT": np.ascontiguousarray(rT[b, :, sl]),
            "wkvT": np.ascontiguousarray(wkvT[b, :, sl]),
            "wout": asf(inputs["L1_attn_W_out"]),
            "ln2wb": ln2wb,
        })
    r7_ = run_bass_kernel_spmd(nc7, in_maps, core_ids=cores).results
    xm = gather([np.asarray(r["o_xm"]) for r in r7_])
    h2 = gather([np.asarray(r["o_h2"]) for r in r7_])

    nc8 = build_b5()
    tm2p = np.ascontiguousarray(np.concatenate(
        [asf(inputs[k]).reshape(8, 128).T for k in ["L1_attn_time_mix_k2", "L1_attn_time_mix_r2"]], axis=1))
    in_maps = []
    for i in range(NCORE):
        b, q = i // 4, i % 4
        lo = q * S
        h2T = np.zeros((D, S + 1), np.float32)
        h2T[:, 1:] = h2[b, lo:lo + S].T
        if q > 0:
            h2T[:, 0] = h2[b, lo - 1]
        in_maps.append({
            "h2T": np.ascontiguousarray(h2T), "tm2p": tm2p,
            "wk2": asf(inputs["L1_attn_W_key2"]),
            "wr2": asf(inputs["L1_attn_W_recept2"]),
            "wv2": asf(inputs["L1_attn_W_value2"]),
            "xmT": np.ascontiguousarray(xm[b, lo:lo + S].T),
        })
    r8_ = run_bass_kernel_spmd(nc8, in_maps, core_ids=cores).results
    s_parts = [np.ascontiguousarray(np.asarray(r["o_sT"]).T) for r in r8_]

    nc9 = build_b6()
    in_maps = []
    for i in range(NCORE):
        b, q = i // 4, i % 4
        sl = slice(q * S, (q + 1) * S)
        in_maps.append({
            "xn1": np.ascontiguousarray(xn1[b, sl]),
            "x1": np.ascontiguousarray(x1[b, sl]),
            "s": s_parts[i],
            "w": asf(inputs["L1_ffn_weight"]),
            "b": asf(inputs["L1_ffn_bias"]).reshape(1, D),
        })
    r9_ = run_bass_kernel_spmd(nc9, in_maps, core_ids=cores).results
    out = gather([np.asarray(r["o_out"]) for r in r9_])
    return out


# version 12
# speedup vs baseline: 1.2096x; 1.2096x over previous
import numpy as np
import ml_dtypes
from contextlib import ExitStack
import concourse.bass as bass
import concourse.mybir as mybir
from concourse.bass_utils import run_bass_kernel_spmd

F32 = mybir.dt.float32
BF16 = mybir.dt.bfloat16
AF = mybir.ActivationFunctionType
ALU = mybir.AluOpType

NCORE = 8
B, T, D = 2, 8192, 1024
S = 2048
NT = S // 128
H, M = 16, 64
NU = 4
NCH = T // 128
BF = ml_dtypes.bfloat16


class Ctr:
    def __init__(self):
        self.v = {}

    def inc(self, name, amt=1):
        self.v[name] = self.v.get(name, 0) + amt
        return self.v[name]

    def get(self, name):
        return self.v.get(name, 0)


def build_a2():
    nc = bass.Bass("TRN2", target_bir_lowering=False)
    x = nc.dram_tensor("x", [S, D], F32, kind="ExternalInput").ap()
    qkv_w = nc.dram_tensor("qkv_w", [D, 3 * D], F32, kind="ExternalInput").ap()
    qkvb_p = nc.dram_tensor("qkvb_p", [128, 16], F32, kind="ExternalInput").ap()
    qkv_b = nc.dram_tensor("qkv_b", [1, 3 * D], F32, kind="ExternalInput").ap()
    proj = nc.dram_tensor("proj", [H, M, M], F32, kind="ExternalInput").ap()
    o_xn = nc.dram_tensor("o_xn", [S, D], F32, kind="ExternalOutput").ap()
    o_qp = nc.dram_tensor("o_qp", [H, M, S], BF16, kind="ExternalOutput").ap()
    o_kp = nc.dram_tensor("o_kp", [H, M, S], BF16, kind="ExternalOutput").ap()
    o_kptm = nc.dram_tensor("o_kptm", [H, 128, 16, M], BF16, kind="ExternalOutput").ap()
    o_vtm = nc.dram_tensor("o_vtm", [S, D], BF16, kind="ExternalOutput").ap()

    xr = x.rearrange("(n p) d -> n p d", p=128)
    xnr = o_xn.rearrange("(n p) d -> n p d", p=128)
    vr = o_vtm.rearrange("(n p) d -> n p d", p=128)

    es = ExitStack()
    sb = lambda name, shape, dt: es.enter_context(nc.sbuf_tensor(name, shape, dt))
    with es:
        consts = sb("consts", [128, 2], F32)
        EPS, NEGH = consts[:, 0:1], consts[:, 1:2]
        xt = [sb(f"xt{i}", [128, D], F32) for i in range(2)]
        xnt = [sb(f"xnt{i}", [128, D], F32) for i in range(2)]
        xnb = [sb(f"xnb{i}", [128, D], BF16) for i in range(2)]
        stat = sb("stat", [128, 8], F32)
        xnT = sb("xnT", [128, 8, S], BF16)
        qkvw_sb = sb("qkvw_sb", [128, 8, 3 * D], BF16)
        qb_row = sb("qb_row", [1, 3 * D], BF16)
        qkvb_sb = sb("qkvb_sb", [128, 16], F32)
        proj_sb = sb("proj_sb", [128, H, M], BF16)
        ident = sb("ident", [128, 128], BF16)
        idq = sb("idq", [128, 128], F32)
        onesrow = sb("onesrow", [1, 128], BF16)
        qk_buf = [sb(f"qk_buf{i}", [128, 512], BF16) for i in range(2)]
        qkbig = [sb(f"qkbig{i}", [128, S], BF16) for i in range(2)]
        kptm_big = [sb(f"kptm_big{i}", [128, 2048], BF16) for i in range(2)]
        v_big = [sb(f"v_big{i}", [128, D], BF16) for i in range(2)]
        ps_t = [es.enter_context(nc.psum_tensor(f"ps_t{i}", [128, 128], BF16)) for i in range(2)]
        ps_qk = [es.enter_context(nc.psum_tensor(f"ps_qk{i}", [128, 512], F32)) for i in range(2)]
        ps_pf = [es.enter_context(nc.psum_tensor(f"ps_pf{i}", [128, 512], F32)) for i in range(2)]
        ps_ptm = [es.enter_context(nc.psum_tensor(f"ps_ptm{i}", [128, 512], F32)) for i in range(2)]
        LD0 = es.enter_context(nc.semaphore("LD0"))
        LD1 = es.enter_context(nc.semaphore("LD1"))
        LDp = [LD0, LD1]
        GD = es.enter_context(nc.semaphore("GD"))
        TS_ = es.enter_context(nc.semaphore("TS"))
        VS = es.enter_context(nc.semaphore("VS"))
        SC = es.enter_context(nc.semaphore("SC"))
        ST = es.enter_context(nc.semaphore("ST"))
        IO = es.enter_context(nc.semaphore("IO"))
        TQ = es.enter_context(nc.semaphore("TQ"))
        block = es.enter_context(nc.Block())
        c = Ctr()

        @block.sync
        def _(sync):
            for i in range(2):
                sync.dma_start(out=xt[i][:], in_=xr[i]).then_inc(LDp[i], 16)

        qkvw_r = qkv_w.rearrange("(k p) o -> p k o", p=128)
        proj_r = proj.rearrange("h d m -> d h m")

        @block.gpsimd
        def _(gpsimd):
            gpsimd.dma_start(out=qkvw_sb[:, 0:4, :], in_=qkvw_r[:, 0:4, :]).then_inc(GD, 16)
            gpsimd.dma_start(out=qkvw_sb[:, 4:8, :], in_=qkvw_r[:, 4:8, :]).then_inc(GD, 16)
            gpsimd.dma_start(out=qb_row[:], in_=qkv_b).then_inc(GD, 16)
            gpsimd.dma_start(out=qkvb_sb[:], in_=qkvb_p).then_inc(GD, 16)
            gpsimd.dma_start(out=proj_sb[0:64, :, :], in_=proj_r).then_inc(GD, 16)
            gpsimd.dma_start(out=proj_sb[64:128, :, :], in_=proj_r).then_inc(GD, 16)
            c.inc("GD", 16 * 6)
            gpsimd.iota(out=idq[:], pattern=[[1, 128]], base=0, channel_multiplier=-1,
                        allow_small_or_imprecise_dtypes=True).then_inc(IO, 1)
        GD_ALL = c.get("GD")

        @block.vector
        def _(vector):
            vector.memset(consts[:, 0:1], 1e-5)
            vector.memset(consts[:, 1:2], -0.5)
            vector.memset(onesrow[:], 1.0)
            vector.wait_ge(GD, GD_ALL)
            vector.wait_ge(IO, 1)
            vector.tensor_scalar(out=ident[:], in0=idq[:], scalar1=0.0, scalar2=None,
                                 op0=ALU.is_equal).then_inc(VS, 1)
            c.inc("VS")
        SETUP_READY = c.get("VS")

        for tt in range(NT):
            pb = tt % 2

            @block.scalar
            def _(scalar, tt=tt, pb=pb):
                scalar.wait_ge(LDp[pb], 16 * (tt // 2 + 1))
                scalar.wait_ge(VS, SETUP_READY)
                if c.get("ST"):
                    scalar.wait_ge(ST, c.get("ST"))
                scalar.activation(out=xnt[pb][:], in_=xt[pb][:], func=AF.Copy, accum_out=stat[:, 0:1])
                scalar.drain()
                scalar.activation(out=xnb[pb].bitcast(F32)[:, 0:512], in_=xt[pb][:, 0:512], func=AF.Square,
                                  accum_out=stat[:, 1:2])
                scalar.drain()
                scalar.activation(out=xnb[pb].bitcast(F32)[:, 0:512], in_=xt[pb][:, 512:1024], func=AF.Square,
                                  accum_out=stat[:, 7:8]).then_inc(SC, 1)
                c.inc("SC")
            SC_STAT = c.get("SC")

            @block.vector
            def _(vector, SC_STAT=SC_STAT):
                vector.wait_ge(SC, SC_STAT)
                vector.tensor_scalar(out=stat[:, 2:3], in0=stat[:, 0:1], scalar1=1.0 / D, scalar2=None, op0=ALU.mult)
                vector.tensor_tensor(out=stat[:, 3:4], in0=stat[:, 1:2], in1=stat[:, 7:8], op=ALU.add)
                vector.drain()
                vector.tensor_tensor(out=stat[:, 4:5], in0=stat[:, 2:3], in1=stat[:, 2:3], op=ALU.mult)
                vector.tensor_scalar(out=stat[:, 3:4], in0=stat[:, 3:4], scalar1=1.0 / D, scalar2=None, op0=ALU.mult)
                vector.drain()
                vector.tensor_tensor(out=stat[:, 5:6], in0=stat[:, 3:4], in1=stat[:, 4:5], op=ALU.subtract).then_inc(VS, 1)
                c.inc("VS")
            VS_VAR = c.get("VS")

            @block.scalar
            def _(scalar, VS_VAR=VS_VAR):
                scalar.wait_ge(VS, VS_VAR)
                scalar.activation(out=stat[:, 6:7], in_=stat[:, 5:6], func=AF.Ln, bias=EPS)
                scalar.drain()
                scalar.activation(out=stat[:, 6:7], in_=stat[:, 6:7], func=AF.Exp, scale=NEGH).then_inc(SC, 1)
                c.inc("SC")
            SC_RSTD = c.get("SC")

            @block.vector
            def _(vector, tt=tt, pb=pb, SC_RSTD=SC_RSTD):
                vector.wait_ge(SC, SC_RSTD)
                prev = c.get(f"tp_mm_{tt - 2}")
                if prev:
                    vector.wait_ge(TS_, prev)
                vector.tensor_scalar(out=xnt[pb][:], in0=xt[pb][:], scalar1=stat[:, 2:3], scalar2=stat[:, 6:7],
                                     op0=ALU.subtract, op1=ALU.mult)
                vector.drain()
                vector.tensor_copy(out=xnb[pb][:], in_=xnt[pb][:]).then_inc(VS, 1)
                c.inc("VS")
            VS_XN = c.get("VS")

            @block.gpsimd
            def _(gpsimd, tt=tt, pb=pb, VS_XN=VS_XN):
                gpsimd.wait_ge(VS, VS_XN)
                gpsimd.dma_start(out=xnr[tt], in_=xnt[pb][:]).then_inc(ST, 16)
                c.inc("ST", 16)
                c.v[f"st_xn_{tt}"] = c.get("ST")

            @block.sync
            def _(sync, tt=tt, pb=pb):
                if tt + 2 < NT:
                    sync.wait_ge(ST, c.get("ST"))
                    sync.dma_start(out=xt[pb][:], in_=xr[tt + 2]).then_inc(LDp[pb], 16)

            for kc in range(8):
                @block.tensor
                def _(tensor, tt=tt, pb=pb, VS_XN=VS_XN, kc=kc):
                    if kc == 0:
                        tensor.wait_ge(VS, VS_XN)
                    prev = c.get(f"tp_ev_{tt}_{kc - 2}") if kc >= 2 else c.get(f"tp_ev_{tt - 1}_{kc + 6}")
                    if prev:
                        tensor.wait_ge(VS, prev)
                    tensor.transpose(out=ps_t[kc % 2][:], in_=xnb[pb][:, kc * 128:(kc + 1) * 128],
                                     identity=ident[:]).then_inc(TS_, 1)
                    c.inc("TS")
                    c.v[f"tp_mm_{tt}_{kc}"] = c.get("TS")
                    c.v[f"tp_mm_{tt}"] = c.get("TS")

                @block.vector
                def _(vector, tt=tt, kc=kc):
                    vector.wait_ge(TS_, c.get(f"tp_mm_{tt}_{kc}"))
                    vector.tensor_copy(out=xnT[:, kc, tt * 128:(tt + 1) * 128], in_=ps_t[kc % 2][:]).then_inc(VS, 1)
                    c.inc("VS")
                    c.v[f"tp_ev_{tt}_{kc}"] = c.get("VS")
        XNT_READY = c.get("VS")

        for oc in range(16):
            is_k = oc >= 8
            for nt4 in range(4):
                it = oc * 4 + nt4
                bk = it % 2

                @block.tensor
                def _(tensor, oc=oc, nt4=nt4, it=it, bk=bk):
                    if it == 0:
                        tensor.wait_ge(VS, XNT_READY)
                    prev = c.get(f"qk_ev_{it - 2}")
                    if prev:
                        tensor.wait_ge(VS, prev)
                    for kc in range(8):
                        mm = tensor.matmul(out=ps_qk[bk][:], lhsT=qkvw_sb[:, kc, oc * 128:(oc + 1) * 128],
                                           rhs=xnT[:, kc, nt4 * 512:(nt4 + 1) * 512],
                                           start=(kc == 0), stop=(kc == 7))
                    mm.then_inc(TS_, 1)
                    c.inc("TS")
                    c.v[f"qk_mm_{it}"] = c.get("TS")

                @block.vector
                def _(vector, oc=oc, it=it, bk=bk):
                    vector.wait_ge(TS_, c.get(f"qk_mm_{it}"))
                    prev = c.get(f"pj_mm_{it - 2}")
                    if prev:
                        vector.wait_ge(TS_, prev)
                    vector.tensor_scalar(out=qk_buf[bk][:], in0=ps_qk[bk][:], scalar1=qkvb_sb[:, oc:oc + 1],
                                         scalar2=None, op0=ALU.add).then_inc(VS, 1)
                    c.inc("VS")
                    c.v[f"qk_ev_{it}"] = c.get("VS")

                @block.tensor
                def _(tensor, oc=oc, nt4=nt4, it=it, bk=bk, is_k=is_k):
                    tensor.wait_ge(VS, c.get(f"qk_ev_{it}"))
                    prev = c.get(f"pj_ev_{it - 2}")
                    if prev:
                        tensor.wait_ge(SC, prev)
                    for h2x in range(2):
                        h = 2 * (oc % 8) + h2x
                        pbase = h2x * 64
                        mm = tensor.matmul(out=ps_pf[bk][pbase:pbase + 64, :],
                                           lhsT=proj_sb[pbase:pbase + 64, h, :],
                                           rhs=qk_buf[bk][pbase:pbase + 64, :], start=True, stop=True,
                                           tile_position=(pbase, pbase))
                        if is_k:
                            for ts4 in range(4):
                                mm = tensor.matmul(out=ps_ptm[bk][:, h2x * 256 + ts4 * 64:h2x * 256 + (ts4 + 1) * 64],
                                                   lhsT=qk_buf[bk][pbase:pbase + 64, ts4 * 128:(ts4 + 1) * 128],
                                                   rhs=proj_sb[pbase:pbase + 64, h, :], start=True, stop=True,
                                                   tile_position=(pbase, 0))
                    mm.then_inc(TS_, 1)
                    c.inc("TS")
                    c.v[f"pj_mm_{it}"] = c.get("TS")

                @block.scalar
                def _(scalar, oc=oc, nt4=nt4, it=it, bk=bk, is_k=is_k):
                    scalar.wait_ge(TS_, c.get(f"pj_mm_{it}"))
                    ob = oc % 2
                    if nt4 == 0:
                        prev = c.get(f"qkst_{ob}")
                        if prev:
                            scalar.wait_ge(ST, prev)
                    ins = scalar.activation(out=qkbig[ob][:, nt4 * 512:(nt4 + 1) * 512],
                                            in_=ps_pf[bk][:], func=AF.Relu)
                    if is_k:
                        for h2x in range(2):
                            co_ = h2x * 1024 + nt4 * 256
                            ins = scalar.activation(out=kptm_big[ob][:, co_:co_ + 256],
                                                    in_=ps_ptm[bk][:, h2x * 256:(h2x + 1) * 256], func=AF.Relu)
                    ins.then_inc(SC, 1)
                    c.inc("SC")
                    c.v[f"pj_ev_{it}"] = c.get("SC")

            @block.gpsimd
            def _(gpsimd, oc=oc, is_k=is_k):
                ob = oc % 2
                hp = oc % 8
                gpsimd.wait_ge(SC, c.get(f"pj_ev_{oc * 4 + 3}"))
                dstq = (o_kp if is_k else o_qp)[2 * hp:2 * hp + 2].rearrange("h m t -> (h m) t")
                gpsimd.dma_start(out=dstq, in_=qkbig[ob][:]).then_inc(ST, 16)
                c.inc("ST", 16)
                if is_k:
                    for h2x in range(2):
                        gpsimd.dma_start(out=o_kptm[2 * hp + h2x].rearrange("p n m -> p (n m)"),
                                         in_=kptm_big[ob][:, h2x * 1024:(h2x + 1) * 1024]).then_inc(ST, 16)
                        c.inc("ST", 16)
                c.v[f"qkst_{ob}"] = c.get("ST")
        QK_LAST = [c.get("qk_ev_62"), c.get("qk_ev_63")]

        for tt in range(NT):
            for nb in range(2):
                itv = tt * 2 + nb
                pv = itv % 2

                @block.tensor
                def _(tensor, tt=tt, nb=nb, itv=itv, pv=pv):
                    if itv < 2:
                        tensor.wait_ge(VS, QK_LAST[pv])
                    prev = c.get(f"v_ev_{itv - 2}")
                    if prev:
                        tensor.wait_ge(SC, prev)
                    for kc in range(8):
                        tensor.matmul(out=ps_qk[pv][:], lhsT=xnT[:, kc, tt * 128:(tt + 1) * 128],
                                      rhs=qkvw_sb[:, kc, 2 * D + nb * 512:2 * D + (nb + 1) * 512],
                                      start=(kc == 0), stop=False)
                    tensor.matmul(out=ps_qk[pv][:], lhsT=onesrow[:],
                                  rhs=qb_row[:, 2 * D + nb * 512:2 * D + (nb + 1) * 512],
                                  start=False, stop=True).then_inc(TS_, 1)
                    c.inc("TS")
                    c.v[f"v_mm_{itv}"] = c.get("TS")

                @block.scalar
                def _(scalar, tt=tt, nb=nb, itv=itv, pv=pv):
                    scalar.wait_ge(TS_, c.get(f"v_mm_{itv}"))
                    tb = tt % 2
                    if nb == 0:
                        prev = c.get(f"vst_{tb}")
                        if prev:
                            scalar.wait_ge(ST, prev)
                    scalar.copy(out=v_big[tb][:, nb * 512:(nb + 1) * 512], in_=ps_qk[pv][:]).then_inc(SC, 1)
                    c.inc("SC")
                    c.v[f"v_ev_{itv}"] = c.get("SC")

            @block.gpsimd
            def _(gpsimd, tt=tt):
                gpsimd.wait_ge(SC, c.get(f"v_ev_{tt * 2 + 1}"))
                tb = tt % 2
                gpsimd.dma_start(out=vr[tt], in_=v_big[tb][:]).then_inc(ST, 16)
                c.inc("ST", 16)
                c.v[f"vst_{tb}"] = c.get("ST")

        @block.gpsimd
        def _(gpsimd):
            gpsimd.wait_ge(ST, c.get("ST"))

    return nc


def build_a3():
    nc = bass.Bass("TRN2", target_bir_lowering=False)
    i_qp = nc.dram_tensor("i_qp", [NU, M, T], BF16, kind="ExternalInput").ap()
    i_kp = nc.dram_tensor("i_kp", [NU, M, T], BF16, kind="ExternalInput").ap()
    i_kptm = nc.dram_tensor("i_kptm", [NU, 128, NCH, M], BF16, kind="ExternalInput").ap()
    i_vtm = nc.dram_tensor("i_vtm", [NU, 128, NCH, M], BF16, kind="ExternalInput").ap()
    o_att = nc.dram_tensor("o_att", [NU, 128, NCH, M], BF16, kind="ExternalOutput").ap()

    es = ExitStack()
    sb = lambda name, shape, dt: es.enter_context(nc.sbuf_tensor(name, shape, dt))
    with es:
        qp2 = [sb(f"qp2_{i}", [128, T], BF16) for i in range(2)]
        kp2 = [sb(f"kp2_{i}", [128, T], BF16) for i in range(2)]
        kptm_sb = [sb(f"kptm_sb{i}", [128, NCH, M], BF16) for i in range(NU)]
        vtm_sb = [sb(f"vtm_sb{i}", [128, NCH, M], BF16) for i in range(NU)]
        att_u = [sb(f"att_u{i}", [128, NCH, M], BF16) for i in range(NU)]
        nrm_u = [sb(f"nrm_u{i}", [128, NCH], F32) for i in range(NU)]
        recn = sb("recn", [128, NCH], F32)
        recb = sb("recb", [128, NCH], BF16)
        maskU = sb("maskU", [128, 128], BF16)
        idq = sb("idq", [128, 128], F32)
        onescol = sb("onescol", [128, 1], BF16)
        Sf2 = [sb(f"Sf2_{i}", [128, 72], F32) for i in range(2)]
        Sb2 = [sb(f"Sb2_{i}", [128, 72], BF16) for i in range(2)]
        a_t = [sb(f"a_t{i}", [128, 128], BF16) for i in range(4)]
        rec = sb("rec", [128, 4], F32)
        ps_s = [es.enter_context(nc.psum_tensor(f"ps_s{i}", [128, 128], F32)) for i in range(2)]
        ps_c = [es.enter_context(nc.psum_tensor(f"ps_c{i}", [128, 128], F32)) for i in range(2)]
        ps_u = [es.enter_context(nc.psum_tensor(f"ps_u{i}", [128, 128], F32)) for i in range(2)]
        LU = [es.enter_context(nc.semaphore(f"LU{i}")) for i in range(NU)]
        TS_ = es.enter_context(nc.semaphore("TS"))
        VS = es.enter_context(nc.semaphore("VS"))
        SC = es.enter_context(nc.semaphore("SC"))
        ST = es.enter_context(nc.semaphore("ST"))
        IO = es.enter_context(nc.semaphore("IO"))
        block = es.enter_context(nc.Block())
        c = Ctr()

        @block.gpsimd
        def _(gpsimd):
            gpsimd.iota(out=idq[:], pattern=[[1, 128]], base=0, channel_multiplier=-1,
                        allow_small_or_imprecise_dtypes=True).then_inc(IO, 1)

        @block.vector
        def _(vector):
            vector.memset(onescol[:], 1.0)
            for pr in range(2):
                vector.memset(Sf2[pr][:], 0.0)
            vector.wait_ge(IO, 1)
            vector.tensor_scalar(out=maskU[:], in0=idq[:], scalar1=0.0, scalar2=None,
                                 op0=ALU.is_ge)
            vector.drain()
            vector.tensor_copy(out=Sb2[0][:], in_=Sf2[0][:])
            vector.drain()
            vector.tensor_copy(out=Sb2[1][:], in_=Sf2[1][:]).then_inc(VS, 1)
            c.inc("VS")
        SETUP = c.get("VS")

        @block.sync
        def _(sync):
            for u in range(NU):
                pr, b0 = u // 2, (u % 2) * 64
                sync.dma_start(out=qp2[pr][b0:b0 + 64, :], in_=i_qp[u]).then_inc(LU[u], 16)
                sync.dma_start(out=kp2[pr][b0:b0 + 64, :], in_=i_kp[u]).then_inc(LU[u], 16)
                sync.dma_start(out=kptm_sb[u][:], in_=i_kptm[u]).then_inc(LU[u], 16)
                sync.dma_start(out=vtm_sb[u][:], in_=i_vtm[u]).then_inc(LU[u], 16)

        for n in range(NCH):
            for u in range(NU):
                g = n * NU + u
                pb = g % 2
                ab = g % 4

                pr, b0 = u // 2, (u % 2) * 64

                @block.tensor
                def _(tensor, n=n, u=u, g=g, pb=pb, pr=pr, b0=b0):
                    if n == 0:
                        if u == 0:
                            tensor.wait_ge(VS, SETUP)
                        tensor.wait_ge(LU[u], 64)
                    prev = c.get(f"at_ev_{g - 2}")
                    if prev:
                        tensor.wait_ge(VS, prev)
                    tensor.matmul(out=ps_s[pb][:], lhsT=kp2[pr][b0:b0 + 64, n * 128:(n + 1) * 128],
                                  rhs=qp2[pr][b0:b0 + 64, n * 128:(n + 1) * 128], start=True, stop=True,
                                  tile_position=(b0, 0)).then_inc(TS_, 1)
                    c.inc("TS")
                    c.v[f"s_mm_{g}"] = c.get("TS")

                @block.vector
                def _(vector, g=g, pb=pb, ab=ab):
                    vector.wait_ge(TS_, c.get(f"s_mm_{g}"))
                    prev = c.get(f"c_mm_{g - 4}")
                    if prev:
                        vector.wait_ge(TS_, prev)
                    vector.tensor_tensor(out=a_t[ab][:], in0=ps_s[pb][:], in1=maskU[:], op=ALU.mult).then_inc(VS, 1)
                    c.inc("VS")
                    c.v[f"at_ev_{g}"] = c.get("VS")

                @block.tensor
                def _(tensor, n=n, u=u, g=g, pb=pb, ab=ab, pr=pr, b0=b0):
                    tensor.wait_ge(VS, c.get(f"at_ev_{g}"))
                    prev = c.get(f"sfu_{g - 2}")
                    if prev:
                        tensor.wait_ge(VS, prev)
                    prev = c.get(f"up_ev_{g - 4}")
                    if prev:
                        tensor.wait_ge(SC, prev)
                    tensor.matmul(out=ps_c[pb][:, 0:64], lhsT=a_t[ab][:], rhs=vtm_sb[u][:, n, :],
                                  start=True, stop=False)
                    tensor.matmul(out=ps_c[pb][:, 64:65], lhsT=a_t[ab][:], rhs=onescol[:],
                                  start=False, stop=False)
                    tensor.matmul(out=ps_c[pb][:, 0:64], lhsT=qp2[pr][b0:b0 + 64, n * 128:(n + 1) * 128],
                                  rhs=Sb2[pr][b0:b0 + 64, 0:64], start=False, stop=False,
                                  tile_position=(b0, 0))
                    tensor.matmul(out=ps_c[pb][:, 64:65], lhsT=qp2[pr][b0:b0 + 64, n * 128:(n + 1) * 128],
                                  rhs=Sb2[pr][b0:b0 + 64, 64:65], start=False, stop=True,
                                  tile_position=(b0, 0))
                    tensor.matmul(out=ps_u[pb][b0:b0 + 64, 0:64], lhsT=kptm_sb[u][:, n, :],
                                  rhs=vtm_sb[u][:, n, :], start=True, stop=False,
                                  tile_position=(0, b0))
                    tensor.matmul(out=ps_u[pb][b0:b0 + 64, 64:65], lhsT=kptm_sb[u][:, n, :],
                                  rhs=onescol[:], start=False, stop=True,
                                  tile_position=(0, b0)).then_inc(TS_, 1)
                    c.inc("TS")
                    c.v[f"c_mm_{g}"] = c.get("TS")

                @block.vector
                def _(vector, n=n, u=u, g=g, pb=pb, pr=pr, b0=b0):
                    vector.wait_ge(TS_, c.get(f"c_mm_{g}"))
                    vector.tensor_copy(out=att_u[u][:, n, :], in_=ps_c[pb][:, 0:64])
                    vector.tensor_copy(out=nrm_u[u][:, n:n + 1], in_=ps_c[pb][:, 64:65])
                    vector.tensor_tensor(out=Sf2[pr][b0:b0 + 64, 0:65], in0=Sf2[pr][b0:b0 + 64, 0:65],
                                         in1=ps_u[pb][b0:b0 + 64, 0:65], op=ALU.add).then_inc(VS, 1)
                    c.inc("VS")
                    c.v[f"sfu_{g}"] = c.get("VS")

                @block.scalar
                def _(scalar, g=g, pr=pr, b0=b0):
                    scalar.wait_ge(VS, c.get(f"sfu_{g}"))
                    scalar.copy(out=Sb2[pr][b0:b0 + 64, 0:65],
                                in_=Sf2[pr][b0:b0 + 64, 0:65]).then_inc(SC, 1)
                    c.inc("SC")
                    c.v[f"up_ev_{g}"] = c.get("SC")

        for u in range(NU):
            @block.vector
            def _(vector, u=u):
                vector.wait_ge(VS, c.get(f"sfu_{(NCH - 1) * NU + u}"))
                vector.tensor_scalar(out=recn[:], in0=nrm_u[u][:], scalar1=1e-6, scalar2=None, op0=ALU.add)
                vector.drain()
                vector.reciprocal(out=recn[:], in_=recn[:])
                vector.drain()
                vector.tensor_copy(out=recb[:], in_=recn[:])
                vector.drain()
                rb = bass.AP(recb, 0, [[NCH, 128], [1, NCH], [0, M]])
                vector.tensor_tensor(out=att_u[u][:], in0=att_u[u][:], in1=rb, op=ALU.mult).then_inc(VS, 1)
                c.inc("VS")
                c.v[f"attn_{u}"] = c.get("VS")

            @block.gpsimd
            def _(gpsimd, u=u):
                gpsimd.wait_ge(VS, c.get(f"attn_{u}"))
                gpsimd.dma_start(out=o_att[u], in_=att_u[u][:]).then_inc(ST, 16)
                c.inc("ST", 16)

        @block.gpsimd
        def _(gpsimd):
            gpsimd.wait_ge(ST, c.get("ST"))

    return nc




def build_a4():
    nc = bass.Bass("TRN2", target_bir_lowering=False)
    xn_i = nc.dram_tensor("xn", [S, D], F32, kind="ExternalInput").ap()
    x_i = nc.dram_tensor("x", [S, D], F32, kind="ExternalInput").ap()
    attT_i = nc.dram_tensor("attT", [128, 8, S], BF16, kind="ExternalInput").ap()
    outw_i = nc.dram_tensor("out_w", [D, D], F32, kind="ExternalInput").ap()
    outb_i = nc.dram_tensor("out_b", [1, D], F32, kind="ExternalInput").ap()
    gateT_i = nc.dram_tensor("gateT", [D, 16], F32, kind="ExternalInput").ap()
    expw_i = nc.dram_tensor("expw", [16, D, D], F32, kind="ExternalInput").ap()
    expb_i = nc.dram_tensor("expb", [1, 16 * D], F32, kind="ExternalInput").ap()
    ln1wb_i = nc.dram_tensor("ln1wb", [128, D], F32, kind="ExternalInput").ap()
    o_x1 = nc.dram_tensor("o_x1", [S, D], F32, kind="ExternalOutput").ap()
    o_xn1 = nc.dram_tensor("o_xn1", [S, D], F32, kind="ExternalOutput").ap()
    o_h1 = nc.dram_tensor("o_h1", [S, D], F32, kind="ExternalOutput").ap()

    xnr = xn_i.rearrange("(n p) d -> n p d", p=128)
    xr = x_i.rearrange("(n p) d -> n p d", p=128)
    x1r = o_x1.rearrange("(n p) d -> n p d", p=128)
    xn1r = o_xn1.rearrange("(n p) d -> n p d", p=128)
    h1r = o_h1.rearrange("(n p) d -> n p d", p=128)

    es = ExitStack()
    es_g = ExitStack()
    es_x = ExitStack()
    es_m = ExitStack()
    es_s = ExitStack()
    sb = lambda st, name, shape, dt: st.enter_context(nc.sbuf_tensor(name, shape, dt))
    with es:
        consts = sb(es, "consts", [128, 4], F32)
        EPS5, NEGH, EPS6, INVD = (consts[:, i:i + 1] for i in range(4))
        onesrow = sb(es, "onesrow", [1, 128], BF16)
        gateT_sb = sb(es, "gateT_sb", [128, 8, 16], F32)
        coef = sb(es, "coef", [128, 16 * 16], F32)
        stat = sb(es, "stat", [128, 16], F32)
        acc = sb(es, "acc", [128, NT, D], F32)
        ln1wb = sb(es, "ln1wb_sb", [128, D], F32)
        outb_sb = sb(es, "outb_sb", [1, D], BF16)
        xnT = sb(es_x, "xnT", [128, 8, S], BF16)
        identf = sb(es_g, "identf", [128, 128], F32)
        idq = sb(es_g, "idq", [128, 128], F32)
        xnt = [sb(es_g, f"xnt{i}", [128, D], F32) for i in range(2)]
        xTf = sb(es_g, "xTf", [128, 8, 128], F32)
        glog = sb(es_g, "glog", [128, 256], F32)
        gt1 = sb(es_g, "gt1", [128, 240], F32)
        gst = sb(es_g, "gst", [128, 160], F32)
        ps_t = [es_g.enter_context(nc.psum_tensor(f"ps_t{i}", [128, 128], F32)) for i in range(2)]
        ps_g = es_g.enter_context(nc.psum_tensor("ps_g", [128, 256], F32))
        LD0 = es.enter_context(nc.semaphore("LD0"))
        LD1 = es.enter_context(nc.semaphore("LD1"))
        LDp = [LD0, LD1]
        GC = es.enter_context(nc.semaphore("GC"))
        GW0 = es.enter_context(nc.semaphore("GW0"))
        GW1 = es.enter_context(nc.semaphore("GW1"))
        GWp = [GW0, GW1]
        GA = es.enter_context(nc.semaphore("GA"))
        TS_ = es.enter_context(nc.semaphore("TS"))
        VS = es.enter_context(nc.semaphore("VS"))
        SC = es.enter_context(nc.semaphore("SC"))
        ST = es.enter_context(nc.semaphore("ST"))
        IO = es.enter_context(nc.semaphore("IO"))
        PO = es.enter_context(nc.semaphore("PO"))
        GX = es.enter_context(nc.semaphore("GX"))
        block = es.enter_context(nc.Block())
        c = Ctr()

        @block.gpsimd
        def _(gpsimd):
            gpsimd.iota(out=idq[:], pattern=[[1, 128]], base=0, channel_multiplier=-1,
                        allow_small_or_imprecise_dtypes=True).then_inc(IO, 1)
            gpsimd.dma_start(out=gateT_sb[:], in_=gateT_i.rearrange("(k p) e -> p k e", p=128)).then_inc(GC, 16)
            gpsimd.dma_start(out=outb_sb[:], in_=outb_i).then_inc(GC, 16)
            gpsimd.dma_start(out=ln1wb[:], in_=ln1wb_i).then_inc(GC, 16)
            c.inc("GC", 16 * 3)
        GD_ALL = c.get("GC")

        @block.vector
        def _(vector):
            vector.memset(consts[:, 0:1], 1e-5)
            vector.memset(consts[:, 1:2], -0.5)
            vector.memset(consts[:, 2:3], 1e-6)
            vector.memset(consts[:, 3:4], 1.0 / D)
            vector.memset(onesrow[:], 1.0)
            vector.wait_ge(IO, 1)
            vector.tensor_scalar(out=identf[:], in0=idq[:], scalar1=0.0, scalar2=None,
                                 op0=ALU.is_equal).then_inc(VS, 1)
            c.inc("VS")
        SETUP = c.get("VS")

        @block.sync
        def _(sync):
            for i in range(2):
                sync.dma_start(out=xnt[i][:], in_=xnr[i]).then_inc(LDp[i], 16)
            c.v["xn_loads_0"] = 1
            c.v["xn_loads_1"] = 1

        GSTW = 160
        for tt in range(NT):
            pb = tt % 2
            for kc in range(8):
                @block.tensor
                def _(tensor, tt=tt, pb=pb, kc=kc):
                    if tt == 0 and kc == 0:
                        tensor.wait_ge(VS, SETUP)
                    if kc == 0:
                        tensor.wait_ge(LDp[pb], 16 * c.get(f"xn_loads_{pb}"))
                    prev = c.get(f"gtp_ev_{tt}_{kc - 2}") if kc >= 2 else c.get(f"gtp_ev_{tt - 1}_{kc + 6}")
                    if prev:
                        tensor.wait_ge(VS, prev)
                    tensor.transpose(out=ps_t[kc % 2][:], in_=xnt[pb][:, kc * 128:(kc + 1) * 128],
                                     identity=identf[:]).then_inc(TS_, 1)
                    c.inc("TS")
                    c.v[f"gtp_mm_{tt}_{kc}"] = c.get("TS")

                @block.vector
                def _(vector, tt=tt, kc=kc):
                    vector.wait_ge(TS_, c.get(f"gtp_mm_{tt}_{kc}"))
                    if kc == 0:
                        prev = c.get(f"glog_mm_{tt - 1}")
                        if prev:
                            vector.wait_ge(TS_, prev)
                        prev = c.get(f"xnTc_{tt - 1}")
                        if prev:
                            vector.wait_ge(GX, prev)
                    vector.tensor_copy(out=xTf[:, kc, :], in_=ps_t[kc % 2][:]).then_inc(VS, 1)
                    c.inc("VS")
                    c.v[f"gtp_ev_{tt}_{kc}"] = c.get("VS")

            @block.gpsimd
            def _(gpsimd, tt=tt):
                gpsimd.wait_ge(VS, c.get(f"gtp_ev_{tt}_7"))
                gpsimd.dma_start(out=xnT[:, :, tt * 128:(tt + 1) * 128], in_=xTf[:]).then_inc(GX, 16)
                c.inc("GX", 16)
                c.v[f"xnTc_{tt}"] = c.get("GX")

            @block.tensor
            def _(tensor, tt=tt):
                tensor.wait_ge(VS, c.get(f"gtp_ev_{tt}_7"))
                if tt == 0:
                    tensor.wait_ge(GC, GD_ALL)
                for kc in range(8):
                    mm = tensor.matmul(out=ps_g[:, tt * 16:(tt + 1) * 16], lhsT=xTf[:, kc, :],
                                       rhs=gateT_sb[:, kc, :], start=(kc == 0), stop=(kc == 7))
                mm.then_inc(TS_, 1)
                c.inc("TS")
                c.v[f"glog_mm_{tt}"] = c.get("TS")

            @block.sync
            def _(sync, tt=tt, pb=pb):
                if tt + 2 < NT:
                    sync.wait_ge(TS_, c.get(f"gtp_mm_{tt}_7"))
                    sync.dma_start(out=xnt[pb][:], in_=xnr[tt + 2]).then_inc(LDp[pb], 16)
                    c.inc(f"xn_loads_{pb}")

        def BC(off):
            return bass.AP(gst, off, [[GSTW, 128], [1, 16], [0, 16]])

        def segtree(vector, src, op):
            vector.tensor_tensor(out=gt1[:, 0:128], in0=src[:, 0:256:2], in1=src[:, 1:256:2], op=op)
            vector.drain()
            vector.tensor_tensor(out=gt1[:, 128:192], in0=gt1[:, 0:128:2], in1=gt1[:, 1:128:2], op=op)
            vector.drain()
            vector.tensor_tensor(out=gt1[:, 192:224], in0=gt1[:, 128:192:2], in1=gt1[:, 129:192:2], op=op)
            vector.drain()
            vector.tensor_tensor(out=gt1[:, 224:240], in0=gt1[:, 192:224:2], in1=gt1[:, 193:224:2], op=op)
            vector.drain()

        @block.vector
        def _(vector):
            vector.wait_ge(TS_, c.get(f"glog_mm_{NT - 1}"))
            vector.tensor_copy(out=glog[:], in_=ps_g[:])
            vector.drain()
            segtree(vector, glog, ALU.max)
            vector.tensor_copy(out=gst[:, 0:16], in_=gt1[:, 224:240])
            vector.drain()
            vector.tensor_tensor(out=glog[:], in0=glog[:], in1=BC(0), op=ALU.subtract).then_inc(VS, 1)
            c.inc("VS")
        GSUB = c.get("VS")

        @block.scalar
        def _(scalar):
            scalar.wait_ge(VS, GSUB)
            scalar.activation(out=glog[:], in_=glog[:], func=AF.Exp).then_inc(SC, 1)
            c.inc("SC")
        GEXP = c.get("SC")

        @block.vector
        def _(vector):
            vector.wait_ge(SC, GEXP)
            segtree(vector, glog, ALU.add)
            vector.reciprocal(out=gst[:, 16:32], in_=gt1[:, 224:240])
            vector.drain()
            vector.tensor_tensor(out=glog[:], in0=glog[:], in1=BC(16), op=ALU.mult)
            vector.drain()
            segtree(vector, glog, ALU.max)
            vector.tensor_copy(out=gst[:, 32:48], in_=gt1[:, 224:240])
            vector.drain()
            vector.tensor_tensor(out=coef[:], in0=glog[:], in1=BC(32), op=ALU.is_ge)
            vector.drain()
            segtree(vector, coef, ALU.add)
            vector.tensor_copy(out=gst[:, 48:64], in_=gt1[:, 224:240])
            vector.drain()
            vector.scalar_tensor_tensor(out=coef[:], in0=coef[:], scalar=-1.0, in1=glog[:],
                                        op0=ALU.mult, op1=ALU.mult)
            vector.drain()
            vector.tensor_tensor(out=coef[:], in0=glog[:], in1=coef[:], op=ALU.add)
            vector.drain()
            segtree(vector, coef, ALU.max)
            vector.tensor_scalar(out=gst[:, 64:80], in0=gst[:, 48:64], scalar1=2.0, scalar2=None,
                                 op0=ALU.is_ge)
            vector.tensor_tensor(out=gst[:, 80:96], in0=gst[:, 32:48], in1=gt1[:, 224:240],
                                 op=ALU.subtract)
            vector.drain()
            vector.tensor_tensor(out=gst[:, 96:112], in0=gst[:, 64:80], in1=gst[:, 80:96], op=ALU.mult)
            vector.drain()
            vector.tensor_tensor(out=gst[:, 96:112], in0=gt1[:, 224:240], in1=gst[:, 96:112],
                                 op=ALU.add)
            vector.drain()
            vector.tensor_tensor(out=coef[:], in0=glog[:], in1=BC(96), op=ALU.is_ge)
            vector.drain()
            vector.tensor_tensor(out=coef[:], in0=glog[:], in1=coef[:], op=ALU.mult).then_inc(VS, 1)
            c.inc("VS")
        COEF_ALL = c.get("VS")

        wbuf = [sb(es_m, f"wbuf{i}", [128, 8, D], BF16) for i in range(2)]
        expb_row = [sb(es_m, f"expb_row{i}", [1, D], BF16) for i in range(2)]
        ge_buf = [sb(es_m, f"ge_buf{i}", [128, 512], F32) for i in range(4)]
        ps_e = [es_m.enter_context(nc.psum_tensor(f"ps_e{i}", [128, 512], F32)) for i in range(4)]

        expw_r = expw_i.rearrange("e (k p) d -> e p k d", p=128)

        def emit_wload_g(gpsimd, e):
            p = e % 2
            gpsimd.dma_start(out=wbuf[p][:], in_=expw_r[e]).then_inc(GWp[p], 16)
            gpsimd.dma_start(out=expb_row[p][:], in_=expb_i[0:1, e * D:(e + 1) * D]).then_inc(GWp[p], 16)
            c.inc(f"gw_{p}", 2)
            c.v[f"gwv_{e}"] = 16 * c.get(f"gw_{p}")

        @block.gpsimd
        def _(gpsimd):
            emit_wload_g(gpsimd, 0)
            emit_wload_g(gpsimd, 1)

        for e in range(16):
            eb = e % 2
            for tt in range(NT):
                it = e * NT + tt

                @block.tensor
                def _(tensor, e=e, eb=eb, tt=tt, it=it):
                    if e == 0 and tt == 0:
                        tensor.wait_ge(VS, COEF_ALL)
                        tensor.wait_ge(GX, c.get("GX"))
                    if tt == 0:
                        tensor.wait_ge(GWp[eb], c.get(f"gwv_{e}"))
                    prev = c.get(f"moe_ev_{it - 2}")
                    if prev:
                        tensor.wait_ge(SC, prev)
                    for nb in range(2):
                        bk = 2 * (it % 2) + nb
                        for kc in range(8):
                            tensor.matmul(out=ps_e[bk][:], lhsT=xnT[:, kc, tt * 128:(tt + 1) * 128],
                                          rhs=wbuf[eb][:, kc, nb * 512:(nb + 1) * 512],
                                          start=(kc == 0), stop=False)
                        tensor.matmul(out=ps_e[bk][:], lhsT=onesrow[:],
                                      rhs=expb_row[eb][:, nb * 512:(nb + 1) * 512],
                                      start=False, stop=True).then_inc(TS_, 1)
                        c.inc("TS")
                    c.v[f"moe_mm_{it}"] = c.get("TS")

                @block.scalar
                def _(scalar, e=e, tt=tt, it=it):
                    scalar.wait_ge(TS_, c.get(f"moe_mm_{it}"))
                    prev = c.get(f"moe_acc_{it - 2}")
                    if prev:
                        scalar.wait_ge(PO, prev)
                    for nb in range(2):
                        bk = 2 * (it % 2) + nb
                        scalar.activation(out=ge_buf[bk][:], in_=ps_e[bk][:], func=AF.Relu,
                                          scale=coef[:, tt * 16 + e:tt * 16 + e + 1]).then_inc(SC, 1)
                        c.inc("SC")
                    c.v[f"moe_ev_{it}"] = c.get("SC")

                @block.gpsimd
                def _(gpsimd, e=e, tt=tt, it=it):
                    gpsimd.wait_ge(SC, c.get(f"moe_ev_{it}"))
                    for nb in range(2):
                        bk = 2 * (it % 2) + nb
                        if e == 0:
                            ins = gpsimd.tensor_copy(out=acc[:, tt, nb * 512:(nb + 1) * 512], in_=ge_buf[bk][:])
                        else:
                            ins = gpsimd.tensor_tensor(out=acc[:, tt, nb * 512:(nb + 1) * 512],
                                                       in0=acc[:, tt, nb * 512:(nb + 1) * 512],
                                                       in1=ge_buf[bk][:], op=ALU.add)
                    ins.then_inc(PO, 1)
                    c.inc("PO")
                    c.v[f"moe_acc_{it}"] = c.get("PO")
                    if e == 15:
                        c.v[f"moe_acc_t{tt}"] = c.get("PO")

            @block.gpsimd
            def _(gpsimd, e=e):
                if e + 2 < 16:
                    gpsimd.wait_ge(TS_, c.get(f"moe_mm_{e * NT + NT - 1}"))
                    emit_wload_g(gpsimd, e + 2)
        MOE_MM_ALL = c.get("TS")
        MOE_VS_ALL = c.get("VS")
        MOE_PO_ALL = c.get("PO")
        es_m.close()
        es_g.close()
        es_x.close()

        ps_s = [es_s.enter_context(nc.psum_tensor(f"ps_s{i}", [128, 512], F32)) for i in range(4)]
        attT_sb = sb(es_s, "attT_sb", [128, 8, S], BF16)
        outw_sb = sb(es_s, "outw_sb", [128, 8, D], BF16)
        xt = [sb(es_s, f"xt{i}", [128, D], F32) for i in range(2)]
        g0t = [sb(es_s, f"g0t{i}", [128, D], F32) for i in range(2)]
        x1t = [sb(es_s, f"x1t{i}", [128, D], F32) for i in range(2)]
        xn1t = [sb(es_s, f"xn1t{i}", [128, D], F32) for i in range(2)]
        h1t = [sb(es_s, f"h1t{i}", [128, D], F32) for i in range(2)]

        outw_r = outw_i.rearrange("(k p) d -> p k d", p=128)

        @block.gpsimd
        def _(gpsimd):
            gpsimd.wait_ge(TS_, MOE_MM_ALL)
            gpsimd.wait_ge(VS, MOE_VS_ALL)
            gpsimd.wait_ge(PO, MOE_PO_ALL)
            gpsimd.dma_start(out=outw_sb[:, 0:4, :], in_=outw_r[:, 0:4, :]).then_inc(GA, 16)
            gpsimd.dma_start(out=outw_sb[:, 4:8, :], in_=outw_r[:, 4:8, :]).then_inc(GA, 16)
            gpsimd.dma_start(out=attT_sb[:, 0:4, :], in_=attT_i[:, 0:4, :]).then_inc(GA, 16)
            c.inc("GA", 16 * 3)

        @block.sync
        def _(sync):
            sync.wait_ge(TS_, MOE_MM_ALL)
            sync.wait_ge(VS, MOE_VS_ALL)
            sync.wait_ge(PO, MOE_PO_ALL)
            sync.dma_start(out=attT_sb[:, 4:8, :], in_=attT_i[:, 4:8, :]).then_inc(GA, 16)
            c.inc("GA", 16 * 1)
        GA_ALL = c.get("GA")

        for tt in range(NT):
            pb = tt % 2
            ob = tt % 2

            @block.sync
            def _(sync, tt=tt, pb=pb):
                sync.wait_ge(LDp[pb], (c.get(f"xn_loads_{pb}") + c.get(f"x_loads_{pb}")) * 16)
                prev = c.get(f"x1_{tt - 2}")
                if prev:
                    sync.wait_ge(VS, prev)
                sync.dma_start(out=xt[pb][:], in_=xr[tt]).then_inc(LDp[pb], 16)
                c.inc(f"x_loads_{pb}")
                c.v[f"xld_{tt}"] = (c.get(f"xn_loads_{pb}") + c.get(f"x_loads_{pb}")) * 16

            @block.tensor
            def _(tensor, tt=tt):
                if tt == 0:
                    tensor.wait_ge(GA, GA_ALL)
                prev = c.get(f"s0_ev_{tt - 2}")
                if prev:
                    tensor.wait_ge(VS, prev)
                for nb in range(2):
                    bk = 2 * (tt % 2) + nb
                    for kc in range(8):
                        tensor.matmul(out=ps_s[bk][:], lhsT=attT_sb[:, kc, tt * 128:(tt + 1) * 128],
                                      rhs=outw_sb[:, kc, nb * 512:(nb + 1) * 512],
                                      start=(kc == 0), stop=False)
                    tensor.matmul(out=ps_s[bk][:], lhsT=onesrow[:],
                                  rhs=outb_sb[:, nb * 512:(nb + 1) * 512],
                                  start=False, stop=True).then_inc(TS_, 1)
                    c.inc("TS")
                c.v[f"s0_mm_{tt}"] = c.get("TS")

            @block.scalar
            def _(scalar, tt=tt, ob=ob):
                scalar.wait_ge(PO, c.get(f"moe_acc_t{tt}"))
                prev = c.get(f"comb_done_{tt - 2}")
                if prev:
                    scalar.wait_ge(VS, prev)
                scalar.activation(out=g0t[ob][:], in_=acc[:, tt, :], func=AF.Silu).then_inc(SC, 1)
                c.inc("SC")
                c.v[f"g0_{tt}"] = c.get("SC")

            @block.vector
            def _(vector, tt=tt, pb=pb, ob=ob):
                vector.wait_ge(TS_, c.get(f"s0_mm_{tt}"))
                vector.wait_ge(SC, c.get(f"g0_{tt}"))
                vector.wait_ge(LDp[pb], c.get(f"xld_{tt}"))
                prev = c.get(f"stv_{tt - 2}")
                if prev:
                    vector.wait_ge(ST, prev)
                for nb in range(2):
                    bk = 2 * (tt % 2) + nb
                    vector.tensor_tensor(out=x1t[ob][:, nb * 512:(nb + 1) * 512], in0=ps_s[bk][:],
                                         in1=g0t[ob][:, nb * 512:(nb + 1) * 512], op=ALU.mult)
                vector.drain()
                vector.tensor_tensor(out=x1t[ob][:], in0=xt[pb][:], in1=x1t[ob][:], op=ALU.add).then_inc(VS, 1)
                c.inc("VS")
                c.v[f"s0_ev_{tt}"] = c.get("VS")
                c.v[f"x1_{tt}"] = c.get("VS")

            @block.scalar
            def _(scalar, tt=tt, ob=ob):
                scalar.wait_ge(VS, c.get(f"x1_{tt}"))
                co = 8 + (tt % 2) * 4
                scalar.activation(out=xn1t[ob][:], in_=x1t[ob][:], func=AF.Copy, accum_out=stat[:, co:co + 1])
                scalar.drain()
                scalar.activation(out=xn1t[ob][:],
                                  in_=x1t[ob][:], func=AF.Square, accum_out=stat[:, co + 1:co + 2]).then_inc(SC, 1)
                c.inc("SC")
                c.v[f"lnstat_{tt}"] = c.get("SC")

            @block.vector
            def _(vector, tt=tt):
                vector.wait_ge(SC, c.get(f"lnstat_{tt}"))
                co = 8 + (tt % 2) * 4
                vector.tensor_scalar(out=stat[:, co:co + 1], in0=stat[:, co:co + 1], scalar1=1.0 / D, scalar2=None, op0=ALU.mult)
                vector.tensor_scalar(out=stat[:, co + 1:co + 2], in0=stat[:, co + 1:co + 2], scalar1=1.0 / D, scalar2=None, op0=ALU.mult)
                vector.drain()
                vector.tensor_tensor(out=stat[:, co + 2:co + 3], in0=stat[:, co:co + 1], in1=stat[:, co:co + 1], op=ALU.mult)
                vector.drain()
                vector.tensor_tensor(out=stat[:, co + 2:co + 3], in0=stat[:, co + 1:co + 2], in1=stat[:, co + 2:co + 3], op=ALU.subtract).then_inc(VS, 1)
                c.inc("VS")
                c.v[f"lnvar_{tt}"] = c.get("VS")

            @block.scalar
            def _(scalar, tt=tt):
                scalar.wait_ge(VS, c.get(f"lnvar_{tt}"))
                co = 8 + (tt % 2) * 4
                scalar.activation(out=stat[:, co + 3:co + 4], in_=stat[:, co + 2:co + 3], func=AF.Ln, bias=EPS5)
                scalar.drain()
                scalar.activation(out=stat[:, co + 3:co + 4], in_=stat[:, co + 3:co + 4], func=AF.Exp, scale=NEGH).then_inc(SC, 1)
                c.inc("SC")
                c.v[f"lnrstd_{tt}"] = c.get("SC")

            @block.vector
            def _(vector, tt=tt, ob=ob):
                vector.wait_ge(SC, c.get(f"lnrstd_{tt}"))
                co = 8 + (tt % 2) * 4
                vector.tensor_scalar(out=xn1t[ob][:], in0=x1t[ob][:], scalar1=stat[:, co:co + 1],
                                     scalar2=stat[:, co + 3:co + 4], op0=ALU.subtract, op1=ALU.mult).then_inc(VS, 1)
                c.inc("VS")
                c.v[f"xn1_{tt}"] = c.get("VS")

            @block.scalar
            def _(scalar, tt=tt, ob=ob):
                scalar.wait_ge(VS, c.get(f"xn1_{tt}"))
                co = 8 + (tt % 2) * 4
                scalar.activation(out=h1t[ob][:], in_=xn1t[ob][:], func=AF.Square, accum_out=stat[:, co + 1:co + 2])
                scalar.drain()
                scalar.activation(out=stat[:, co + 2:co + 3], in_=stat[:, co + 1:co + 2], func=AF.Ln, bias=EPS6, scale=INVD)
                scalar.drain()
                scalar.activation(out=stat[:, co + 2:co + 3], in_=stat[:, co + 2:co + 3], func=AF.Exp, scale=NEGH).then_inc(SC, 1)
                c.inc("SC")
                c.v[f"rms_{tt}"] = c.get("SC")

            @block.vector
            def _(vector, tt=tt, ob=ob):
                vector.wait_ge(SC, c.get(f"rms_{tt}"))
                co = 8 + (tt % 2) * 4
                if tt == 0:
                    vector.wait_ge(GC, GD_ALL)
                vector.tensor_scalar(out=h1t[ob][:], in0=xn1t[ob][:], scalar1=stat[:, co + 2:co + 3],
                                     scalar2=None, op0=ALU.mult)
                vector.drain()
                vector.tensor_tensor(out=h1t[ob][:], in0=h1t[ob][:], in1=ln1wb[:], op=ALU.mult).then_inc(VS, 1)
                c.inc("VS")
                c.v[f"comb_done_{tt}"] = c.get("VS")

            @block.gpsimd
            def _(gpsimd, tt=tt, ob=ob):
                gpsimd.wait_ge(VS, c.get(f"comb_done_{tt}"))
                gpsimd.dma_start(out=x1r[tt], in_=x1t[ob][:]).then_inc(ST, 16)
                gpsimd.dma_start(out=xn1r[tt], in_=xn1t[ob][:]).then_inc(ST, 16)
                gpsimd.dma_start(out=h1r[tt], in_=h1t[ob][:]).then_inc(ST, 16)
                c.inc("ST", 48)
                c.v[f"stv_{tt}"] = c.get("ST")

        @block.gpsimd
        def _(gpsimd):
            gpsimd.wait_ge(ST, c.get("ST"))
        es_s.close()

    return nc



def build_b2():
    nc = bass.Bass("TRN2", target_bir_lowering=False)
    h1T_i = nc.dram_tensor("h1T", [D, S + 1], F32, kind="ExternalInput").ap()
    tmp_i = nc.dram_tensor("tmp", [128, 24], F32, kind="ExternalInput").ap()
    wk_i = nc.dram_tensor("wk", [D, D], F32, kind="ExternalInput").ap()
    wv_i = nc.dram_tensor("wv", [D, D], F32, kind="ExternalInput").ap()
    wr_i = nc.dram_tensor("wr", [D, D], F32, kind="ExternalInput").ap()
    o_kT = nc.dram_tensor("o_kT", [D, S], F32, kind="ExternalOutput").ap()
    o_vT = nc.dram_tensor("o_vT", [D, S], F32, kind="ExternalOutput").ap()
    o_rT = nc.dram_tensor("o_rT", [D, S], F32, kind="ExternalOutput").ap()
    W_IN = [wk_i, wv_i, wr_i]
    W_OUT = [o_kT, o_vT, o_rT]

    es = ExitStack()
    sb = lambda name, shape, dt: es.enter_context(nc.sbuf_tensor(name, shape, dt))
    with es:
        h1c = [sb(f"h1c{i}", [128, S + 1], F32) for i in range(2)]
        tmp_sb = sb("tmp_sb", [128, 24], F32)
        dscr = sb("dscr", [128, S], F32)
        xf = [sb(f"xf{i}", [128, 8, S], BF16) for i in range(3)]
        wslab = [sb(f"wslab{i}", [128, 8, 512], BF16) for i in range(2)]
        ob_ = [sb(f"ob{i}", [128, S], F32) for i in range(2)]
        ps = [es.enter_context(nc.psum_tensor(f"ps{i}", [128, 512], F32)) for i in range(4)]
        LD0 = es.enter_context(nc.semaphore("LD0"))
        LD1 = es.enter_context(nc.semaphore("LD1"))
        LDp = [LD0, LD1]
        GD = es.enter_context(nc.semaphore("GD"))
        TS_ = es.enter_context(nc.semaphore("TS"))
        VS = es.enter_context(nc.semaphore("VS"))
        SC = es.enter_context(nc.semaphore("SC"))
        ST = es.enter_context(nc.semaphore("ST"))
        block = es.enter_context(nc.Block())
        c = Ctr()

        GC = es.enter_context(nc.semaphore("GC"))

        @block.gpsimd
        def _(gpsimd):
            gpsimd.dma_start(out=tmp_sb[:], in_=tmp_i).then_inc(GC, 16)
            c.inc("GC", 16)
        GD0 = c.get("GC")

        @block.sync
        def _(sync):
            for i in range(2):
                sync.dma_start(out=h1c[i][:], in_=h1T_i[i * 128:(i + 1) * 128, :]).then_inc(LDp[i], 16)
                c.inc(f"hl_{i}")

        for cc in range(8):
            pb = cc % 2

            @block.vector
            def _(vector, cc=cc, pb=pb):
                vector.wait_ge(LDp[pb], 16 * c.get(f"hl_{pb}"))
                if cc == 0:
                    vector.wait_ge(GC, GD0)
                vector.tensor_tensor(out=dscr[:], in0=h1c[pb][:, 1:S + 1], in1=h1c[pb][:, 0:S], op=ALU.subtract)
                vector.drain()
                for w in range(3):
                    vector.scalar_tensor_tensor(out=xf[w][:, cc, :], in0=dscr[:],
                                                scalar=tmp_sb[:, 8 * w + cc:8 * w + cc + 1],
                                                in1=h1c[pb][:, 0:S], op0=ALU.mult, op1=ALU.add).then_inc(VS, 1)
                    c.inc("VS")
                c.v[f"mix_{cc}"] = c.get("VS")

            @block.sync
            def _(sync, cc=cc, pb=pb):
                if cc + 2 < 8:
                    sync.wait_ge(LDp[pb], 16 * c.get(f"hl_{pb}"))
                    sync.wait_ge(VS, c.get(f"mix_{cc}"))
                    sync.dma_start(out=h1c[pb][:], in_=h1T_i[(cc + 2) * 128:(cc + 3) * 128, :]).then_inc(LDp[pb], 16)
                    c.inc(f"hl_{pb}")
        MIX_ALL = c.get("VS")

        first = [True]
        for w in range(3):
            w_r = W_IN[w].rearrange("(k p) o -> p k o", p=128)
            for og in range(2):
                sl_i = w * 2 + og
                slb = sl_i % 2

                def emit_slab(q, sl_i=sl_i, slb=slb, w_r=w_r, og=og):
                    prev = c.get(f"wmm_{sl_i - 2}")
                    if prev:
                        q.wait_ge(TS_, prev)
                    q.dma_start(out=wslab[slb][:], in_=w_r[:, :, og * 512:(og + 1) * 512]).then_inc(GD, 16)
                    c.inc("GD", 16)
                    c.v[f"wld_{sl_i}"] = c.get("GD")

                @block.gpsimd
                def _(gpsimd, f=emit_slab):
                    f(gpsimd)

                for oc4 in range(4):
                    oc = og * 4 + oc4
                    ocb = oc4 % 2
                    for nt4 in range(4):
                        it = sl_i * 16 + oc4 * 4 + nt4
                        bk = it % 4

                        @block.tensor
                        def _(tensor, w=w, sl_i=sl_i, slb=slb, oc4=oc4, nt4=nt4, it=it, bk=bk):
                            if oc4 == 0 and nt4 == 0:
                                tensor.wait_ge(GD, c.get(f"wld_{sl_i}"))
                            if first[0]:
                                tensor.wait_ge(VS, MIX_ALL)
                                first[0] = False
                            prev = c.get(f"wev_{it - 4}")
                            if prev:
                                tensor.wait_ge(SC, prev)
                            for kc in range(8):
                                mm = tensor.matmul(out=ps[bk][:],
                                                   lhsT=wslab[slb][:, kc, oc4 * 128:(oc4 + 1) * 128],
                                                   rhs=xf[w][:, kc, nt4 * 512:(nt4 + 1) * 512],
                                                   start=(kc == 0), stop=(kc == 7))
                            mm.then_inc(TS_, 1)
                            c.inc("TS")
                            c.v[f"wmm_{sl_i}"] = c.get("TS")
                            c.v[f"wmm_it_{it}"] = c.get("TS")

                        @block.scalar
                        def _(scalar, w=w, oc=oc, ocb=ocb, nt4=nt4, it=it, bk=bk):
                            scalar.wait_ge(TS_, c.get(f"wmm_it_{it}"))
                            if nt4 == 0:
                                prev = c.get(f"gob_{ocb}")
                                if prev:
                                    scalar.wait_ge(ST, prev)
                            if w == 2:
                                scalar.activation(out=ob_[ocb][:, nt4 * 512:(nt4 + 1) * 512],
                                                  in_=ps[bk][:], func=AF.Sigmoid).then_inc(SC, 1)
                            else:
                                scalar.copy(out=ob_[ocb][:, nt4 * 512:(nt4 + 1) * 512],
                                            in_=ps[bk][:]).then_inc(SC, 1)
                            c.inc("SC")
                            c.v[f"wev_{it}"] = c.get("SC")

                    @block.gpsimd
                    def _(gpsimd, w=w, oc=oc, ocb=ocb, sl_i=sl_i, oc4=oc4):
                        gpsimd.wait_ge(SC, c.get(f"wev_{sl_i * 16 + oc4 * 4 + 3}"))
                        gpsimd.dma_start(out=W_OUT[w][oc * 128:(oc + 1) * 128, :],
                                         in_=ob_[ocb][:]).then_inc(ST, 16)
                        c.inc("ST", 16)
                        c.v[f"gob_{ocb}"] = c.get("ST")

        @block.gpsimd
        def _(gpsimd):
            gpsimd.wait_ge(ST, c.get("ST"))

    return nc


NBLK = 2
NSTR = 4
SW = T // NSTR


def build_b3():
    nc = bass.Bass("TRN2", target_bir_lowering=False)
    kT_i = nc.dram_tensor("kT", [NBLK, 128, T], F32, kind="ExternalInput").ap()
    vT_i = nc.dram_tensor("vT", [NBLK, 128, T], F32, kind="ExternalInput").ap()
    up_i = nc.dram_tensor("up", [128, NBLK], F32, kind="ExternalInput").ap()
    tdp_i = nc.dram_tensor("tdp", [128, NBLK], F32, kind="ExternalInput").ap()
    o_wkv = nc.dram_tensor("o_wkv", [NBLK, 128, T], F32, kind="ExternalOutput").ap()

    es = ExitStack()
    sb = lambda name, shape, dt: es.enter_context(nc.sbuf_tensor(name, shape, dt))
    with es:
        cst3 = sb("cst3", [128, 1], F32)
        kt = [sb(f"kt{i}", [128, SW], F32) for i in range(2)]
        vt = [sb(f"vt{i}", [128, SW], F32) for i in range(2)]
        vec = sb("vec", [128, 8], F32)
        EW = sb("EW", [128, SW + 1], F32)
        ekvb = sb("ekvb", [128, SW + 1], F32)
        ekb = sb("ekb", [128, SW + 1], F32)
        Aout = sb("Aout", [128, SW + 1], F32)
        Bout = sb("Bout", [128, SW + 1], F32)
        ekt = sb("ekt", [128, SW], F32)
        n1 = sb("n1", [128, SW], F32)
        d1 = sb("d1", [128, SW], F32)
        d2 = sb("d2", [128, SW], F32)
        wout = [sb(f"wout{i}", [128, SW], F32) for i in range(2)]
        LD0 = es.enter_context(nc.semaphore("LD0"))
        LD1 = es.enter_context(nc.semaphore("LD1"))
        LDp = [LD0, LD1]
        GD = es.enter_context(nc.semaphore("GD"))
        VS = es.enter_context(nc.semaphore("VS"))
        SC = es.enter_context(nc.semaphore("SC"))
        ST = es.enter_context(nc.semaphore("ST"))
        block = es.enter_context(nc.Block())
        c = Ctr()

        @block.gpsimd
        def _(gpsimd):
            gpsimd.dma_start(out=vec[:, 0:2], in_=up_i).then_inc(GD, 16)
            gpsimd.dma_start(out=vec[:, 2:4], in_=tdp_i).then_inc(GD, 16)
            c.inc("GD", 32)
        GD0 = c.get("GD")

        @block.vector
        def _(vector):
            vector.memset(cst3[:], -1.0).then_inc(VS, 1)
            c.inc("VS")
        NEG1 = c.get("VS")

        @block.sync
        def _(sync):
            for i in range(2):
                sync.dma_start(out=kt[i][:], in_=kT_i[0, :, i * SW:(i + 1) * SW]).then_inc(LDp[i], 16)
                sync.dma_start(out=vt[i][:], in_=vT_i[0, :, i * SW:(i + 1) * SW]).then_inc(LDp[i], 16)
                c.inc(f"kl_{i}", 2)

        for bc in range(NBLK):
            @block.scalar
            def _(scalar, bc=bc):
                scalar.wait_ge(GD, GD0)
                if bc > 0:
                    scalar.wait_ge(VS, c.get(f"wkvd_{bc - 1}_{NSTR - 1}"))
                scalar.activation(out=vec[:, 6:7], in_=vec[:, 2 + bc:3 + bc], func=AF.Exp).then_inc(SC, 1)
                c.inc("SC")
            E1 = c.get("SC")

            @block.vector
            def _(vector, bc=bc, E1=E1):
                vector.wait_ge(SC, E1)
                vector.tensor_scalar(out=vec[:, 6:7], in0=vec[:, 6:7], scalar1=-1.0, scalar2=-20.0,
                                     op0=ALU.mult, op1=ALU.max)
                vector.drain()
                vector.tensor_scalar(out=vec[:, 6:7], in0=vec[:, 6:7], scalar1=20.0, scalar2=None,
                                     op0=ALU.min).then_inc(VS, 1)
                c.inc("VS")
            E2 = c.get("VS")

            @block.scalar
            def _(scalar, E2=E2):
                scalar.wait_ge(VS, E2)
                scalar.activation(out=vec[:, 7:8], in_=vec[:, 6:7], func=AF.Exp).then_inc(SC, 1)
                c.inc("SC")
            E3 = c.get("SC")

            @block.vector
            def _(vector, bc=bc, E3=E3):
                vector.wait_ge(SC, E3)
                vector.memset(EW[:, 0:1], 1.0)
                vector.memset(ekvb[:, 0:1], 0.0)
                vector.memset(ekb[:, 0:1], 0.0)
                vector.memset(vec[:, 4:5], 0.0)
                vector.memset(vec[:, 5:6], 0.0)
                ew_b = bass.AP(vec, 7, [[8, 128], [0, SW]])
                vector.tensor_copy(out=EW[:, 1:SW + 1], in_=ew_b).then_inc(VS, 1)
                c.inc("VS")
                c.v[f"blkinit_{bc}"] = c.get("VS")

            for st in range(NSTR):
                pb = st % 2
                gl = bc * NSTR + st

                @block.vector
                def _(vector, bc=bc, st=st, pb=pb, gl=gl):
                    vector.wait_ge(VS, c.get(f"blkinit_{bc}"))
                    vector.wait_ge(LDp[pb], 16 * c.get(f"kl_{pb}"))
                    prev = c.get(f"rden_{gl - 1}")
                    if prev:
                        vector.wait_ge(SC, prev)
                    vector.drain()
                    vector.tensor_scalar(out=d2[:], in0=kt[pb][:], scalar1=-20.0, scalar2=20.0,
                                         op0=ALU.max, op1=ALU.min)
                    vector.tensor_scalar(out=n1[:], in0=kt[pb][:], scalar1=vec[:, 0 + bc:1 + bc], scalar2=-20.0,
                                         op0=ALU.add, op1=ALU.max)
                    vector.drain()
                    vector.tensor_scalar(out=n1[:], in0=n1[:], scalar1=20.0, scalar2=None,
                                         op0=ALU.min).then_inc(VS, 1)
                    c.inc("VS")
                    c.v[f"clip_{gl}"] = c.get("VS")

                @block.scalar
                def _(scalar, gl=gl, pb=pb):
                    scalar.wait_ge(VS, c.get(f"clip_{gl}"))
                    scalar.activation(out=ekb[:, 1:SW + 1], in_=d2[:], func=AF.Exp)
                    scalar.drain()
                    scalar.activation(out=ekt[:], in_=n1[:], func=AF.Exp).then_inc(SC, 1)
                    c.inc("SC")
                    c.v[f"sc_ekt_{gl}"] = c.get("SC")

                @block.vector
                def _(vector, bc=bc, st=st, pb=pb, gl=gl):
                    vector.wait_ge(SC, c.get(f"sc_ekt_{gl}"))
                    vector.tensor_tensor(out=ekvb[:, 1:SW + 1], in0=ekb[:, 1:SW + 1], in1=vt[pb][:], op=ALU.mult)
                    vector.drain()
                    vector.tensor_tensor_scan(out=Aout[:], data0=EW[:], data1=ekvb[:],
                                              initial=vec[:, 4:5], op0=ALU.mult, op1=ALU.add)
                    vector.tensor_tensor_scan(out=Bout[:], data0=EW[:], data1=ekb[:],
                                              initial=vec[:, 5:6], op0=ALU.mult, op1=ALU.add)
                    vector.drain()
                    vector.tensor_copy(out=vec[:, 4:5], in_=Aout[:, SW:SW + 1])
                    vector.tensor_copy(out=vec[:, 5:6], in_=Bout[:, SW:SW + 1])
                    vector.drain()
                    vector.tensor_tensor(out=n1[:], in0=ekt[:], in1=vt[pb][:], op=ALU.mult)
                    vector.tensor_scalar(out=d1[:], in0=ekt[:], scalar1=1e-8, scalar2=None, op0=ALU.add)
                    vector.drain()
                    vector.tensor_tensor(out=n1[:], in0=Aout[:, 0:SW], in1=n1[:], op=ALU.add)
                    vector.tensor_tensor(out=d1[:], in0=Bout[:, 0:SW], in1=d1[:], op=ALU.add).then_inc(VS, 1)
                    c.inc("VS")
                    c.v[f"numden_{gl}"] = c.get("VS")

                @block.scalar
                def _(scalar, gl=gl):
                    scalar.wait_ge(VS, c.get(f"numden_{gl}"))
                    scalar.wait_ge(VS, NEG1)
                    scalar.drain()
                    scalar.activation(out=d2[:], in_=d1[:], func=AF.Ln)
                    scalar.drain()
                    scalar.activation(out=d2[:], in_=d2[:], func=AF.Exp, scale=cst3[:, 0:1]).then_inc(SC, 1)
                    c.inc("SC")
                    c.v[f"rden_{gl}"] = c.get("SC")

                @block.vector
                def _(vector, bc=bc, st=st, gl=gl):
                    vector.wait_ge(SC, c.get(f"rden_{gl}"))
                    ob2 = gl % 2
                    if c.get("ST"):
                        vector.wait_ge(ST, c.get("ST"))
                    vector.tensor_tensor(out=wout[ob2][:], in0=n1[:], in1=d2[:], op=ALU.mult).then_inc(VS, 1)
                    c.inc("VS")
                    c.v[f"wkvd_{bc}_{st}"] = c.get("VS")

                @block.gpsimd
                def _(gpsimd, bc=bc, st=st, gl=gl):
                    gpsimd.wait_ge(VS, c.get(f"wkvd_{bc}_{st}"))
                    ob2 = gl % 2
                    gpsimd.dma_start(out=o_wkv[bc, :, st * SW:(st + 1) * SW], in_=wout[ob2][:]).then_inc(ST, 16)
                    c.inc("ST", 16)

                @block.sync
                def _(sync, bc=bc, st=st, pb=pb, gl=gl):
                    if gl + 2 < NBLK * NSTR:
                        nbc, nst = divmod(gl + 2, NSTR)
                        sync.wait_ge(LDp[pb], 16 * c.get(f"kl_{pb}"))
                        sync.wait_ge(VS, c.get(f"numden_{gl}"))
                        sync.dma_start(out=kt[pb][:], in_=kT_i[nbc, :, nst * SW:(nst + 1) * SW]).then_inc(LDp[pb], 16)
                        sync.dma_start(out=vt[pb][:], in_=vT_i[nbc, :, nst * SW:(nst + 1) * SW]).then_inc(LDp[pb], 16)
                        c.inc(f"kl_{pb}", 2)

        @block.gpsimd
        def _(gpsimd):
            gpsimd.wait_ge(ST, c.get("ST"))

    return nc




def build_b4():
    nc = bass.Bass("TRN2", target_bir_lowering=False)
    xn1_i = nc.dram_tensor("xn1", [S, D], F32, kind="ExternalInput").ap()
    rT_i = nc.dram_tensor("rT", [D, S], F32, kind="ExternalInput").ap()
    wkvT_i = nc.dram_tensor("wkvT", [D, S], F32, kind="ExternalInput").ap()
    wout_i = nc.dram_tensor("wout", [D, D], F32, kind="ExternalInput").ap()
    ln2wb_i = nc.dram_tensor("ln2wb", [128, D], F32, kind="ExternalInput").ap()
    o_xm = nc.dram_tensor("o_xm", [S, D], F32, kind="ExternalOutput").ap()
    o_h2 = nc.dram_tensor("o_h2", [S, D], F32, kind="ExternalOutput").ap()
    xn1r = xn1_i.rearrange("(n p) d -> n p d", p=128)
    xmr = o_xm.rearrange("(n p) d -> n p d", p=128)
    h2r = o_h2.rearrange("(n p) d -> n p d", p=128)

    es = ExitStack()
    sb = lambda name, shape, dt: es.enter_context(nc.sbuf_tensor(name, shape, dt))
    with es:
        consts = sb("consts", [128, 4], F32)
        EPS6, NEGH, INVD = (consts[:, i:i + 1] for i in range(3))
        rc = [sb(f"rc{i}", [128, S], F32) for i in range(2)]
        wc = [sb(f"wc{i}", [128, S], F32) for i in range(2)]
        rw = sb("rw", [128, 8, S], BF16)
        wout_sb = sb("wout_sb", [128, 8, D], BF16)
        ln2wb = sb("ln2wb_sb", [128, D], F32)
        xnt = [sb(f"xnt{i}", [128, D], F32) for i in range(2)]
        xmt = [sb(f"xmt{i}", [128, D], F32) for i in range(2)]
        h2t = [sb(f"h2t{i}", [128, D], F32) for i in range(2)]
        stat = sb("stat", [128, 4], F32)
        ps = [es.enter_context(nc.psum_tensor(f"ps{i}", [128, 512], F32)) for i in range(4)]
        LD0 = es.enter_context(nc.semaphore("LD0"))
        LD1 = es.enter_context(nc.semaphore("LD1"))
        LDp = [LD0, LD1]
        GD = es.enter_context(nc.semaphore("GD"))
        TS_ = es.enter_context(nc.semaphore("TS"))
        VS = es.enter_context(nc.semaphore("VS"))
        SC = es.enter_context(nc.semaphore("SC"))
        ST = es.enter_context(nc.semaphore("ST"))
        block = es.enter_context(nc.Block())
        c = Ctr()

        @block.gpsimd
        def _(gpsimd):
            gpsimd.dma_start(out=ln2wb[:], in_=ln2wb_i).then_inc(GD, 16)
            for kc in range(8):
                gpsimd.dma_start(out=wout_sb[:, kc, :], in_=wout_i[kc * 128:(kc + 1) * 128, :]).then_inc(GD, 16)
            c.inc("GD", 16 * 9)
        GD0 = c.get("GD")

        @block.vector
        def _(vector):
            vector.memset(consts[:, 0:1], 1e-6)
            vector.memset(consts[:, 1:2], -0.5)
            vector.memset(consts[:, 2:3], 1.0 / D).then_inc(VS, 1)
            c.inc("VS")
        SETUP = c.get("VS")

        @block.sync
        def _(sync):
            for i in range(2):
                sync.dma_start(out=rc[i][:], in_=rT_i[i * 128:(i + 1) * 128, :]).then_inc(LDp[i], 16)
                sync.dma_start(out=wc[i][:], in_=wkvT_i[i * 128:(i + 1) * 128, :]).then_inc(LDp[i], 16)
                c.inc(f"rl_{i}", 2)

        for cc in range(8):
            pb = cc % 2

            @block.vector
            def _(vector, cc=cc, pb=pb):
                vector.wait_ge(LDp[pb], 16 * c.get(f"rl_{pb}"))
                vector.tensor_tensor(out=rw[:, cc, :], in0=rc[pb][:], in1=wc[pb][:], op=ALU.mult).then_inc(VS, 1)
                c.inc("VS")
                c.v[f"rw_{cc}"] = c.get("VS")

            @block.sync
            def _(sync, cc=cc, pb=pb):
                if cc + 2 < 8:
                    sync.wait_ge(LDp[pb], 16 * c.get(f"rl_{pb}"))
                    sync.wait_ge(VS, c.get(f"rw_{cc}"))
                    sync.dma_start(out=rc[pb][:], in_=rT_i[(cc + 2) * 128:(cc + 3) * 128, :]).then_inc(LDp[pb], 16)
                    sync.dma_start(out=wc[pb][:], in_=wkvT_i[(cc + 2) * 128:(cc + 3) * 128, :]).then_inc(LDp[pb], 16)
                    c.inc(f"rl_{pb}", 2)
        RW_ALL = c.get("VS")

        for tt in range(NT):
            pb = tt % 2

            @block.sync
            def _(sync, tt=tt, pb=pb):
                sync.wait_ge(LDp[pb], 16 * (c.get(f"rl_{pb}") + c.get(f"xl_{pb}")))
                prev = c.get(f"xm_{tt - 2}")
                if prev:
                    sync.wait_ge(VS, prev)
                sync.dma_start(out=xnt[pb][:], in_=xn1r[tt]).then_inc(LDp[pb], 16)
                c.inc(f"xl_{pb}")
                c.v[f"xnl_{tt}"] = 16 * (c.get(f"rl_{pb}") + c.get(f"xl_{pb}"))

            @block.tensor
            def _(tensor, tt=tt):
                if tt == 0:
                    tensor.wait_ge(VS, RW_ALL)
                    tensor.wait_ge(GD, GD0)
                prev = c.get(f"xm_{tt - 2}")
                if prev:
                    tensor.wait_ge(VS, prev)
                for nb in range(2):
                    bk = 2 * (tt % 2) + nb
                    for kc in range(8):
                        mm = tensor.matmul(out=ps[bk][:], lhsT=rw[:, kc, tt * 128:(tt + 1) * 128],
                                           rhs=wout_sb[:, kc, nb * 512:(nb + 1) * 512],
                                           start=(kc == 0), stop=(kc == 7))
                    mm.then_inc(TS_, 1)
                    c.inc("TS")
                c.v[f"o1_{tt}"] = c.get("TS")

            @block.vector
            def _(vector, tt=tt, pb=pb):
                vector.wait_ge(TS_, c.get(f"o1_{tt}"))
                vector.wait_ge(LDp[pb], c.get(f"xnl_{tt}"))
                if c.get("ST"):
                    vector.wait_ge(ST, c.get("ST"))
                for nb in range(2):
                    bk = 2 * (tt % 2) + nb
                    ins = vector.tensor_tensor(out=xmt[pb][:, nb * 512:(nb + 1) * 512], in0=ps[bk][:],
                                               in1=xnt[pb][:, nb * 512:(nb + 1) * 512], op=ALU.add)
                ins.then_inc(VS, 1)
                c.inc("VS")
                c.v[f"xm_{tt}"] = c.get("VS")

            @block.scalar
            def _(scalar, tt=tt, pb=pb):
                scalar.wait_ge(VS, c.get(f"xm_{tt}"))
                scalar.drain()
                scalar.activation(out=h2t[pb][:], in_=xmt[pb][:], func=AF.Square, accum_out=stat[:, 0:1])
                scalar.drain()
                scalar.activation(out=stat[:, 1:2], in_=stat[:, 0:1], func=AF.Ln, bias=EPS6, scale=INVD)
                scalar.drain()
                scalar.activation(out=stat[:, 1:2], in_=stat[:, 1:2], func=AF.Exp, scale=NEGH).then_inc(SC, 1)
                c.inc("SC")
                c.v[f"rms_{tt}"] = c.get("SC")

            @block.vector
            def _(vector, tt=tt, pb=pb):
                vector.wait_ge(SC, c.get(f"rms_{tt}"))
                if tt == 0:
                    vector.wait_ge(GD, GD0)
                vector.tensor_scalar(out=h2t[pb][:], in0=xmt[pb][:], scalar1=stat[:, 1:2], scalar2=None, op0=ALU.mult)
                vector.drain()
                vector.tensor_tensor(out=h2t[pb][:], in0=h2t[pb][:], in1=ln2wb[:], op=ALU.mult).then_inc(VS, 1)
                c.inc("VS")
                c.v[f"h2_{tt}"] = c.get("VS")

            @block.gpsimd
            def _(gpsimd, tt=tt, pb=pb):
                gpsimd.wait_ge(VS, c.get(f"h2_{tt}"))
                gpsimd.dma_start(out=xmr[tt], in_=xmt[pb][:]).then_inc(ST, 16)
                gpsimd.dma_start(out=h2r[tt], in_=h2t[pb][:]).then_inc(ST, 16)
                c.inc("ST", 32)

        @block.gpsimd
        def _(gpsimd):
            gpsimd.wait_ge(ST, c.get("ST"))

    return nc


def build_b5():
    nc = bass.Bass("TRN2", target_bir_lowering=False)
    h2T_i = nc.dram_tensor("h2T", [D, S + 1], F32, kind="ExternalInput").ap()
    tm2p_i = nc.dram_tensor("tm2p", [128, 16], F32, kind="ExternalInput").ap()
    wk2_i = nc.dram_tensor("wk2", [D, 4 * D], F32, kind="ExternalInput").ap()
    wr2_i = nc.dram_tensor("wr2", [D, D], F32, kind="ExternalInput").ap()
    wv2_i = nc.dram_tensor("wv2", [4 * D, D], F32, kind="ExternalInput").ap()
    xmT_i = nc.dram_tensor("xmT", [D, S], F32, kind="ExternalInput").ap()
    o_sT = nc.dram_tensor("o_sT", [D, S], F32, kind="ExternalOutput").ap()
    HW = S // 2

    es = ExitStack()
    sb = lambda name, shape, dt: es.enter_context(nc.sbuf_tensor(name, shape, dt))
    with es:
        h2c = [sb(f"h2c{i}", [128, HW + 1], F32) for i in range(2)]
        tm2 = sb("tm2", [128, 16], F32)
        dscr = sb("dscr", [128, HW], F32)
        xk2 = sb("xk2", [128, 8, HW], BF16)
        xr2 = sb("xr2", [128, 8, HW], BF16)
        k2 = sb("k2", [128, 32, HW], BF16)
        r2 = sb("r2", [128, 8, HW], F32)
        kslab = [sb(f"kslab{i}", [128, 8, 512], BF16) for i in range(2)]
        vslab = [sb(f"vslab{i}", [128, 16, 256], BF16) for i in range(2)]
        relu_b = [sb(f"relu_b{i}", [128, 512], BF16) for i in range(4)]
        xmc = [sb(f"xmc{i}", [128, HW], F32) for i in range(2)]
        s_b = [sb(f"s_b{i}", [128, HW], F32) for i in range(2)]
        ps = [es.enter_context(nc.psum_tensor(f"ps{i}", [128, 512], F32)) for i in range(8)]
        LD0 = es.enter_context(nc.semaphore("LD0"))
        LD1 = es.enter_context(nc.semaphore("LD1"))
        LDp = [LD0, LD1]
        GD = es.enter_context(nc.semaphore("GD"))
        GC = es.enter_context(nc.semaphore("GC"))
        GW0 = es.enter_context(nc.semaphore("GW0"))
        GW1 = es.enter_context(nc.semaphore("GW1"))
        GWp = [GW0, GW1]
        TS_ = es.enter_context(nc.semaphore("TS"))
        VS = es.enter_context(nc.semaphore("VS"))
        SC = es.enter_context(nc.semaphore("SC"))
        ST = es.enter_context(nc.semaphore("ST"))
        block = es.enter_context(nc.Block())
        c = Ctr()

        @block.gpsimd
        def _(gpsimd):
            gpsimd.dma_start(out=tm2[:], in_=tm2p_i).then_inc(GC, 16)
            c.inc("GC", 16)
        GD0 = c.get("GC")
        wk2_r = wk2_i.rearrange("(k p) o -> p k o", p=128)
        wr2_r = wr2_i.rearrange("(k p) o -> p k o", p=128)
        wv2_r = wv2_i.rearrange("(k p) o -> p k o", p=128)

        for hf in range(2):
            co = hf * HW

            @block.sync
            def _(sync, hf=hf, co=co):
                for i in range(2):
                    if c.get(f"ldc_{i}"):
                        sync.wait_ge(LDp[i], 16 * c.get(f"ldc_{i}"))
                    if hf == 1:
                        prev = c.get(f"mix_{hf - 1}_{6 + i}")
                        if prev:
                            sync.wait_ge(VS, prev)
                    sync.dma_start(out=h2c[i][:], in_=h2T_i[i * 128:(i + 1) * 128, co:co + HW + 1]).then_inc(LDp[i], 16)
                    c.inc(f"ldc_{i}")
                    c.v[f"h2ld_{hf}_{i}"] = 16 * c.get(f"ldc_{i}")

            for cc in range(8):
                pb = cc % 2

                @block.vector
                def _(vector, hf=hf, cc=cc, pb=pb, co=co):
                    vector.wait_ge(LDp[pb], c.get(f"h2ld_{hf}_{cc}") if c.get(f"h2ld_{hf}_{cc}") else c.get(f"h2ld_{hf}_{pb}"))
                    if hf == 0 and cc == 0:
                        vector.wait_ge(GC, GD0)
                    prevk = c.get(f"r2mm_all_{hf - 1}") if cc < 2 else 0
                    if prevk:
                        vector.wait_ge(TS_, prevk)
                    vector.drain()
                    vector.tensor_tensor(out=dscr[:], in0=h2c[pb][:, 1:HW + 1], in1=h2c[pb][:, 0:HW], op=ALU.subtract)
                    vector.drain()
                    vector.scalar_tensor_tensor(out=xk2[:, cc, :], in0=dscr[:], scalar=tm2[:, cc:cc + 1],
                                                in1=h2c[pb][:, 0:HW], op0=ALU.mult, op1=ALU.add)
                    vector.scalar_tensor_tensor(out=xr2[:, cc, :], in0=dscr[:], scalar=tm2[:, 8 + cc:9 + cc],
                                                in1=h2c[pb][:, 0:HW], op0=ALU.mult, op1=ALU.add).then_inc(VS, 1)
                    c.inc("VS")
                    c.v[f"mix_{hf}_{cc}"] = c.get("VS")

                @block.sync
                def _(sync, hf=hf, cc=cc, pb=pb, co=co):
                    if cc + 2 < 8:
                        sync.wait_ge(LDp[pb], 16 * c.get(f"ldc_{pb}"))
                        sync.wait_ge(VS, c.get(f"mix_{hf}_{cc}"))
                        sync.dma_start(out=h2c[pb][:], in_=h2T_i[(cc + 2) * 128:(cc + 3) * 128, co:co + HW + 1]).then_inc(LDp[pb], 16)
                        c.inc(f"ldc_{pb}")
                        c.v[f"h2ld_{hf}_{cc + 2}"] = 16 * c.get(f"ldc_{pb}")
            MIX = c.get("VS")

            for og in range(8):
                slq = c.inc("slq")
                slb = slq % 2

                def emit_kslab(q, og=og, slb=slb, slq=slq):
                    prev = c.get(f"kslabmm_{slq - 2}")
                    if prev:
                        q.wait_ge(TS_, prev)
                    q.dma_start(out=kslab[slb][:],
                                in_=wk2_r[:, :, og * 512:(og + 1) * 512]).then_inc(GD, 16)
                    c.inc("GD", 16)
                    c.v[f"kw_{slq}"] = c.get("GD")

                @block.gpsimd
                def _(gpsimd, f=emit_kslab):
                    f(gpsimd)

                for oc4 in range(4):
                    oc = og * 4 + oc4
                    for nt in range(2):
                        it = oc * 2 + nt
                        bk = it % 4

                        @block.tensor
                        def _(tensor, hf=hf, og=og, slb=slb, slq=slq, oc4=oc4, oc=oc, nt=nt, it=it, bk=bk):
                            if oc4 == 0 and nt == 0:
                                tensor.wait_ge(GD, c.get(f"kw_{slq}"))
                                if og == 0:
                                    tensor.wait_ge(VS, MIX)
                                    prev = c.get(f"sev_all_{hf - 1}")
                                    if prev:
                                        tensor.wait_ge(VS, prev)
                            prev = c.get(f"k2ev_{hf}_{it - 4}")
                            if prev:
                                tensor.wait_ge(SC, prev)
                            for kc in range(8):
                                mm = tensor.matmul(out=ps[bk][:],
                                                   lhsT=kslab[slb][:, kc, oc4 * 128:(oc4 + 1) * 128],
                                                   rhs=xk2[:, kc, nt * 512:(nt + 1) * 512],
                                                   start=(kc == 0), stop=(kc == 7))
                            mm.then_inc(TS_, 1)
                            c.inc("TS")
                            c.v[f"kslabmm_{slq}"] = c.get("TS")
                            c.v[f"k2mm_it_{hf}_{it}"] = c.get("TS")

                        @block.scalar
                        def _(scalar, hf=hf, oc=oc, nt=nt, it=it, bk=bk):
                            scalar.wait_ge(TS_, c.get(f"k2mm_it_{hf}_{it}"))
                            prev = c.get(f"k2sq_{hf}_{it - 4}")
                            if prev:
                                scalar.wait_ge(VS, prev)
                            scalar.activation(out=relu_b[bk][:], in_=ps[bk][:], func=AF.Relu).then_inc(SC, 1)
                            c.inc("SC")
                            c.v[f"k2ev_{hf}_{it}"] = c.get("SC")
                            c.v[f"k2ev_all_{hf}"] = c.get("SC")

                        @block.vector
                        def _(vector, hf=hf, oc=oc, nt=nt, it=it, bk=bk):
                            vector.wait_ge(SC, c.get(f"k2ev_{hf}_{it}"))
                            vector.tensor_tensor(out=k2[:, oc, nt * 512:(nt + 1) * 512], in0=relu_b[bk][:],
                                                 in1=relu_b[bk][:], op=ALU.mult).then_inc(VS, 1)
                            c.inc("VS")
                            c.v[f"k2sq_{hf}_{it}"] = c.get("VS")
                            c.v[f"k2sq_all_{hf}"] = c.get("VS")

            for og in range(2):
                slq = c.inc("slq")
                slb = slq % 2

                def emit_rslab(q, og=og, slb=slb, slq=slq):
                    prev = c.get(f"kslabmm_{slq - 2}")
                    if prev:
                        q.wait_ge(TS_, prev)
                    q.dma_start(out=kslab[slb][:],
                                in_=wr2_r[:, :, og * 512:(og + 1) * 512]).then_inc(GD, 16)
                    c.inc("GD", 16)
                    c.v[f"kw_{slq}"] = c.get("GD")

                @block.gpsimd
                def _(gpsimd, f=emit_rslab):
                    f(gpsimd)

                for oc4 in range(4):
                    oc = og * 4 + oc4
                    for nt in range(2):
                        it = oc * 2 + nt
                        bk = 2 + it % 2

                        @block.tensor
                        def _(tensor, hf=hf, og=og, slb=slb, slq=slq, oc4=oc4, oc=oc, nt=nt, it=it, bk=bk):
                            if oc4 == 0 and nt == 0:
                                tensor.wait_ge(GD, c.get(f"kw_{slq}"))
                                if og == 0:
                                    tensor.wait_ge(SC, c.get(f"k2ev_all_{hf}"))
                            prev = c.get(f"r2ev_{hf}_{it - 2}")
                            if prev:
                                tensor.wait_ge(SC, prev)
                            for kc in range(8):
                                mm = tensor.matmul(out=ps[bk][:],
                                                   lhsT=kslab[slb][:, kc, oc4 * 128:(oc4 + 1) * 128],
                                                   rhs=xr2[:, kc, nt * 512:(nt + 1) * 512],
                                                   start=(kc == 0), stop=(kc == 7))
                            mm.then_inc(TS_, 1)
                            c.inc("TS")
                            c.v[f"kslabmm_{slq}"] = c.get("TS")
                            c.v[f"r2mm_it_{hf}_{it}"] = c.get("TS")

                        @block.scalar
                        def _(scalar, hf=hf, oc=oc, nt=nt, it=it, bk=bk):
                            scalar.wait_ge(TS_, c.get(f"r2mm_it_{hf}_{it}"))
                            scalar.drain()
                            scalar.activation(out=r2[:, oc, nt * 512:(nt + 1) * 512], in_=ps[bk][:],
                                              func=AF.Sigmoid).then_inc(SC, 1)
                            c.inc("SC")
                            c.v[f"r2ev_{hf}_{it}"] = c.get("SC")
            R2_ALL = c.get("SC")
            c.v[f"r2mm_all_{hf}"] = c.get("TS")

            for og in range(4):
                for kg in range(2):
                    vsl = c.inc("vsl")
                    vslb = vsl % 2

                    def emit_vslab(q, og=og, kg=kg, vsl=vsl, vslb=vslb):
                        prev = c.get(f"vslabmm_{vsl - 2}")
                        if prev:
                            q.wait_ge(TS_, prev)
                        q.dma_start(out=vslab[vslb][:],
                                    in_=wv2_r[:, kg * 16:(kg + 1) * 16, og * 256:(og + 1) * 256]).then_inc(GD, 16)
                        c.inc("GD", 16)
                        c.v[f"vw_{vsl}"] = c.get("GD")

                    @block.gpsimd
                    def _(gpsimd, f=emit_vslab):
                        f(gpsimd)

                    @block.tensor
                    def _(tensor, hf=hf, og=og, kg=kg, vsl=vsl, vslb=vslb):
                        tensor.wait_ge(GD, c.get(f"vw_{vsl}"))
                        if og == 0 and kg == 0:
                            tensor.wait_ge(SC, R2_ALL)
                            tensor.wait_ge(VS, c.get(f"k2sq_all_{hf}"))
                        if kg == 0:
                            prev = c.get(f"sev_og_{hf}_{og - 1}") if og else c.get(f"sev_og_{hf - 1}_3")
                            if prev:
                                tensor.wait_ge(VS, prev)
                        for oc2 in range(2):
                            for nt in range(2):
                                bk = 4 + oc2 * 2 + nt
                                for kc in range(16):
                                    mm = tensor.matmul(out=ps[bk][:],
                                                       lhsT=vslab[vslb][:, kc, oc2 * 128:(oc2 + 1) * 128],
                                                       rhs=k2[:, kg * 16 + kc, nt * 512:(nt + 1) * 512],
                                                       start=(kg == 0 and kc == 0), stop=(kg == 1 and kc == 15))
                        mm.then_inc(TS_, 1)
                        c.inc("TS")
                        c.v[f"vslabmm_{vsl}"] = c.get("TS")
                        c.v[f"o2mm_{hf}_{og}"] = c.get("TS")

                for oc2 in range(2):
                    oc = og * 2 + oc2
                    pb = oc % 2

                    @block.sync
                    def _(sync, hf=hf, oc=oc, pb=pb, co=co):
                        sync.wait_ge(LDp[pb], 16 * c.get(f"ldc_{pb}"))
                        prev = c.get(f"sdone_{hf}_{oc - 2}")
                        if prev:
                            sync.wait_ge(VS, prev)
                        sync.dma_start(out=xmc[pb][:], in_=xmT_i[oc * 128:(oc + 1) * 128, co:co + HW]).then_inc(LDp[pb], 16)
                        c.inc(f"ldc_{pb}")
                        c.v[f"xmld_{hf}_{oc}"] = 16 * c.get(f"ldc_{pb}")

                    for nt in range(2):
                        bk = 4 + oc2 * 2 + nt

                        @block.vector
                        def _(vector, hf=hf, og=og, oc=oc, oc2=oc2, nt=nt, bk=bk, pb=pb):
                            vector.wait_ge(TS_, c.get(f"o2mm_{hf}_{og}"))
                            vector.wait_ge(LDp[pb], c.get(f"xmld_{hf}_{oc}"))
                            if nt == 0:
                                prev = c.get(f"gsb_{oc % 2}")
                                if prev:
                                    vector.wait_ge(ST, prev)
                            vector.tensor_tensor(out=s_b[oc % 2][:, nt * 512:(nt + 1) * 512], in0=ps[bk][:],
                                                 in1=r2[:, oc, nt * 512:(nt + 1) * 512], op=ALU.mult)
                            vector.drain()
                            vector.tensor_tensor(out=s_b[oc % 2][:, nt * 512:(nt + 1) * 512],
                                                 in0=s_b[oc % 2][:, nt * 512:(nt + 1) * 512],
                                                 in1=xmc[pb][:, nt * 512:(nt + 1) * 512], op=ALU.add).then_inc(VS, 1)
                            c.inc("VS")
                            c.v[f"sev_{hf}_{oc}_{nt}"] = c.get("VS")
                            c.v[f"sdone_{hf}_{oc}"] = c.get("VS")
                            c.v[f"sev_og_{hf}_{og}"] = c.get("VS")
                            c.v[f"sev_all_{hf}"] = c.get("VS")

                    @block.gpsimd
                    def _(gpsimd, hf=hf, oc=oc, co=co):
                        gpsimd.wait_ge(VS, c.get(f"sdone_{hf}_{oc}"))
                        gpsimd.dma_start(out=o_sT[oc * 128:(oc + 1) * 128, co:co + HW],
                                         in_=s_b[oc % 2][:]).then_inc(ST, 16)
                        c.inc("ST", 16)
                        c.v[f"gsb_{oc % 2}"] = c.get("ST")

        @block.gpsimd
        def _(gpsimd):
            gpsimd.wait_ge(ST, c.get("ST"))

    return nc


def build_b6():
    nc = bass.Bass("TRN2", target_bir_lowering=False)
    xn1_i = nc.dram_tensor("xn1", [S, D], F32, kind="ExternalInput").ap()
    x1_i = nc.dram_tensor("x1", [S, D], F32, kind="ExternalInput").ap()
    s_i = nc.dram_tensor("s", [S, D], F32, kind="ExternalInput").ap()
    w_i = nc.dram_tensor("w", [D, D], F32, kind="ExternalInput").ap()
    b_i = nc.dram_tensor("b", [1, D], F32, kind="ExternalInput").ap()
    o_out = nc.dram_tensor("o_out", [S, D], F32, kind="ExternalOutput").ap()
    xn1r = xn1_i.rearrange("(n p) d -> n p d", p=128)
    x1r = x1_i.rearrange("(n p) d -> n p d", p=128)
    sr = s_i.rearrange("(n p) d -> n p d", p=128)
    outr = o_out.rearrange("(n p) d -> n p d", p=128)
    wr = w_i.rearrange("(n p) d -> n p d", p=128)
    MAGIC = float(3 * 2 ** 22)

    es = ExitStack()
    sb = lambda name, shape, dt: es.enter_context(nc.sbuf_tensor(name, shape, dt))
    with es:
        mgc = sb("mgc", [128, 2], F32)
        MG, MGN = mgc[:, 0:1], mgc[:, 1:2]
        qin = [sb(f"qin{i}", [128, D], F32) for i in range(2)]
        at = [sb(f"at{i}", [128, D], F32) for i in range(2)]
        bmx = [sb(f"bmx{i}", [128, 1024], F32) for i in range(2)]
        yt = sb("yt", [128, D], F32)
        m2t = sb("m2t", [128, D], F32)
        m4t = sb("m4t", [128, D], F32)
        rcst = sb("rcst", [128, D], F32)
        stt = sb("stt", [128, D], F32)
        zt = sb("zt", [128, D], F32)
        qt = sb("qt", [128, D], F32)
        xdb = [sb(f"xdb{i}", [128, D], BF16) for i in range(2)]
        xdT = sb("xdT", [128, 8, S], BF16)
        wdT = sb("wdT", [128, 8, D], BF16)
        brow = sb("brow", [1, D], BF16)
        onesrow = sb("onesrow", [1, 128], BF16)
        ident = sb("ident", [128, 128], BF16)
        idq = sb("idq", [128, 128], F32)
        gt = sb("gt", [128, D], F32)
        x1t = [sb(f"x1t{i}", [128, D], F32) for i in range(2)]
        st_ = [sb(f"st{i}", [128, D], F32) for i in range(2)]
        outb = [sb(f"outb{i}", [128, D], F32) for i in range(2)]
        ps_t = [es.enter_context(nc.psum_tensor(f"ps_t{i}", [128, 128], BF16)) for i in range(2)]
        ps = [es.enter_context(nc.psum_tensor(f"ps{i}", [128, 512], F32)) for i in range(4)]
        LD0 = es.enter_context(nc.semaphore("LD0"))
        LD1 = es.enter_context(nc.semaphore("LD1"))
        LDp = [LD0, LD1]
        GD = es.enter_context(nc.semaphore("GD"))
        TQ = es.enter_context(nc.semaphore("TQ"))
        IO = es.enter_context(nc.semaphore("IO"))
        PO = es.enter_context(nc.semaphore("PO"))
        TS_ = es.enter_context(nc.semaphore("TS"))
        VS = es.enter_context(nc.semaphore("VS"))
        SC = es.enter_context(nc.semaphore("SC"))
        ST = es.enter_context(nc.semaphore("ST"))
        block = es.enter_context(nc.Block())
        c = Ctr()

        @block.gpsimd
        def _(gpsimd):
            gpsimd.iota(out=idq[:], pattern=[[1, 128]], base=0, channel_multiplier=-1,
                        allow_small_or_imprecise_dtypes=True).then_inc(IO, 1)
            gpsimd.dma_start(out=brow[:], in_=b_i).then_inc(GD, 16)
            c.inc("GD", 16)
        GD0 = c.get("GD")

        @block.vector
        def _(vector):
            vector.memset(mgc[:, 0:1], MAGIC)
            vector.memset(mgc[:, 1:2], -MAGIC)
            vector.memset(onesrow[:], 1.0)
            vector.wait_ge(IO, 1)
            vector.tensor_scalar(out=ident[:], in0=idq[:], scalar1=0.0, scalar2=None,
                                 op0=ALU.is_equal).then_inc(VS, 1)
            c.inc("VS")
        SETUP = c.get("VS")

        for wt_ in range(24):
            pb = wt_ % 2
            srcr = wr[wt_] if wt_ < 8 else xn1r[wt_ - 8]
            dst, dcol = (wdT, wt_) if wt_ < 8 else (xdT, wt_ - 8)

            @block.sync
            def _(sync, wt_=wt_, pb=pb, srcr=srcr):
                sync.wait_ge(LDp[pb], 16 * c.get(f"ql_{pb}"))
                prev = c.get(f"qy_{wt_ - 2}")
                if prev:
                    sync.wait_ge(VS, prev)
                sync.dma_start(out=qin[pb][:], in_=srcr).then_inc(LDp[pb], 16)
                c.inc(f"ql_{pb}")
                c.v[f"qld_{wt_}"] = 16 * c.get(f"ql_{pb}")

            @block.scalar
            def _(scalar, wt_=wt_, pb=pb):
                scalar.wait_ge(LDp[pb], c.get(f"qld_{wt_}"))
                prev = c.get(f"qz_{wt_ - 2}")
                if prev:
                    scalar.wait_ge(VS, prev)
                scalar.activation(out=at[pb][:], in_=qin[pb][:], func=AF.Abs).then_inc(SC, 1)
                c.inc("SC")
                c.v[f"qabs_{wt_}"] = c.get("SC")

            @block.vector
            def _(vector, wt_=wt_, pb=pb):
                vector.wait_ge(SC, c.get(f"qabs_{wt_}"))
                prev = c.get(f"qd_{wt_ - 2}")
                if prev:
                    vector.wait_ge(VS, prev)
                b_ = bmx[pb]
                vector.tensor_tensor(out=b_[:, 0:512], in0=at[pb][:, 0:1024:2], in1=at[pb][:, 1:1024:2], op=ALU.max)
                vector.drain()
                vector.tensor_tensor(out=b_[:, 512:768], in0=b_[:, 0:512:2], in1=b_[:, 1:512:2], op=ALU.max)
                vector.drain()
                vector.tensor_tensor(out=b_[:, 768:896], in0=b_[:, 512:768:2], in1=b_[:, 513:768:2], op=ALU.max)
                vector.drain()
                vector.tensor_tensor(out=b_[:, 896:960], in0=b_[:, 768:896:2], in1=b_[:, 769:896:2], op=ALU.max)
                vector.drain()
                vector.tensor_tensor(out=b_[:, 960:992], in0=b_[:, 896:960:2], in1=b_[:, 897:960:2], op=ALU.max)
                vector.drain()
                vector.tensor_scalar(out=b_[:, 960:992], in0=b_[:, 960:992], scalar1=1.0 / 7.5, scalar2=1e-30,
                                     op0=ALU.mult, op1=ALU.max)
                vector.drain()
                vector.reciprocal(out=b_[:, 992:1024], in_=b_[:, 960:992])
                vector.drain()
                rsv = bass.AP(b_, 992, [[1024, 128], [1, 32], [0, 32]])
                vector.tensor_tensor(out=yt[:], in0=qin[pb][:], in1=rsv, op=ALU.mult)
                vector.tensor_tensor(out=at[pb][:], in0=at[pb][:], in1=rsv, op=ALU.mult).then_inc(VS, 1)
                c.inc("VS")
                c.v[f"qy_{wt_}"] = c.get("VS")

            @block.vector
            def _(vector, wt_=wt_, pb=pb):
                vector.wait_ge(VS, c.get(f"qy_{wt_}"))
                vector.drain()
                vector.tensor_scalar(out=m2t[:], in0=at[pb][:], scalar1=2.0, scalar2=None, op0=ALU.is_ge)
                vector.tensor_scalar(out=m4t[:], in0=at[pb][:], scalar1=4.0, scalar2=None, op0=ALU.is_ge)
                vector.drain()
                vector.tensor_scalar(out=rcst[:], in0=m2t[:], scalar1=-4.0, scalar2=8.0, op0=ALU.mult, op1=ALU.add)
                vector.tensor_scalar(out=stt[:], in0=m2t[:], scalar1=0.125, scalar2=0.125, op0=ALU.mult, op1=ALU.add)
                vector.drain()
                vector.scalar_tensor_tensor(out=rcst[:], in0=m4t[:], scalar=-2.0, in1=rcst[:],
                                            op0=ALU.mult, op1=ALU.add)
                vector.scalar_tensor_tensor(out=stt[:], in0=m4t[:], scalar=0.25, in1=stt[:],
                                            op0=ALU.mult, op1=ALU.add)
                vector.drain()
                vector.tensor_tensor(out=zt[:], in0=yt[:], in1=rcst[:], op=ALU.mult).then_inc(VS, 1)
                c.inc("VS")
                c.v[f"qz_{wt_}"] = c.get("VS")

            @block.scalar
            def _(scalar, wt_=wt_):
                scalar.wait_ge(VS, c.get(f"qz_{wt_}"))
                scalar.activation(out=zt[:], in_=zt[:], func=AF.Copy, bias=MAGIC)
                scalar.drain()
                scalar.activation(out=zt[:], in_=zt[:], func=AF.Copy, bias=-MAGIC).then_inc(SC, 1)
                c.inc("SC")
                c.v[f"qmg_{wt_}"] = c.get("SC")

            @block.vector
            def _(vector, wt_=wt_, pb=pb):
                vector.wait_ge(SC, c.get(f"qmg_{wt_}"))
                prev = c.get(f"qtp_{wt_ - 2}")
                if prev:
                    vector.wait_ge(TS_, prev)
                vector.tensor_tensor(out=qt[:], in0=zt[:], in1=stt[:], op=ALU.mult)
                vector.drain()
                sv = bass.AP(bmx[pb], 960, [[1024, 128], [1, 32], [0, 32]])
                vector.tensor_tensor(out=xdb[pb][:], in0=qt[:], in1=sv, op=ALU.mult).then_inc(VS, 1)
                c.inc("VS")
                c.v[f"qd_{wt_}"] = c.get("VS")

            for kc in range(8):
                @block.tensor
                def _(tensor, wt_=wt_, pb=pb, kc=kc, dst=dst, dcol=dcol):
                    if kc == 0:
                        tensor.wait_ge(VS, c.get(f"qd_{wt_}"))
                        if wt_ == 0:
                            tensor.wait_ge(VS, SETUP)
                    prev = c.get(f"qtpc_{wt_}_{kc - 2}") if kc >= 2 else c.get(f"qtpc_{wt_ - 1}_{kc + 6}")
                    if prev:
                        tensor.wait_ge(VS, prev)
                    tensor.transpose(out=ps_t[kc % 2][:], in_=xdb[pb][:, kc * 128:(kc + 1) * 128],
                                     identity=ident[:]).then_inc(TS_, 1)
                    c.inc("TS")
                    c.v[f"qtpm_{wt_}_{kc}"] = c.get("TS")
                    if kc == 7:
                        c.v[f"qtp_{wt_}"] = c.get("TS")

                @block.vector
                def _(vector, wt_=wt_, kc=kc, dst=dst, dcol=dcol):
                    vector.wait_ge(TS_, c.get(f"qtpm_{wt_}_{kc}"))
                    vector.tensor_copy(out=dst[:, kc, dcol * 128:(dcol + 1) * 128],
                                       in_=ps_t[kc % 2][:]).then_inc(VS, 1)
                    c.inc("VS")
                    c.v[f"qtpc_{wt_}_{kc}"] = c.get("VS")
                    if kc == 7:
                        c.v[f"qtpv_{wt_}"] = c.get("VS")
        QALL = c.get("VS")

        for tt in range(NT):
            pb = tt % 2

            @block.sync
            def _(sync, tt=tt, pb=pb):
                sync.wait_ge(LDp[pb], 16 * (c.get(f"ql_{pb}") + c.get(f"fl_{pb}")))
                prev = c.get(f"fin_{tt - 2}")
                if prev:
                    sync.wait_ge(VS, prev)
                sync.dma_start(out=x1t[pb][:], in_=x1r[tt]).then_inc(LDp[pb], 16)
                sync.dma_start(out=st_[pb][:], in_=sr[tt]).then_inc(LDp[pb], 16)
                c.inc(f"fl_{pb}", 2)
                c.v[f"fld_{tt}"] = 16 * (c.get(f"ql_{pb}") + c.get(f"fl_{pb}"))

            @block.tensor
            def _(tensor, tt=tt):
                if tt == 0:
                    tensor.wait_ge(VS, QALL)
                    tensor.wait_ge(GD, GD0)
                prev = c.get(f"gsil_{tt - 2}")
                if prev:
                    tensor.wait_ge(SC, prev)
                for nb in range(2):
                    bk = 2 * (tt % 2) + nb
                    for kc in range(8):
                        tensor.matmul(out=ps[bk][:], lhsT=xdT[:, kc, tt * 128:(tt + 1) * 128],
                                      rhs=wdT[:, kc, nb * 512:(nb + 1) * 512],
                                      start=(kc == 0), stop=False)
                    tensor.matmul(out=ps[bk][:], lhsT=onesrow[:], rhs=brow[:, nb * 512:(nb + 1) * 512],
                                  start=False, stop=True).then_inc(TS_, 1)
                    c.inc("TS")
                c.v[f"ffn_{tt}"] = c.get("TS")

            @block.scalar
            def _(scalar, tt=tt):
                scalar.wait_ge(TS_, c.get(f"ffn_{tt}"))
                prev = c.get(f"fin_{tt - 1}")
                if prev:
                    scalar.wait_ge(VS, prev)
                for nb in range(2):
                    bk = 2 * (tt % 2) + nb
                    ins = scalar.activation(out=gt[:, nb * 512:(nb + 1) * 512], in_=ps[bk][:], func=AF.Silu)
                ins.then_inc(SC, 1)
                c.inc("SC")
                c.v[f"gsil_{tt}"] = c.get("SC")

            @block.vector
            def _(vector, tt=tt, pb=pb):
                vector.wait_ge(SC, c.get(f"gsil_{tt}"))
                vector.wait_ge(LDp[pb], c.get(f"fld_{tt}"))
                prev = c.get(f"fst_{tt - 2}")
                if prev:
                    vector.wait_ge(ST, prev)
                vector.tensor_tensor(out=outb[pb][:], in0=st_[pb][:], in1=gt[:], op=ALU.mult)
                vector.drain()
                vector.tensor_tensor(out=outb[pb][:], in0=x1t[pb][:], in1=outb[pb][:], op=ALU.add).then_inc(VS, 1)
                c.inc("VS")
                c.v[f"fin_{tt}"] = c.get("VS")

            @block.gpsimd
            def _(gpsimd, tt=tt, pb=pb):
                gpsimd.wait_ge(VS, c.get(f"fin_{tt}"))
                gpsimd.dma_start(out=outr[tt], in_=outb[pb][:]).then_inc(ST, 16)
                c.inc("ST", 16)
                c.v[f"fst_{tt}"] = c.get("ST")

        @block.gpsimd
        def _(gpsimd):
            gpsimd.wait_ge(ST, c.get("ST"))

    return nc


def kernel(**inputs):
    asf = lambda a: np.ascontiguousarray(np.asarray(a, dtype=np.float32))
    bfv = lambda a: (np.asarray(a).view(BF) if np.asarray(a).dtype == np.uint16 else np.asarray(a))
    x = asf(inputs["L0_attn_x"])
    cores = list(range(NCORE))

    def shard(t):
        return [np.ascontiguousarray(t[i // 4, (i % 4) * S:(i % 4 + 1) * S]) for i in range(NCORE)]

    def gather(parts):
        g = np.zeros((B, T, D), dtype=parts[0].dtype)
        for i in range(NCORE):
            g[i // 4, (i % 4) * S:(i % 4 + 1) * S] = parts[i]
        return g

    nc = build_a2()
    qkvb_p = np.ascontiguousarray(asf(inputs["L0_attn_qkv_b"])[:2048].reshape(16, 128).T)
    xs = shard(x)
    in_maps = [{
        "x": xs[i],
        "qkv_w": asf(inputs["L0_attn_qkv_w"]),
        "qkvb_p": qkvb_p,
        "qkv_b": asf(inputs["L0_attn_qkv_b"]).reshape(1, 3 * D),
        "proj": asf(inputs["L0_attn_proj"]),
    } for i in range(NCORE)]
    r2_ = run_bass_kernel_spmd(nc, in_maps, core_ids=cores).results

    xn = gather([np.asarray(r["o_xn"]) for r in r2_])

    qp = np.zeros((B, H, M, T), dtype=BF)
    kp = np.zeros((B, H, M, T), dtype=BF)
    kptmP = np.zeros((B, H, 128, NCH, M), dtype=BF)
    vtmP = np.zeros((B, H, 128, NCH, M), dtype=BF)
    for i in range(NCORE):
        b, q = i // 4, i % 4
        sl = slice(q * S, (q + 1) * S)
        cs = slice(q * 16, (q + 1) * 16)
        qp[b, :, :, sl] = bfv(r2_[i]["o_qp"])
        kp[b, :, :, sl] = bfv(r2_[i]["o_kp"])
        kptmP[b, :, :, cs, :] = bfv(r2_[i]["o_kptm"])
        v = bfv(r2_[i]["o_vtm"]).reshape(16, 128, H, M)
        vtmP[b, :, :, cs, :] = v.transpose(2, 1, 0, 3)
    nc3 = build_a3()
    in_maps = []
    for i in range(NCORE):
        us = [divmod(i * NU + j, H) for j in range(NU)]
        in_maps.append({
            "i_qp": np.ascontiguousarray(np.stack([qp[b, h] for (b, h) in us])),
            "i_kp": np.ascontiguousarray(np.stack([kp[b, h] for (b, h) in us])),
            "i_kptm": np.ascontiguousarray(np.stack([kptmP[b, h] for (b, h) in us])),
            "i_vtm": np.ascontiguousarray(np.stack([vtmP[b, h] for (b, h) in us])),
        })
    r3_ = run_bass_kernel_spmd(nc3, in_maps, core_ids=cores).results
    att = np.zeros((B, T, H, M), dtype=BF)
    for i in range(NCORE):
        a = bfv(r3_[i]["o_att"])
        for j in range(NU):
            b, h = divmod(i * NU + j, H)
            att[b, :, h, :] = a[j].transpose(1, 0, 2).reshape(T, M)
    att = att.reshape(B, T, D)

    nc4 = build_a4()
    ln1wb = np.ascontiguousarray(np.broadcast_to(asf(inputs["L1_attn_ln1_w"]), (128, D)))
    in_maps = []
    for i in range(NCORE):
        b, q = i // 4, i % 4
        sl = slice(q * S, (q + 1) * S)
        in_maps.append({
            "xn": np.ascontiguousarray(xn[b, sl]),
            "x": xs[i],
            "attT": np.ascontiguousarray(att[b, sl].T.reshape(8, 128, S).transpose(1, 0, 2)),
            "out_w": asf(inputs["L0_attn_out_w"]),
            "out_b": asf(inputs["L0_attn_out_b"]).reshape(1, D),
            "gateT": np.ascontiguousarray(asf(inputs["L0_ffn_gate_w"]).T),
            "expw": np.ascontiguousarray(asf(inputs["L0_ffn_expert_w"]).transpose(0, 2, 1)),
            "expb": asf(inputs["L0_ffn_expert_b"]).reshape(1, 16 * D),
            "ln1wb": ln1wb,
        })
    r4_ = run_bass_kernel_spmd(nc4, in_maps, core_ids=cores).results
    x1 = gather([np.asarray(r["o_x1"]) for r in r4_])
    xn1 = gather([np.asarray(r["o_xn1"]) for r in r4_])
    h1 = gather([np.asarray(r["o_h1"]) for r in r4_])

    nc5 = build_b2()
    tmp = np.concatenate([asf(inputs[k]).reshape(8, 128).T for k in
                          ["L1_attn_time_mix_k", "L1_attn_time_mix_v", "L1_attn_time_mix_r"]], axis=1)
    in_maps = []
    for i in range(NCORE):
        b, q = i // 4, i % 4
        lo = q * S
        h1T = np.zeros((D, S + 1), np.float32)
        h1T[:, 1:] = h1[b, lo:lo + S].T
        if q > 0:
            h1T[:, 0] = h1[b, lo - 1]
        in_maps.append({
            "h1T": np.ascontiguousarray(h1T), "tmp": np.ascontiguousarray(tmp),
            "wk": asf(inputs["L1_attn_W_key"]),
            "wv": asf(inputs["L1_attn_W_value"]),
            "wr": asf(inputs["L1_attn_W_recept"]),
        })
    r5_ = run_bass_kernel_spmd(nc5, in_maps, core_ids=cores).results
    kT = np.zeros((B, D, T), np.float32)
    vT = np.zeros((B, D, T), np.float32)
    rT = np.zeros((B, D, T), np.float32)
    for i in range(NCORE):
        b, q = i // 4, i % 4
        sl = slice(q * S, (q + 1) * S)
        kT[b, :, sl] = np.asarray(r5_[i]["o_kT"])
        vT[b, :, sl] = np.asarray(r5_[i]["o_vT"])
        rT[b, :, sl] = np.asarray(r5_[i]["o_rT"])

    nc6 = build_b3()
    u_ = asf(inputs["L1_attn_time_first"])
    td_ = asf(inputs["L1_attn_time_decay"])
    in_maps = []
    for i in range(NCORE):
        blocks = [2 * i, 2 * i + 1]
        kk = np.stack([kT[bc // 8][(bc % 8) * 128:(bc % 8 + 1) * 128] for bc in blocks])
        vv = np.stack([vT[bc // 8][(bc % 8) * 128:(bc % 8 + 1) * 128] for bc in blocks])
        up = np.stack([u_[(bc % 8) * 128:(bc % 8 + 1) * 128] for bc in blocks], axis=1)
        tdp = np.stack([td_[(bc % 8) * 128:(bc % 8 + 1) * 128] for bc in blocks], axis=1)
        in_maps.append({"kT": np.ascontiguousarray(kk), "vT": np.ascontiguousarray(vv),
                        "up": np.ascontiguousarray(up), "tdp": np.ascontiguousarray(tdp)})
    r6_ = run_bass_kernel_spmd(nc6, in_maps, core_ids=cores).results
    wkvT = np.zeros((B, D, T), np.float32)
    for i in range(NCORE):
        w = np.asarray(r6_[i]["o_wkv"])
        for j, bc in enumerate([2 * i, 2 * i + 1]):
            wkvT[bc // 8, (bc % 8) * 128:(bc % 8 + 1) * 128] = w[j]

    nc7 = build_b4()
    ln2wb = np.ascontiguousarray(np.broadcast_to(asf(inputs["L1_attn_ln2_w"]), (128, D)))
    in_maps = []
    for i in range(NCORE):
        b, q = i // 4, i % 4
        sl = slice(q * S, (q + 1) * S)
        in_maps.append({
            "xn1": np.ascontiguousarray(xn1[b, sl]),
            "rT": np.ascontiguousarray(rT[b, :, sl]),
            "wkvT": np.ascontiguousarray(wkvT[b, :, sl]),
            "wout": asf(inputs["L1_attn_W_out"]),
            "ln2wb": ln2wb,
        })
    r7_ = run_bass_kernel_spmd(nc7, in_maps, core_ids=cores).results
    xm = gather([np.asarray(r["o_xm"]) for r in r7_])
    h2 = gather([np.asarray(r["o_h2"]) for r in r7_])

    nc8 = build_b5()
    tm2p = np.ascontiguousarray(np.concatenate(
        [asf(inputs[k]).reshape(8, 128).T for k in ["L1_attn_time_mix_k2", "L1_attn_time_mix_r2"]], axis=1))
    in_maps = []
    for i in range(NCORE):
        b, q = i // 4, i % 4
        lo = q * S
        h2T = np.zeros((D, S + 1), np.float32)
        h2T[:, 1:] = h2[b, lo:lo + S].T
        if q > 0:
            h2T[:, 0] = h2[b, lo - 1]
        in_maps.append({
            "h2T": np.ascontiguousarray(h2T), "tm2p": tm2p,
            "wk2": asf(inputs["L1_attn_W_key2"]),
            "wr2": asf(inputs["L1_attn_W_recept2"]),
            "wv2": asf(inputs["L1_attn_W_value2"]),
            "xmT": np.ascontiguousarray(xm[b, lo:lo + S].T),
        })
    r8_ = run_bass_kernel_spmd(nc8, in_maps, core_ids=cores).results
    s_parts = [np.ascontiguousarray(np.asarray(r["o_sT"]).T) for r in r8_]

    nc9 = build_b6()
    in_maps = []
    for i in range(NCORE):
        b, q = i // 4, i % 4
        sl = slice(q * S, (q + 1) * S)
        in_maps.append({
            "xn1": np.ascontiguousarray(xn1[b, sl]),
            "x1": np.ascontiguousarray(x1[b, sl]),
            "s": s_parts[i],
            "w": asf(inputs["L1_ffn_weight"]),
            "b": asf(inputs["L1_ffn_bias"]).reshape(1, D),
        })
    r9_ = run_bass_kernel_spmd(nc9, in_maps, core_ids=cores).results
    out = gather([np.asarray(r["o_out"]) for r in r9_])
    return out
